# version 05
# baseline (speedup 1.0000x reference)
import numpy as np
import ml_dtypes
import concourse.bass as bass
import concourse.mybir as mybir
from concourse.bass_utils import run_bass_kernel_spmd

F32 = mybir.dt.float32
BF16 = mybir.dt.bfloat16
I32 = mybir.dt.int32
ALU = mybir.AluOpType
AF = mybir.ActivationFunctionType
AX = mybir.AxisListType.X

NCORES = 8
T = 1024
NT = 8
D = 2048
NH = 16
DFF = 8192
NKC = 16
EPS_LN = 1e-5
EPS_AT = 1e-6
EPS_Q = 1e-5
RHALF = 0.49999994

ENGS = ("sync", "vector", "scalar", "tensor", "gpsimd")


class Sem:
    def __init__(self, name):
        self.name = name
        self.n = 0
        self.h = None

    def bump(self, k=1):
        self.n += k
        return self.n


class Prog:
    def __init__(self):
        self.q = {k: [] for k in ENGS}

    def op(self, eng, meth, *args, wait=(), inc=(), **kw):
        def f(e, meth=meth, args=args, kw=kw, wait=tuple(wait), inc=tuple(inc)):
            for s, v in wait:
                e.wait_ge(s.h, v)
            bi = getattr(e, meth)(*args, **kw)
            for s, k in inc:
                bi.then_inc(s.h, k)
        self.q[eng].append(f)

    def wait(self, eng, sem, val):
        def f(e, sem=sem, val=val):
            e.wait_ge(sem.h, val)
        self.q[eng].append(f)


def build_program(stop="full", b2er=4, b2h=16, noil=""):
    nc = bass.Bass("TRN2", target_bir_lowering=False, detect_race_conditions=False)

    def din(name, shape, dt=F32):
        return nc.dram_tensor(name, shape, dt, kind="ExternalInput").ap()

    x_in = din("x", [T, D])
    Wq = din("Wq", [D, D]); Wk = din("Wk", [D, D]); Wv = din("Wv", [D, D]); Wo = din("Wo", [D, D])
    bqk_in = din("bqk", [128, 32])
    bv_in = din("bv_b", [128, D])
    bo_in = din("bo_b", [128, D])
    w1s_in = din("w1s", [256, DFF])
    w2s_in = din("w2s", [T, D])
    maskT_in = din("maskT", [128, 128], BF16)
    eye_in = din("eye", [128, 128], BF16)
    eyef_in = din("eyef32", [128, 128], F32)
    ones_in = din("onescol", [128, 1], BF16)
    pmask_in = din("pmask", [128, 8])
    cstf_in = din("cstf", [128, 2])

    y_out = nc.dram_tensor("y", [T, D], F32, kind="ExternalOutput").ap()

    def dint(name, shape, dt=F32):
        return nc.dram_tensor(name, shape, dt, kind="Internal").ap()

    prefixd = dint("prefixd", [NT * 128, NH * 129])
    sc_in = dint("sc_in", [128, NH * 129])
    sc_out = dint("sc_out", [NCORES * 128, NH * 129])
    p1c_d = dint("p1c_d", [128, 1]); p2c_d = dint("p2c_d", [128, 1])
    scl_in = dint("scl_in", [1, 2]); scl_out = dint("scl_out", [1, 2])
    q1s_d = dint("q1s_d", [256, DFF], BF16)
    q1_d = dint("q1_d", [D, DFF], BF16)
    q2s_d = dint("q2s_d", [T, D], BF16)
    q2_d = dint("q2_d", [DFF, D], BF16)
    x2_d = dint("x2_d", [T, D])
    wo_bf = dint("wo_bf", [D, D], BF16)
    xint_d = dint("xint_d", [T, D])
    hr2_d = dint("hr2_d", [DFF, T])
    b1_d = dint("b1_d", [1, T]); b2_d = dint("b2_d", [1, T])

    SBB = (nc.sbuf_base + 31) // 32 * 32

    def sb(name, off, shape, dt):
        return nc.alloc_sbuf_tensor_at(name, shape, dt, offset=SBB + off).ap()

    xnfm = sb("xnfm", 0, [128, 16384], BF16)
    wob0 = sb("wob0", 16384, [128, 512], BF16)
    wob1 = sb("wob1", 17440, [128, 512], BF16)
    Pc = sb("Pc", 0, [128, NH * 129], F32)
    tbuf = sb("tbuf", 0, [128, 2048], F32)
    x2b = sb("x2b", 8192, [128, 2048], F32)
    xn2fm = sb("xn2fm", 0, [128, 16384], BF16)

    qf = sb("qf", 32768, [128, 16384], BF16)
    xb0 = sb("xb0", 32768, [128, 2048], F32)
    xb1 = sb("xb1", 40960, [128, 2048], F32)
    xnstg = sb("xnstg", 49152, [128, 2048], BF16)
    xnstg1 = sb("xnstg1", 53248, [128, 2048], BF16)
    sqdump = sb("sqdump", 57344, [128, 2048], BF16)
    wsl0 = sb("wsl0", 180480, [128, 1024], F32)
    wsl1 = sb("wsl1", 184576, [128, 1024], F32)
    qa = sb("qa", 188672, [128, 1024], F32)
    qb = sb("qb", 192768, [128, 1024], F32)
    swsl0 = sb("swsl0", 196608, [128, 512], F32)
    swsl1 = sb("swsl1", 198656, [128, 512], F32)
    wbs0 = sb("wbs0", 110592, [128, 2048], BF16)
    wbs1 = sb("wbs1", 114688, [128, 2048], BF16)
    yo = sb("yo", 32768, [128, 16384], F32)
    outsb = sb("outsb", 32768, [128, 16384], F32)

    kffm = sb("kffm", 65536, [128, 16384], BF16)
    qout = sb("qout", 205056, [128, 1024], BF16)
    xn2stg = sb("xn2stg", 69632, [128, 2048], BF16)
    xn2stg1 = sb("xn2stg1", 73728, [128, 2048], BF16)
    sqd2 = sb("sqd2", 77824, [128, 2048], BF16)
    qstr0 = sb("qstr0", 32768, [128, 16384], BF16)
    qstr1 = sb("qstr1", 65536, [128, 16384], BF16)

    As = sb("As", 98304, [128, 16384], BF16)
    elu_m = sb("elu_m", 98304, [128, 1024], F32)
    elu_e = sb("elu_e", 102400, [128, 1024], F32)
    elu_p = sb("elu_p", 106496, [128, 1024], F32)
    r2e0 = sb("r2e0", 98304, [128, 1024], F32)
    r2e1 = sb("r2e1", 102400, [128, 1024], F32)
    r2tmp = sb("r2tmp", 106496, [128, 1024], F32)
    f4dump = sb("f4dump", 98304, [128, 2048], BF16)
    runmax = sb("runmax", 110592, [128, 1024], F32)
    r2r = [sb(f"r2r{i}", 114688 + 4096 * i, [128, 1024], F32) for i in range(4)]

    vtm = sb("vtm", 131072, [128, 16384], BF16)
    b2dump = sb("b2dump", 131072, [128, 512], BF16)
    r2rb = [sb(f"r2rb{i}", 131072 + 4096 * i, [128, 1024], F32) for i in range(4)]
    q2r = [sb(f"q2r{i}", 147456 + 4096 * i, [128, 2048], BF16) for i in range(4)]

    stripes = sb("stripes", 163840, [128, 16384], BF16)
    state = sb("state", 163840, [128, NH * 129], F32)
    kftm0 = sb("kftm0", 172096, [128, 2048], BF16)
    kftm1 = sb("kftm1", 176192, [128, 2048], BF16)
    prefb = sb("prefb", 163840, [128, NH * 129], F32)
    statebf = sb("statebf", 172096, [128, NH * 129], BF16)
    o_tm = sb("o_tm", 176256, [128, 2048], BF16)
    zcol = sb("zcol", 180352, [128, 16], F32)
    rcol = sb("rcol", 180416, [128, 16], F32)
    q2rb = [sb(f"q2rb{i}", 163840 + 4096 * i, [128, 2048], BF16) for i in range(4)]
    hqr = [sb(f"hqr{i}", 180480 + 2048 * i, [128, 1024], BF16) for i in range(4)]
    hqrb = [sb(f"hqrb{i}", 98304 + 2048 * i, [128, 1024], BF16) for i in range(4)]
    intq = sb("intq", 188672, [128, 1024], I32)
    bvbo = sb("bvbo", 200960, [128, 2048], BF16)
    b1b = sb("b1b", 192768, [128, 1024], F32)
    b2b = sb("b2b", 196864, [128, 1024], F32)
    xbuf = sb("xbuf", 200960, [128, 2048], F32)
    cst = 209152

    def cst_sb(name, shape, dt):
        nonlocal cst
        t = sb(name, cst, shape, dt)
        nbytes = shape[1] * mybir.dt.size(dt)
        cst += (nbytes + 31) // 32 * 32
        return t

    bqk = cst_sb("bqk", [128, 32], F32)
    maskT = cst_sb("maskT", [128, 128], BF16)
    eye = cst_sb("eye", [128, 128], BF16)
    eyef = cst_sb("eyef", [128, 128], F32)
    onescol = cst_sb("onescol_s", [128, 1], BF16)
    pmask = cst_sb("pmask_s", [128, 8], F32)
    st_s1 = cst_sb("st_s1", [128, 8], F32)
    st_m = cst_sb("st_m", [128, 8], F32)
    st_s2 = cst_sb("st_s2", [128, 8], F32)
    st_v = cst_sb("st_v", [128, 8], F32)
    st_r = cst_sb("st_r", [128, 8], F32)
    st_t1 = cst_sb("st_t1", [128, 8], F32)
    scl2 = cst_sb("scl2", [128, 4], F32)
    pr12 = cst_sb("pr12", [128, 2], F32)
    prrow_t = cst_sb("prrow", [1, 128], F32)
    prsc = cst_sb("prsc", [1, 2], F32)
    xmaxc = cst_sb("xmaxc", [128, 8], F32)
    b1c = cst_sb("b1c", [128, 8], F32)
    b2c = cst_sb("b2c", [128, 8], F32)
    cstf = cst_sb("cstf", [128, 2], F32)
    dumc = cst_sb("dumc", [128, 1], F32)
    nc.const_aps.register(F32, EPS_LN, cstf[:, 0:1])
    nc.const_aps.register(F32, -0.5, cstf[:, 1:2])
    assert cst <= 211328, cst
    nc.bump_sbuf(211328)

    pq = [nc.alloc_psum_tensor(f"pq{i}", [128, 1024], F32).ap() for i in range(4)]
    pqb = [p.bitcast(BF16) for p in pq]

    sems = {n: Sem(n) for n in ("sLD", "sWD", "sST", "sPE", "sDV", "sAC", "sCC", "sTT")}
    sLD, sWD, sST, sPE, sDV, sAC, sCC, sTT = (sems[n] for n in
        ("sLD", "sWD", "sST", "sPE", "sDV", "sAC", "sCC", "sTT"))

    P = Prog()

    def finish_early(copy_from=None, sb_dump=None):
        P.op("vector", "memset", x2b, 0.0, wait=((sST, sST.n),), inc=((sDV, sDV.bump() and 1),))
        zv = sDV.n
        if copy_from is not None:
            dma("sync", y_out.flatten(), copy_from, sST,
                wait=((sDV, zv), (sST, sST.n), (sCC, sCC.n)))
        elif sb_dump is not None:
            for j in range(NT):
                dma("sync", y_out[j * 128:(j + 1) * 128, :], x2b, sST, wait=((sDV, zv),))
            dma("gpsimd", y_out[0:128, 0:sb_dump.shape[1]], sb_dump, sST,
                wait=((sST, sST.n), (sAC, sAC.n), (sDV, sDV.n), (sPE, sPE.n), (sCC, sCC.n)))
        else:
            for j in range(NT):
                dma("sync", y_out[j * 128:(j + 1) * 128, :], x2b, sST, wait=((sDV, zv),))
        P.wait("sync", sST, sST.n)
        _emit(nc, P, sems)
        return nc

    def dma(eng, out, in_, sem, wait=()):
        P.op(eng, "dma_start", wait=wait, inc=((sem, sem.bump(16) and 16),), out=out, in_=in_)

    for dst, src in ((bqk, bqk_in), (maskT, maskT_in), (eye, eye_in), (eyef, eyef_in),
                     (onescol, ones_in), (pmask, pmask_in), (cstf, cstf_in)):
        dma("sync", dst, src, sLD)
    s0_done = sLD.n
    P.wait("tensor", sLD, s0_done)
    P.wait("vector", sLD, s0_done)
    P.wait("scalar", sLD, s0_done)

    def ln_sweep(src_dram, dst_fm, stg_pair, dump, bank_lo, extra_first_wait=(),
                 bounce_dst=None, bounce_st_out=None):
        tr_done_prev = None
        prev_evac = None
        for j in range(NT):
            xb = xb0 if j % 2 == 0 else xb1
            stg = stg_pair[j % 2]
            w = list(extra_first_wait) if j < 2 else [(sAC, xn_ready[j - 2])]
            if bounce_dst is not None and j >= 2:
                w.append((sST, bounce_st_out[j - 2]))
            dma("sync", xb, src_dram[j * 128:(j + 1) * 128, :], sLD, wait=tuple(w))
            ldv = sLD.n
            if bounce_dst is not None:
                dma("sync", bounce_dst[j * 128:(j + 1) * 128, :], xb, sST,
                    wait=((sLD, ldv),))
                bounce_st_out[j] = sST.n
            P.wait("vector", sLD, ldv)
            P.op("vector", "reduce_sum", st_s1[:, j:j + 1], xb, axis=AX, inc=((sDV, sDV.bump() and 1),))
            P.wait("vector", sDV, sDV.n)
            P.wait("scalar", sLD, ldv)
            P.op("scalar", "activation", dump, xb, AF.Square, accum_out=st_s2[:, j:j + 1],
                 inc=((sAC, sAC.bump() and 1),))
            sq_done = sAC.n
            P.op("vector", "tensor_scalar", st_m[:, j:j + 1], st_s1[:, j:j + 1], 1.0 / D, None, ALU.mult,
                 inc=((sDV, sDV.bump() and 1),))
            P.wait("vector", sAC, sq_done)
            P.op("vector", "tensor_scalar", st_s2[:, j:j + 1], st_s2[:, j:j + 1], 1.0 / D, None, ALU.mult,
                 inc=((sDV, sDV.bump() and 1),))
            P.wait("vector", sDV, sDV.n)
            P.op("vector", "tensor_mul", st_t1[:, j:j + 1], st_m[:, j:j + 1], st_m[:, j:j + 1],
                 inc=((sDV, sDV.bump() and 1),))
            P.wait("vector", sDV, sDV.n)
            P.op("vector", "tensor_sub", st_v[:, j:j + 1], st_s2[:, j:j + 1], st_t1[:, j:j + 1],
                 inc=((sDV, sDV.bump() and 1),))
            var_done = sDV.n
            P.wait("scalar", sDV, var_done)
            P.op("scalar", "activation", st_t1[:, j:j + 1], st_v[:, j:j + 1], AF.Ln, bias=EPS_LN,
                 inc=((sAC, sAC.bump() and 1),))
            P.wait("scalar", sAC, sAC.n)
            P.op("scalar", "activation", st_r[:, j:j + 1], st_t1[:, j:j + 1], AF.Exp, scale=-0.5,
                 inc=((sAC, sAC.bump() and 1),))
            rstd_done = sAC.n
            P.wait("vector", sAC, rstd_done)
            P.op("vector", "tensor_scalar", st_t1[:, j:j + 1], st_m[:, j:j + 1],
                 st_r[:, j:j + 1], -1.0, ALU.mult, ALU.mult, inc=((sDV, sDV.bump() and 1),))
            bc_v = sDV.n
            if tr_done_prev is not None:
                P.wait("scalar", sTT, tr_done_prev)
            P.wait("scalar", sDV, bc_v)
            P.op("scalar", "activation", stg, xb, AF.Identity, bias=st_t1[:, j:j + 1],
                 scale=st_r[:, j:j + 1], inc=((sAC, sAC.bump() and 1),))
            if j == 0:
                xn_ready = {}
            xn_ready[j] = sAC.n
            for k in range(NKC):
                P.op("sync", "dma_start",
                     wait=((sAC, xn_ready[j]),) if k == 0 else (),
                     inc=((sTT, sTT.bump(16) and 16),),
                     out=dst_fm[:, k * 1024 + j * 128: k * 1024 + (j + 1) * 128],
                     in_=stg[:, k * 128:(k + 1) * 128], transpose=True)
            tr_done_prev = sTT.n
        return sTT.n

    xb_bounce = {}
    s1_done_ac = ln_sweep(x_in, xnfm, (xnstg, xnstg1), sqdump, 0,
                          bounce_dst=xint_d, bounce_st_out=xb_bounce)
    if stop == "s1":
        return finish_early()
    if stop == "s1x":
        return finish_early(sb_dump=xnfm[:, 0:2048])

    def proj_sweep_fm(Wsrc, out_sbuf, bias_off, first_wait=(), slot_hook=None):
        stripe_done = {}
        bank_free = {}
        for qt in range(4):
            stripe = stripes[:, (qt % 2) * 8192:(qt % 2) * 8192 + 8192]
            w = ((sPE, stripe_done[qt - 2]),) if qt >= 2 else tuple(first_wait)
            for k in range(NKC):
                dma("gpsimd", stripe[:, k * 512:(k + 1) * 512],
                    Wsrc[k * 128:(k + 1) * 128, qt * 512:(qt + 1) * 512], sWD,
                    wait=w if k == 0 else ())
            wv = sWD.n
            P.wait("tensor", sWD, wv)
            if qt == 0:
                P.wait("tensor", sTT, s1_done_ac)
            for hh in range(4):
                h = qt * 4 + hh
                pp = pq[hh]
                if hh in bank_free:
                    P.wait("tensor", sDV, bank_free[hh])
                for k in range(NKC):
                    lhsT = stripe[:, k * 512 + hh * 128: k * 512 + (hh + 1) * 128]
                    for half in range(2):
                        P.op("tensor", "matmul", pp[:, half * 512:(half + 1) * 512],
                             lhsT, xnfm[:, k * 1024 + half * 512: k * 1024 + (half + 1) * 512],
                             start=(k == 0), stop=(k == NKC - 1),
                             inc=((sPE, sPE.bump() and 1),) if (k == NKC - 1 and half == 1) else ())
                    if slot_hook is not None and k % 4 == 0:
                        slot_hook()
                mm_v = sPE.n
                bcol = bqk[:, bias_off + h: bias_off + h + 1]
                P.wait("vector", sPE, mm_v)
                P.op("vector", "tensor_scalar", elu_m, pp[:], bcol, 0.0, ALU.add, ALU.min,
                     inc=((sDV, sDV.bump() and 1),))
                mv = sDV.n
                P.wait("scalar", sDV, mv)
                P.op("scalar", "activation", elu_e, elu_m, AF.Exp, inc=((sAC, sAC.bump() and 1),))
                ev = sAC.n
                P.op("vector", "tensor_scalar", elu_p, pp[:], bcol, 0.0, ALU.add, ALU.max,
                     inc=((sDV, sDV.bump() and 1),))
                P.wait("vector", sDV, sDV.n)
                P.wait("vector", sAC, ev)
                P.op("vector", "tensor_add", out_sbuf[:, h * 1024:(h + 1) * 1024], elu_p, elu_e,
                     inc=((sDV, sDV.bump() and 1),))
                bank_free[hh] = sDV.n
            stripe_done[qt] = sPE.n
        return sDV.n

    if "f" not in noil:
        P.op("vector", "memset", pr12, 0.0, wait=((sLD, s0_done),),
             inc=((sDV, sDV.bump() and 1),))
    sc_pieces = []
    for (srcw, rows, colw, colidx) in ((w1s_in, 2, DFF, 0), (w2s_in, 8, D, 1)):
        srcv = srcw.rearrange("(n p) e -> n p e", p=128)
        for rr in range(rows):
            for cc in range(colw // 512):
                sc_pieces.append((srcv[rr, :, cc * 512:(cc + 1) * 512], colidx))
    sc_red_done = {}
    sc_state = {"i": 0}

    def scale_piece_hook():
        pi = sc_state["i"]
        if pi >= len(sc_pieces):
            return
        sc_state["i"] = pi + 1
        view, colidx = sc_pieces[pi]
        wb = swsl0 if pi % 2 == 0 else swsl1
        w = ((sDV, sc_red_done[pi - 2]),) if pi >= 2 else ()
        dma("sync", wb, view, sLD, wait=w)
        lv = sLD.n
        P.wait("vector", sLD, lv)
        P.op("vector", "reduce_sum", b1c[:, 0:1], wb, axis=AX, apply_absolute_value=True,
             inc=((sDV, sDV.bump() and 1),))
        P.wait("vector", sDV, sDV.n)
        P.op("vector", "tensor_add", pr12[:, colidx:colidx + 1], pr12[:, colidx:colidx + 1],
             b1c[:, 0:1], inc=((sDV, sDV.bump() and 1),))
        sc_red_done[pi] = sDV.n
        P.wait("vector", sDV, sDV.n)

    k_done_dv = proj_sweep_fm(
        Wk, kffm, 16,
        slot_hook=None if ("s" in noil or "f" in noil) else scale_piece_hook)
    if "f" not in noil:
        while sc_state["i"] < len(sc_pieces):
            scale_piece_hook()
    if stop == "s2":
        return finish_early()
    if stop == "s2x":
        return finish_early(sb_dump=kffm[:, 0:2048])

    if "f" in noil:
        prsc_copy_v = 0
        scl_store_v = sST.n
    else:
        scale_red_dv = sDV.n
        P.wait("tensor", sDV, scale_red_dv)
        P.op("tensor", "transpose", pq[3][0:2, 512:640], pr12, eyef,
             inc=((sPE, sPE.bump() and 1),))
        trv = sPE.n
        P.wait("vector", sPE, trv)
        P.op("vector", "tensor_copy", r2e0[0:2, 0:128], pq[3][0:2, 512:640],
             inc=((sDV, sDV.bump() and 1),))
        P.wait("vector", sDV, sDV.n)
        P.op("vector", "reduce_sum", r2e0[0:2, 128:129], r2e0[0:2, 0:128], axis=AX,
             inc=((sDV, sDV.bump() and 1),))
        prr_v = sDV.n
        P.wait("tensor", sDV, prr_v)
        P.op("tensor", "transpose", pq[3][0:1, 768:776], r2e0[0:8, 128:129], eyef[0:8, 0:8],
             inc=((sPE, sPE.bump() and 1),))
        trv2 = sPE.n
        P.wait("vector", sPE, trv2)
        P.op("vector", "tensor_copy", prsc, pq[3][0:1, 768:770],
             inc=((sDV, sDV.bump() and 1),))
        prsc_copy_v = sDV.n
        dma("sync", scl_in, prsc, sST, wait=((sDV, prsc_copy_v),))
        scl_store_v = sST.n

    dma("gpsimd", bvbo, bv_in, sWD, wait=((sDV, k_done_dv),))
    bv_ready = sWD.n
    wo_bounce_st = {}
    v_stripe_done = {}
    v_tile_done = {}
    for er in range(4):
        stripe = stripes[:, (er % 2) * 8192:(er % 2) * 8192 + 8192]
        w = ((sPE, v_stripe_done[er - 2]),) if er >= 2 else ((sDV, k_done_dv),)
        for k in range(NKC):
            dma("gpsimd", stripe[:, k * 512:(k + 1) * 512],
                Wv[k * 128:(k + 1) * 128, er * 512:(er + 1) * 512], sWD,
                wait=w if k == 0 else ())
        wv = sWD.n
        for c in range(4):
            hbc = er * 4 + c
            wbb = wbs0 if hbc % 2 == 0 else wbs1
            bw = ((sST, wo_bounce_st[hbc - 2]),) if hbc >= 2 else ()
            dma("gpsimd", wbb, Wo[hbc * 128:(hbc + 1) * 128, :], sWD, wait=bw)
            blv = sWD.n
            dma("sync", wo_bf[hbc * 128:(hbc + 1) * 128, :], wbb, sST, wait=((sWD, blv),))
            wo_bounce_st[hbc] = sST.n
        P.wait("tensor", sWD, wv)
        if er == 0:
            P.wait("tensor", sDV, k_done_dv)
        for j in range(NT):
            pp = pq[j // 2][:, (j % 2) * 512:(j % 2) * 512 + 512]
            if er >= 1:
                P.wait("tensor", sDV, v_tile_done[(er - 1, j)])
            if er == 0 and j == 7 and "f" not in noil:
                P.wait("tensor", sDV, prsc_copy_v)
            for k in range(NKC):
                P.op("tensor", "matmul", pp,
                     xnfm[:, k * 1024 + j * 128: k * 1024 + (j + 1) * 128],
                     stripe[:, k * 512:(k + 1) * 512],
                     start=(k == 0), stop=(k == NKC - 1),
                     inc=((sPE, sPE.bump() and 1),) if k == NKC - 1 else ())
            mm_v = sPE.n
            P.wait("vector", sPE, mm_v)
            if er == 0 and j == 0:
                P.wait("vector", sWD, bv_ready)
            P.op("vector", "tensor_add",
                 vtm[:, j * 2048 + er * 512: j * 2048 + (er + 1) * 512],
                 pp, bvbo[:, er * 512:(er + 1) * 512], inc=((sDV, sDV.bump() and 1),))
            v_tile_done[(er, j)] = sDV.n
        v_stripe_done[er] = sPE.n
    v_done_dv = sDV.n
    if stop == "s4":
        return finish_early()
    if stop == "s4x":
        import os as _os
        _o = int(_os.environ.get("KMK_DOFF", "0"))
        return finish_early(sb_dump=vtm[:, _o:_o + 2048])

    if "f" in noil:
        scales_done = 0
    else:
        P.op("gpsimd", "collective_compute", "AllReduce", ALU.add,
             [list(range(NCORES))], [scl_in.opt()], [scl_out.opt()],
             wait=((sST, scl_store_v),), inc=((sCC, sCC.bump() and 1),))
        scale_ar = sCC.n
        dma("sync", scl2[:, 0:2], scl_out.broadcast_to([128, 2]), sLD, wait=((sCC, scale_ar),))
        scl_bcast_v = sLD.n
        P.wait("vector", sLD, scl_bcast_v)
        P.op("vector", "tensor_scalar", scl2[:, 2:3], scl2[:, 0:1], 1.0 / (D * DFF), EPS_Q,
             ALU.mult, ALU.max, inc=((sDV, sDV.bump() and 1),))
        P.op("vector", "tensor_scalar", scl2[:, 3:4], scl2[:, 1:2], 1.0 / (D * DFF), EPS_Q,
             ALU.mult, ALU.max, inc=((sDV, sDV.bump() and 1),))
        P.wait("vector", sDV, sDV.n)
        P.op("vector", "reciprocal", scl2[:, 0:1], scl2[:, 2:3],
             inc=((sDV, sDV.bump() and 1),))
        P.op("vector", "reciprocal", scl2[:, 1:2], scl2[:, 3:4],
             inc=((sDV, sDV.bump() and 1),))
        P.wait("vector", sDV, sDV.n)
        scales_done = sDV.n

    P.wait("tensor", sDV, v_done_dv)
    P.op("vector", "memset", state, 0.0, wait=((sDV, v_done_dv),), inc=((sDV, sDV.bump() and 1),))
    state_zero = sDV.n
    prev_state_add = state_zero
    prefix_written = []
    kv_evac = {}
    kv_mm_done = {}
    for j in range(NT):
        ktb = kftm0 if j % 2 == 0 else kftm1
        for h in range(NH):
            w = []
            if h == 0:
                w.append((sDV, v_done_dv))
                if j >= 2:
                    w.append((sPE, kv_mm_done[j - 2]))
            P.op("sync", "dma_start", wait=tuple(w),
                 inc=((sTT, sTT.bump(16) and 16),),
                 out=ktb[:, h * 128:(h + 1) * 128],
                 in_=kffm[:, h * 1024 + j * 128: h * 1024 + (j + 1) * 128], transpose=True)
        kt_done = sTT.n
        dma("sync", prefixd[j * 128:(j + 1) * 128, :], state, sST,
            wait=((sDV, prev_state_add),))
        prefix_written.append(sST.n)
        P.wait("tensor", sTT, kt_done)
        for h in range(NH):
            i = j * NH + h
            pp = pq[h % 2]
            if i >= 2:
                P.wait("tensor", sDV, kv_evac[i - 2])
            P.op("tensor", "matmul", pp[:, 0:128], ktb[:, h * 128:(h + 1) * 128],
                 vtm[:, j * 2048 + h * 128: j * 2048 + (h + 1) * 128], start=True, stop=True)
            P.op("tensor", "matmul", pp[:, 128:129], ktb[:, h * 128:(h + 1) * 128],
                 onescol, start=True, stop=True, inc=((sPE, sPE.bump() and 1),))
            mm_v = sPE.n
            P.wait("vector", sPE, mm_v)
            if h == 0:
                P.wait("vector", sST, prefix_written[j])
            P.op("vector", "tensor_add", state[:, h * 129:(h + 1) * 129],
                 state[:, h * 129:(h + 1) * 129], pp[:, 0:129], inc=((sDV, sDV.bump() and 1),))
            kv_evac[i] = sDV.n
        prev_state_add = sDV.n
        kv_mm_done[j] = sPE.n

    dma("gpsimd", bvbo, bo_in, sWD, wait=((sDV, v_done_dv),))
    bo_ready = sWD.n
    dma("sync", sc_in, state, sST, wait=((sDV, prev_state_add),))
    scin_done = sST.n
    P.op("gpsimd", "collective_compute", "AllGather", ALU.bypass,
         [list(range(NCORES))], [sc_in.opt()], [sc_out.opt()],
         wait=((sST, scin_done),), inc=((sCC, sCC.bump() and 1),))
    state_ag_done = sCC.n
    if stop == "s5":
        return finish_early()

    q_done_dv = proj_sweep_fm(Wq, qf, 0, first_wait=((sST, scin_done),))
    if stop == "s6":
        return finish_early()

    q_pieces = []
    for (srcw, dstw, rows, colw, invc) in ((w1s_in, q1s_d, 2, DFF, 0),
                                           (w2s_in, q2s_d, 8, D, 1)):
        srcv = srcw.rearrange("(n p) e -> n p e", p=128)
        dstv = dstw.rearrange("(n p) e -> n p e", p=128)
        for rr in range(rows):
            for cc in range(colw // 1024):
                q_pieces.append((srcv[rr, :, cc * 1024:(cc + 1) * 1024],
                                 dstv[rr, :, cc * 1024:(cc + 1) * 1024], invc))
    qz_rd_done = {}
    qz_store = {}
    qz_state = {"i": 0}
    qz_ag = []

    def quant_piece_hook():
        pi = qz_state["i"]
        if pi >= len(q_pieces):
            return
        qz_state["i"] = pi + 1
        view, dview, invc = q_pieces[pi]
        wb = wsl0 if pi % 2 == 0 else wsl1
        w = ((sDV, qz_rd_done[pi - 2]),) if pi >= 2 else ((sDV, q_done_dv),)
        dma("sync", wb, view, sLD, wait=w)
        lv = sLD.n
        P.wait("vector", sLD, lv)
        if pi == 0:
            P.wait("vector", sDV, scales_done)
        if pi >= 1:
            P.wait("vector", sST, qz_store[pi - 1])
        P.op("vector", "tensor_scalar", qa, wb, scl2[:, invc:invc + 1], 0.5,
             ALU.mult, ALU.is_ge, inc=((sDV, sDV.bump() and 1),))
        P.op("vector", "tensor_scalar", qb, wb, scl2[:, invc:invc + 1], -0.5,
             ALU.mult, ALU.is_ge, inc=((sDV, sDV.bump() and 1),))
        qz_rd_done[pi] = sDV.n
        P.wait("vector", sDV, sDV.n)
        P.op("vector", "scalar_tensor_tensor", qout, qa, -1.0, qb, ALU.add, ALU.add,
             inc=((sDV, sDV.bump() and 1),))
        dma("sync", dview, qout, sST, wait=((sDV, sDV.n),))
        qz_store[pi] = sST.n

    P.wait("tensor", sDV, q_done_dv)
    at_evac = {}
    for j in range(NT):
        for h in range(NH):
            i = j * NH + h
            pp = pq[i % 4]
            if i >= 4:
                P.wait("tensor", sDV, at_evac[i - 4])
            P.op("tensor", "matmul", pp[:, 0:128],
                 kffm[:, h * 1024 + j * 128: h * 1024 + (j + 1) * 128],
                 qf[:, h * 1024 + j * 128: h * 1024 + (j + 1) * 128],
                 start=True, stop=True, inc=((sPE, sPE.bump() and 1),))
            mm_v = sPE.n
            P.wait("vector", sPE, mm_v)
            P.op("vector", "tensor_mul", As[:, i * 128:(i + 1) * 128], pp[:, 0:128], maskT,
                 inc=((sDV, sDV.bump() and 1),))
            at_evac[i] = sDV.n
        if not any(c in noil for c in "qQf"):
            for _ in range(4):
                quant_piece_hook()
    if not any(c in noil for c in "Qf"):
        while qz_state["i"] < len(q_pieces):
            quant_piece_hook()
    if not any(c in noil for c in "QfG"):
        P.op("gpsimd", "collective_compute", "AllGather", ALU.bypass,
             [list(range(NCORES))], [q1s_d.opt()], [q1_d.opt()],
             wait=((sST, sST.n),), inc=((sCC, sCC.bump() and 1),))
        P.op("gpsimd", "collective_compute", "AllGather", ALU.bypass,
             [list(range(NCORES))], [q2s_d.opt()], [q2_d.opt()],
             inc=((sCC, sCC.bump() and 1),))
        qz_ag.append(sCC.n)
    q_ag_done = qz_ag[-1] if qz_ag else 0
    as_done = sDV.n
    if stop == "s7":
        return finish_early()
    if stop == "s7x":
        import os as _os
        _o = int(_os.environ.get("KMK_DOFF", "0"))
        return finish_early(sb_dump=As[:, _o:_o + 2048])

    P.op("vector", "memset", Pc, 0.0, inc=((sDV, sDV.bump() and 1),))
    pc_zero = sDV.n
    P.wait("sync", sCC, state_ag_done)
    pc_acc = pc_zero
    for r in range(NCORES - 1):
        w = [(sDV, pc_acc)]
        if r == 0:
            w.append((sDV, q_done_dv))
        dma("sync", prefb, sc_out[r * 128:(r + 1) * 128, :], sLD, wait=tuple(w))
        rv = sLD.n
        P.wait("vector", sLD, rv)
        P.wait("vector", sDV, pc_acc)
        P.op("vector", "scalar_tensor_tensor", Pc, prefb, pmask[:, r:r + 1], Pc,
             ALU.mult, ALU.add, inc=((sDV, sDV.bump() and 1),))
        pc_acc = sDV.n
    pc_done = pc_acc
    if stop == "b0":
        return finish_early()

    oa_evac = {}
    ofm_evacs = []
    for j in range(NT):
        dma("sync", prefb, prefixd[j * 128:(j + 1) * 128, :], sLD,
            wait=((sDV, pc_done if j == 0 else stbf_used),))
        rv = sLD.n
        P.wait("vector", sLD, rv)
        P.op("vector", "tensor_add", statebf, prefb, Pc, inc=((sDV, sDV.bump() and 1),))
        stbf_v = sDV.n
        P.wait("tensor", sDV, stbf_v)
        if j == 0:
            P.wait("tensor", sDV, as_done)
        for h in range(NH):
            i = j * NH + h
            pp = pq[i % 2]
            if i >= 2:
                P.wait("tensor", sDV, oa_evac[i - 2])
            P.op("tensor", "matmul", pp[:, 0:128], As[:, i * 128:(i + 1) * 128],
                 vtm[:, j * 2048 + h * 128: j * 2048 + (h + 1) * 128], start=True, stop=False)
            P.op("tensor", "matmul", pp[:, 128:129], As[:, i * 128:(i + 1) * 128],
                 onescol, start=False, stop=False)
            P.op("tensor", "matmul", pp[:, 0:129],
                 qf[:, h * 1024 + j * 128: h * 1024 + (j + 1) * 128],
                 statebf[:, h * 129:(h + 1) * 129], start=False, stop=True,
                 inc=((sPE, sPE.bump() and 1),))
            mm_v = sPE.n
            P.wait("vector", sPE, mm_v)
            P.op("vector", "tensor_scalar", zcol[:, h:h + 1], pp[:, 128:129], EPS_AT, None, ALU.add,
                 inc=((sDV, sDV.bump() and 1),))
            P.wait("vector", sDV, sDV.n)
            P.op("vector", "reciprocal", rcol[:, h:h + 1], zcol[:, h:h + 1],
                 inc=((sDV, sDV.bump() and 1),))
            P.wait("vector", sDV, sDV.n)
            P.op("vector", "tensor_scalar", o_tm[:, h * 128:(h + 1) * 128], pp[:, 0:128],
                 rcol[:, h:h + 1], None, ALU.mult, inc=((sDV, sDV.bump() and 1),))
            oa_evac[i] = sDV.n
        stbf_used = sDV.n
        otm_done = sDV.n
        for h in range(NH):
            P.op("sync", "dma_start",
                 wait=((sDV, otm_done),) if h == 0 else (),
                 inc=((sTT, sTT.bump(16) and 16),),
                 out=As[:, (j * NH + h) * 128:(j * NH + h + 1) * 128],
                 in_=o_tm[:, h * 128:(h + 1) * 128], transpose=True)
        ofm_evacs.append(sTT.n)
    b1_done_ac = sTT.n
    b1_done_dv = sDV.n
    if stop == "f0q":
        return finish_early(q1_d.flatten().bitcast(F32)[0:T * D])
    if stop == "f0q2":
        return finish_early(q2_d.flatten().bitcast(F32)[0:T * D])
    if stop == "b1":
        return finish_early()
    if stop == "b1x":
        import os as _os
        _o = int(_os.environ.get("KMK_DOFF", "0"))
        return finish_early(sb_dump=As[:, _o:_o + 2048])

    b2_extra = ((sCC, q_ag_done),) if "W" in noil else ()
    P.op("vector", "memset", st_s1, 0.0, wait=((sTT, b1_done_ac),))
    P.op("vector", "memset", st_s2, 0.0, inc=((sDV, sDV.bump() and 1),))
    b2_zero = sDV.n
    wo_h_done = {}
    wo_evac = {}
    for er in range(b2er):
        for h in range(min(b2h, NH)):
            ih = er * NH + h
            wb = wob0 if ih % 2 == 0 else wob1
            w = [(sPE, wo_h_done[ih - 2])] if ih >= 2 else [(sDV, q_done_dv)] + list(b2_extra)
            w.append((sST, wo_bounce_st[h]))
            dma("sync", wb, wo_bf[h * 128:(h + 1) * 128, er * 512:(er + 1) * 512], sLD,
                wait=tuple(w))
            lv = sLD.n
            P.wait("tensor", sLD, lv)
            if ih == 0:
                P.wait("tensor", sTT, b1_done_ac)
                P.wait("tensor", sDV, b1_done_dv)
            for j in range(NT):
                pp = pq[j // 2][:, (j % 2) * 512:(j % 2) * 512 + 512]
                if h == 0 and er > 0:
                    P.wait("tensor", sDV, wo_evac[(er - 1, j)])
                P.op("tensor", "matmul", pp, As[:, (j * NH + h) * 128:(j * NH + h + 1) * 128],
                     wb, start=(h == 0), stop=(h == min(b2h, NH) - 1),
                     inc=((sPE, sPE.bump() and 1),) if j == NT - 1 else ())
            wo_h_done[ih] = sPE.n
            if h == min(b2h, NH) - 1:
                mm_v = sPE.n
                for j in range(NT):
                    pp = pq[j // 2][:, (j % 2) * 512:(j % 2) * 512 + 512]
                    P.wait("vector", sPE, mm_v)
                    if er == 0 and j == 0:
                        P.wait("vector", sWD, bo_ready)
                        P.wait("vector", sDV, b2_zero)
                    P.op("vector", "tensor_add",
                         yo[:, j * 2048 + er * 512: j * 2048 + (er + 1) * 512],
                         pp, bvbo[:, er * 512:(er + 1) * 512],
                         inc=((sDV, sDV.bump() and 1),))
                    P.wait("vector", sDV, sDV.n)
                    P.op("vector", "reduce_sum", st_m[:, j:j + 1],
                         yo[:, j * 2048 + er * 512: j * 2048 + (er + 1) * 512], axis=AX,
                         inc=((sDV, sDV.bump() and 1),))
                    P.wait("vector", sDV, sDV.n)
                    P.op("vector", "tensor_add", st_s1[:, j:j + 1], st_s1[:, j:j + 1],
                         st_m[:, j:j + 1], inc=((sDV, sDV.bump() and 1),))
                    yov = sDV.n
                    P.wait("scalar", sDV, yov)
                    P.op("scalar", "activation", b2dump,
                         yo[:, j * 2048 + er * 512: j * 2048 + (er + 1) * 512],
                         AF.Square, accum_out=st_t1[:, j:j + 1], inc=((sAC, sAC.bump() and 1),))
                    sqv = sAC.n
                    P.wait("vector", sAC, sqv)
                    P.op("vector", "tensor_add", st_s2[:, j:j + 1], st_s2[:, j:j + 1],
                         st_t1[:, j:j + 1], inc=((sDV, sDV.bump() and 1),))
                    P.wait("vector", sDV, sDV.n)
                    wo_evac[(er, j)] = sDV.n
    b2_done_dv = sDV.n
    if stop == "b2":
        return finish_early()
    if stop == "b2y":
        import os as _os
        _o = int(_os.environ.get("KMK_DOFF", "0"))
        return finish_early(sb_dump=yo[:, _o:_o + 2048])

    P.wait("vector", sDV, b2_done_dv)
    P.op("vector", "tensor_scalar", st_m, st_s1, 1.0 / D, None, ALU.mult,
         inc=((sDV, sDV.bump() and 1),))
    P.op("vector", "tensor_scalar", st_s2, st_s2, 1.0 / D, None, ALU.mult,
         inc=((sDV, sDV.bump() and 1),))
    P.wait("vector", sDV, sDV.n)
    P.op("vector", "tensor_mul", st_t1, st_m, st_m, inc=((sDV, sDV.bump() and 1),))
    P.wait("vector", sDV, sDV.n)
    P.op("vector", "tensor_sub", st_v, st_s2, st_t1, inc=((sDV, sDV.bump() and 1),))
    vv = sDV.n
    P.wait("scalar", sDV, vv)
    P.op("scalar", "activation", st_t1, st_v, AF.Ln, bias=EPS_LN,
         inc=((sAC, sAC.bump() and 1),))
    P.wait("scalar", sAC, sAC.n)
    P.op("scalar", "activation", st_r, st_t1, AF.Exp, scale=-0.5, inc=((sAC, sAC.bump() and 1),))
    rv_ = sAC.n
    P.wait("vector", sAC, rv_)
    for j in range(NT):
        w = ((sDV, sDV.n),) if j >= 1 else ((sDV, rv_ and sDV.n),)
        dma("sync", xbuf, xint_d[j * 128:(j + 1) * 128, :], sLD, wait=w)
        xv = sLD.n
        P.op("vector", "tensor_scalar", tbuf, yo[:, j * 2048:(j + 1) * 2048],
             st_m[:, j:j + 1], st_r[:, j:j + 1], ALU.subtract, ALU.mult,
             inc=((sDV, sDV.bump() and 1),))
        P.wait("vector", sDV, sDV.n)
        P.wait("vector", sLD, xv)
        if j >= 1:
            P.wait("vector", sST, prev_x2_store)
        P.op("vector", "tensor_add", x2b, tbuf, xbuf, inc=((sDV, sDV.bump() and 1),))
        dma("sync", x2_d[j * 128:(j + 1) * 128, :], x2b, sST, wait=((sDV, sDV.n),))
        prev_x2_store = sST.n
    x2_done_st = sST.n
    if stop == "b3":
        return finish_early()
    if stop == "b3x":
        return finish_early(x2_d.flatten())

    pass0_done = ln_sweep(x2_d, xn2fm, (xn2stg, xn2stg1), sqd2, 384,
                          extra_first_wait=((sDV, sDV.n), (sST, x2_done_st)))
    if stop == "f1":
        return finish_early()
    if stop == "f1x":
        import os as _os
        _o = int(_os.environ.get("KMK_DOFF", "0"))
        return finish_early(sb_dump=xn2fm[:, _o:_o + 2048])

    P.op("vector", "memset", runmax, 0.0, wait=((sDV, sDV.n),), inc=((sDV, sDV.bump() and 1),))
    rm_zero = sDV.n
    q1_mm_done = {}
    hc_relu = {}
    hc_sq = {}
    hc_dv = {}
    hc_st = {}
    for g in range(8):
        qs = qstr0 if g % 2 == 0 else qstr1
        w = [(sPE, q1_mm_done[g - 2])] if g >= 2 else [(sCC, q_ag_done), (sTT, pass0_done)]
        for k in range(NKC):
            dma("gpsimd", qs[:, k * 1024:(k + 1) * 1024],
                q1_d[k * 128:(k + 1) * 128, g * 1024:(g + 1) * 1024], sWD,
                wait=tuple(w) if k == 0 else ())
        wv = sWD.n
        P.wait("tensor", sWD, wv)
        if g == 0:
            P.wait("tensor", sTT, pass0_done)
        for hc in range(8):
            Hc = g * 8 + hc
            pp = pq[hc % 2]
            if Hc >= 2:
                P.wait("tensor", sAC, hc_relu[Hc - 2])
            for k in range(NKC):
                lhsT = qs[:, k * 1024 + hc * 128: k * 1024 + (hc + 1) * 128]
                for half in range(2):
                    P.op("tensor", "matmul", pp[:, half * 512:(half + 1) * 512],
                         lhsT, xn2fm[:, k * 1024 + half * 512: k * 1024 + (half + 1) * 512],
                         start=(k == 0), stop=(k == NKC - 1),
                         inc=((sPE, sPE.bump() and 1),) if (k == NKC - 1 and half == 1) else ())
            mm_v = sPE.n
            r2b = r2e0 if Hc % 2 == 0 else r2e1
            P.wait("scalar", sPE, mm_v)
            if Hc >= 2:
                P.wait("scalar", sST, hc_st[Hc - 2])
                P.wait("scalar", sDV, hc_dv[Hc - 2])
            P.op("scalar", "activation", r2tmp, pp[:], AF.Relu, inc=((sAC, sAC.bump() and 1),))
            hc_relu[Hc] = sAC.n
            P.wait("scalar", sAC, sAC.n)
            P.op("scalar", "activation", r2b, r2tmp, AF.Square, scale=scl2[:, 2:3],
                 inc=((sAC, sAC.bump() and 1),))
            hc_sq[Hc] = sAC.n
            P.wait("vector", sAC, hc_sq[Hc])
            if Hc == 0:
                P.wait("vector", sDV, rm_zero)
            P.wait("vector", sDV, sDV.n)
            P.op("vector", "tensor_max", runmax, runmax, r2b, inc=((sDV, sDV.bump() and 1),))
            hc_dv[Hc] = sDV.n
            dma("sync", hr2_d[Hc * 128:(Hc + 1) * 128, :], r2b, sST, wait=((sAC, hc_sq[Hc]),))
            hc_st[Hc] = sST.n
        q1_mm_done[g] = sPE.n
    pass1_st = sST.n
    pass1_dv = sDV.n

    xm_prev = None
    P.wait("tensor", sDV, pass1_dv)
    for j in range(NT):
        P.op("tensor", "transpose", pq[2][:, j * 128:(j + 1) * 128],
             runmax[:, j * 128:(j + 1) * 128], eyef)
    P.op("tensor", "matmul", pq[3][0:1, 512:513], onescol, onescol,
         start=True, stop=True, inc=((sPE, sPE.bump() and 1),))
    xtr = sPE.n
    for j in range(NT):
        P.wait("vector", sPE, xtr)
        P.op("vector", "reduce_max", xmaxc[:, j:j + 1], pq[2][:, j * 128:(j + 1) * 128], axis=AX,
             inc=((sDV, sDV.bump() and 1),))
        P.wait("vector", sDV, sDV.n)
        P.op("vector", "tensor_scalar", xmaxc[:, j:j + 1], xmaxc[:, j:j + 1], EPS_Q, None, ALU.max,
             inc=((sDV, sDV.bump() and 1),))
        P.wait("vector", sDV, sDV.n)
        P.op("vector", "reciprocal", b1c[:, j:j + 1], xmaxc[:, j:j + 1],
             inc=((sDV, sDV.bump() and 1),))
        P.op("vector", "tensor_scalar", b2c[:, j:j + 1], xmaxc[:, j:j + 1], scl2[:, 3:4],
             1.0 / 255.0, ALU.mult, ALU.mult, inc=((sDV, sDV.bump() and 1),))
        P.wait("vector", sDV, sDV.n)
        P.op("vector", "tensor_scalar", b1c[:, j:j + 1], b1c[:, j:j + 1], 255.0, None, ALU.mult,
             inc=((sDV, sDV.bump() and 1),))
        xm_prev = sDV.n
    P.wait("tensor", sDV, xm_prev)
    P.op("tensor", "transpose", pq[2][0:8, 128:256], b1c, eyef, inc=((sPE, sPE.bump() and 1),))
    P.op("tensor", "transpose", pq[2][0:8, 384:512], b2c, eyef, inc=((sPE, sPE.bump() and 1),))
    btr = sPE.n
    P.wait("vector", sPE, btr)
    P.op("vector", "tensor_copy", r2e0[0:8, 0:128], pq[2][0:8, 128:256])
    P.op("vector", "tensor_copy", r2e1[0:8, 0:128], pq[2][0:8, 384:512],
         inc=((sDV, sDV.bump() and 1),))
    brow_v = sDV.n
    dma("sync", b1_d.rearrange("1 (j p) -> j p", p=128), r2e0[0:8, 0:128], sST,
        wait=((sDV, brow_v),))
    dma("sync", b2_d.rearrange("1 (j p) -> j p", p=128), r2e1[0:8, 0:128], sST)
    bmax_st = sST.n
    dma("sync", b1b, b1_d.broadcast_to([128, T]), sLD, wait=((sST, bmax_st),))
    bb_ld = sLD.n
    if stop == "f2":
        return finish_early()
    if stop == "f2b":
        P.wait("vector", sLD, bb_ld)
        P.op("vector", "tensor_copy", r2e0, b1b, inc=((sDV, sDV.bump() and 1),))
        P.op("vector", "tensor_copy", r2e1[:, 0:8], b2c, inc=((sDV, sDV.bump() and 1),))
        bbv = sDV.n
        P.op("vector", "memset", x2b, 0.0, wait=((sST, sST.n),), inc=((sDV, sDV.bump() and 1),))
        zv = sDV.n
        for j in range(NT):
            dma("sync", y_out[j * 128:(j + 1) * 128, :], x2b, sST, wait=((sDV, zv),))
        dma("sync", y_out[0:128, 0:1024], r2e0, sST, wait=((sDV, bbv), (sST, sST.n)))
        dma("sync", y_out[128:256, 0:1024], r2e1, sST)
        dma("sync", y_out[256:384, 0:4], scl2, sST)
        P.wait("sync", sST, sST.n)
        _emit(nc, P, sems)
        return nc
    if stop == "f2h":
        return finish_early(hr2_d.flatten()[0:T * D])

    P.op("vector", "memset", outsb[:, 0:8192], 0.0, wait=((sDV, sDV.n),))
    P.op("vector", "memset", outsb[:, 8192:16384], 0.0, inc=((sDV, sDV.bump() and 1),))
    out_zero = sDV.n
    p2_mm_done = {}
    p2_acc = {}
    hq_ready = {}
    for kg in range(16):
        r2bufs = r2r if kg % 2 == 0 else r2rb
        q2bufs = q2r if kg % 2 == 0 else q2rb
        hqb = hqr if (kg % 2 == 0 or "H" in noil) else hqrb
        for kk in range(4):
            kc = kg * 4 + kk
            w = [(sDV, hq_ready[kc - 8])] if kc >= 8 else [(sST, pass1_st)]
            dma("sync", r2bufs[kk], hr2_d[kc * 128:(kc + 1) * 128, :], sLD, wait=tuple(w))
            rl = sLD.n
            w2_ = [(sPE, p2_mm_done[kg - 2])] if kg >= 2 else [(sCC, q_ag_done)]
            dma("gpsimd", q2bufs[kk], q2_d[kc * 128:(kc + 1) * 128, :], sWD, wait=tuple(w2_))
            P.wait("vector", sLD, rl)
            if kc == 0:
                P.wait("vector", sLD, bb_ld)
            if "H" in noil:
                if kg >= 1:
                    P.wait("vector", sPE, p2_mm_done[kg - 1])
            elif kg >= 2:
                P.wait("vector", sPE, p2_mm_done[kg - 2])
            P.op("vector", "tensor_mul", r2bufs[kk], r2bufs[kk], b1b,
                 inc=((sDV, sDV.bump() and 1),))
            P.wait("vector", sDV, sDV.n)
            P.op("vector", "tensor_scalar", intq, r2bufs[kk], RHALF, None, ALU.add,
                 inc=((sDV, sDV.bump() and 1),))
            P.wait("vector", sDV, sDV.n)
            P.op("vector", "tensor_copy", r2bufs[kk], intq,
                 inc=((sDV, sDV.bump() and 1),))
            P.wait("vector", sDV, sDV.n)
            P.op("vector", "tensor_copy", hqb[kk], r2bufs[kk], inc=((sDV, sDV.bump() and 1),))
            hq_ready[kc] = sDV.n
        qload_v = sWD.n
        for j in range(NT):
            for eh in range(2):
                pp = pq[(j % 2) * 2 + eh]
                P.wait("tensor", sDV, hq_ready[kg * 4 + 3])
                P.wait("tensor", sWD, qload_v)
                if kg > 0 or j >= 2:
                    key = (kg, j - 2, eh) if j >= 2 else (kg - 1, NT - 2 + j, eh)
                    P.wait("tensor", sDV, p2_acc[key])
                for kk in range(4):
                    for q_ in range(2):
                        P.op("tensor", "matmul", pp[:, q_ * 512:(q_ + 1) * 512],
                             hqb[kk][:, j * 128:(j + 1) * 128],
                             q2bufs[kk][:, eh * 1024 + q_ * 512: eh * 1024 + (q_ + 1) * 512],
                             start=(kk == 0), stop=(kk == 3),
                             inc=((sPE, sPE.bump() and 1),) if (kk == 3 and q_ == 1) else ())
                mm_v = sPE.n
                P.wait("vector", sPE, mm_v)
                if kg == 0 and j == 0 and eh == 0:
                    P.wait("vector", sDV, out_zero)
                P.op("vector", "scalar_tensor_tensor",
                     outsb[:, j * 2048 + eh * 1024: j * 2048 + (eh + 1) * 1024],
                     pp[:], b2c[:, j:j + 1],
                     outsb[:, j * 2048 + eh * 1024: j * 2048 + (eh + 1) * 1024],
                     ALU.mult, ALU.add, inc=((sDV, sDV.bump() and 1),))
                p2_acc[(kg, j, eh)] = sDV.n
        p2_mm_done[kg] = sPE.n
    pass2_dv = sDV.n
    if stop == "f3":
        return finish_early()
    if stop == "f3z":
        for j in range(NT):
            dma("sync", y_out[j * 128:(j + 1) * 128, :], outsb[:, j * 2048:(j + 1) * 2048],
                sST, wait=((sDV, pass2_dv),) if j == 0 else ())
        P.wait("sync", sST, sST.n)
        _emit(nc, P, sems)
        return nc
    if stop == "f3x":
        import os as _os
        _o = int(_os.environ.get("KMK_DOFF", "0"))
        return finish_early(sb_dump=outsb[:, _o:_o + 2048])

    P.wait("vector", sDV, pass2_dv)
    P.op("vector", "memset", st_s1, 0.0)
    P.op("vector", "memset", st_s2, 0.0, inc=((sDV, sDV.bump() and 1),))
    for j in range(NT):
        P.op("vector", "reduce_sum", st_s1[:, j:j + 1], outsb[:, j * 2048:(j + 1) * 2048],
             axis=AX, inc=((sDV, sDV.bump() and 1),))
        ssv = sDV.n
        P.wait("scalar", sDV, ssv)
        P.op("scalar", "activation", f4dump, outsb[:, j * 2048:(j + 1) * 2048], AF.Square,
             accum_out=st_s2[:, j:j + 1], inc=((sAC, sAC.bump() and 1),))
    f4_sq = sAC.n
    P.wait("vector", sAC, f4_sq)
    P.op("vector", "tensor_scalar", st_m, st_s1, 1.0 / D, None, ALU.mult,
         inc=((sDV, sDV.bump() and 1),))
    P.op("vector", "tensor_scalar", st_s2, st_s2, 1.0 / D, None, ALU.mult,
         inc=((sDV, sDV.bump() and 1),))
    P.wait("vector", sDV, sDV.n)
    P.op("vector", "tensor_mul", st_t1, st_m, st_m, inc=((sDV, sDV.bump() and 1),))
    P.wait("vector", sDV, sDV.n)
    P.op("vector", "tensor_sub", st_v, st_s2, st_t1, inc=((sDV, sDV.bump() and 1),))
    vv2 = sDV.n
    P.wait("scalar", sDV, vv2)
    P.op("scalar", "activation", st_t1, st_v, AF.Ln, bias=EPS_LN,
         inc=((sAC, sAC.bump() and 1),))
    P.wait("scalar", sAC, sAC.n)
    P.op("scalar", "activation", st_r, st_t1, AF.Exp, scale=-0.5, inc=((sAC, sAC.bump() and 1),))
    rv2 = sAC.n
    P.wait("vector", sAC, rv2)
    for j in range(NT):
        w = ((sDV, sDV.n),) if j >= 1 else ()
        dma("sync", xbuf, x2_d[j * 128:(j + 1) * 128, :], sLD, wait=w)
        xv = sLD.n
        P.op("vector", "tensor_scalar", tbuf, outsb[:, j * 2048:(j + 1) * 2048],
             st_m[:, j:j + 1], st_r[:, j:j + 1], ALU.subtract, ALU.mult,
             inc=((sDV, sDV.bump() and 1),))
        P.wait("vector", sDV, sDV.n)
        P.wait("vector", sLD, xv)
        if j >= 1:
            P.wait("vector", sST, prev_y_store)
        P.op("vector", "tensor_add", x2b, tbuf, xbuf, inc=((sDV, sDV.bump() and 1),))
        dma("sync", y_out[j * 128:(j + 1) * 128, :], x2b, sST, wait=((sDV, sDV.n),))
        prev_y_store = sST.n
    P.wait("sync", sST, sST.n)
    _emit(nc, P, sems)
    return nc


def _emit(nc, P, sems):
    from contextlib import ExitStack
    with ExitStack() as ctx:
        for s in sems.values():
            s.h = ctx.enter_context(nc.semaphore(s.name))
        block = ctx.enter_context(nc.Block())
        block.sync(lambda e: [f(e) for f in P.q["sync"]])
        block.vector(lambda e: [f(e) for f in P.q["vector"]])
        block.scalar(lambda e: [f(e) for f in P.q["scalar"]])
        block.tensor(lambda e: [f(e) for f in P.q["tensor"]])
        block.gpsimd(lambda e: [f(e) for f in P.q["gpsimd"]])


_NC_CACHE = None


def kernel(L0_attn_x, L0_attn_Wq, L0_attn_bq, L0_attn_Wk, L0_attn_bk,
           L0_attn_Wv, L0_attn_bv, L0_attn_Wo, L0_attn_bo,
           L0_ffn_c_fc, L0_ffn_c_proj):
    global _NC_CACHE
    x = np.ascontiguousarray(np.asarray(L0_attn_x, dtype=np.float32).reshape(8192, 2048))
    Wq_ = np.ascontiguousarray(np.asarray(L0_attn_Wq, dtype=np.float32))
    Wk_ = np.ascontiguousarray(np.asarray(L0_attn_Wk, dtype=np.float32))
    Wv_ = np.ascontiguousarray(np.asarray(L0_attn_Wv, dtype=np.float32))
    Wo_ = np.ascontiguousarray(np.asarray(L0_attn_Wo, dtype=np.float32))
    bq = np.asarray(L0_attn_bq, dtype=np.float32)
    bk = np.asarray(L0_attn_bk, dtype=np.float32)
    bv = np.asarray(L0_attn_bv, dtype=np.float32)
    bo = np.asarray(L0_attn_bo, dtype=np.float32)
    c_fc = np.ascontiguousarray(np.asarray(L0_ffn_c_fc, dtype=np.float32))
    c_projT = np.ascontiguousarray(np.asarray(L0_ffn_c_proj, dtype=np.float32).T)

    bqk = np.ascontiguousarray(
        np.concatenate([bq.reshape(16, 128).T, bk.reshape(16, 128).T], axis=1))
    bv_b = np.ascontiguousarray(np.broadcast_to(bv[None, :], (128, 2048)))
    bo_b = np.ascontiguousarray(np.broadcast_to(bo[None, :], (128, 2048)))
    maskT = np.triu(np.ones((128, 128), dtype=np.float32)).astype(ml_dtypes.bfloat16)
    eye = np.eye(128, dtype=ml_dtypes.bfloat16)
    eyef32 = np.eye(128, dtype=np.float32)
    onescol = np.ones((128, 1), dtype=ml_dtypes.bfloat16)
    cstf_host = np.ascontiguousarray(
        np.broadcast_to(np.array([1e-5, -0.5], dtype=np.float32)[None, :], (128, 2)))

    if _NC_CACHE is None:
        import os
        _NC_CACHE = build_program(stop=os.environ.get("KMK_STOP", "full"),
                                  noil=os.environ.get("KMK_NOIL", ""),
                                  b2er=int(os.environ.get("KMK_B2ER", "4")),
                                  b2h=int(os.environ.get("KMK_B2H", "16")))
    nc = _NC_CACHE

    in_maps = []
    for c in range(NCORES):
        pm = np.zeros((128, 8), dtype=np.float32)
        pm[:, :c] = 1.0
        in_maps.append({
            "x": np.ascontiguousarray(x[c * T:(c + 1) * T]),
            "Wq": Wq_, "Wk": Wk_, "Wv": Wv_, "Wo": Wo_,
            "bqk": bqk, "bv_b": bv_b, "bo_b": bo_b,
            "w1s": np.ascontiguousarray(c_fc[c * 256:(c + 1) * 256]),
            "w2s": np.ascontiguousarray(c_projT[c * 1024:(c + 1) * 1024]),
            "maskT": maskT, "eye": eye, "eyef32": eyef32, "onescol": onescol,
            "pmask": pm, "cstf": cstf_host,
        })
    res = run_bass_kernel_spmd(nc, in_maps, core_ids=list(range(NCORES)))
    y = np.concatenate([res.results[c]["y"] for c in range(NCORES)], axis=0)
    return y.reshape(1, 8192, 2048)


# version 11 (final)
# speedup vs baseline: 1.0461x; 1.0461x over previous
import numpy as np
import ml_dtypes
import concourse.bass as bass
import concourse.mybir as mybir
from concourse.bass_utils import run_bass_kernel_spmd

F32 = mybir.dt.float32
BF16 = mybir.dt.bfloat16
I32 = mybir.dt.int32
ALU = mybir.AluOpType
AF = mybir.ActivationFunctionType
AX = mybir.AxisListType.X

NCORES = 8
T = 1024
NT = 8
D = 2048
NH = 16
DFF = 8192
NKC = 16
EPS_LN = 1e-5
EPS_AT = 1e-6
EPS_Q = 1e-5
RHALF = 0.49999994

ENGS = ("sync", "vector", "scalar", "tensor", "gpsimd")


class Sem:
    def __init__(self, name):
        self.name = name
        self.n = 0
        self.h = None

    def bump(self, k=1):
        self.n += k
        return self.n


class Prog:
    def __init__(self):
        self.q = {k: [] for k in ENGS}

    def op(self, eng, meth, *args, wait=(), inc=(), **kw):
        def f(e, meth=meth, args=args, kw=kw, wait=tuple(wait), inc=tuple(inc)):
            for s, v in wait:
                e.wait_ge(s.h, v)
            bi = getattr(e, meth)(*args, **kw)
            for s, k in inc:
                bi.then_inc(s.h, k)
        self.q[eng].append(f)

    def wait(self, eng, sem, val):
        def f(e, sem=sem, val=val):
            e.wait_ge(sem.h, val)
        self.q[eng].append(f)


def build_program(stop="full", b2er=4, b2h=16, noil=""):
    nc = bass.Bass("TRN2", target_bir_lowering=False, detect_race_conditions=False)

    def din(name, shape, dt=F32):
        return nc.dram_tensor(name, shape, dt, kind="ExternalInput").ap()

    x_in = din("x", [T, D])
    Wq = din("Wq", [D, D]); Wk = din("Wk", [D, D]); Wv = din("Wv", [D, D]); Wo = din("Wo", [D, D])
    bqk_in = din("bqk", [128, 32])
    bv_in = din("bv_b", [128, D])
    bo_in = din("bo_b", [128, D])
    w1s_in = din("w1s", [256, DFF])
    w2s_in = din("w2s", [T, D])
    maskT_in = din("maskT", [128, 128], BF16)
    eye_in = din("eye", [128, 128], BF16)
    eyef_in = din("eyef32", [128, 128], F32)
    ones_in = din("onescol", [128, 1], BF16)
    pmask_in = din("pmask", [128, 8])
    cstf_in = din("cstf", [128, 2])

    y_out = nc.dram_tensor("y", [T, D], F32, kind="ExternalOutput").ap()

    def dint(name, shape, dt=F32):
        return nc.dram_tensor(name, shape, dt, kind="Internal").ap()

    def dshr(name, shape, dt=F32):
        return nc.dram_tensor(name, shape, dt, kind="Internal",
                              addr_space="Shared").ap()

    prefixd = dint("prefixd", [NT * 128, NH * 129])
    sc_in = dint("sc_in", [128, NH * 129])
    sc_out = dshr("sc_out", [NCORES * 128, NH * 129])
    p1c_d = dint("p1c_d", [128, 1]); p2c_d = dint("p2c_d", [128, 1])
    scl_in = dint("scl_in", [2, 128]); scl_out = dint("scl_out", [2, 128])
    q1s_d = dint("q1s_d", [256, DFF], BF16)
    q1_d = dshr("q1_d", [D, DFF], BF16)
    q2s_d = dint("q2s_d", [T, D], BF16)
    q2_d = dshr("q2_d", [DFF, D], BF16)
    x2_d = dint("x2_d", [T, D])
    wo_bf = dint("wo_bf", [D, D], BF16)
    xint_d = dint("xint_d", [T, D])
    hr2_d = dint("hr2_d", [DFF, T])
    b1_d = dint("b1_d", [1, T]); b2_d = dint("b2_d", [1, T])

    SBB = (nc.sbuf_base + 31) // 32 * 32

    def sb(name, off, shape, dt):
        return nc.alloc_sbuf_tensor_at(name, shape, dt, offset=SBB + off).ap()

    xnfm = sb("xnfm", 0, [128, 16384], BF16)
    wob0 = sb("wob0", 16384, [128, 512], BF16)
    wob1 = sb("wob1", 17440, [128, 512], BF16)
    Pc = sb("Pc", 0, [128, NH * 129], F32)
    tbuf = sb("tbuf", 0, [128, 2048], F32)
    x2b = sb("x2b", 8192, [128, 2048], F32)
    xn2fm = sb("xn2fm", 0, [128, 16384], BF16)

    qf = sb("qf", 32768, [128, 16384], BF16)
    xb0 = sb("xb0", 32768, [128, 2048], F32)
    xb1 = sb("xb1", 40960, [128, 2048], F32)
    xnstg = sb("xnstg", 49152, [128, 2048], BF16)
    xnstg1 = sb("xnstg1", 53248, [128, 2048], BF16)
    sqdump = sb("sqdump", 57344, [128, 2048], BF16)
    wsl0 = sb("wsl0", 110592, [128, 1024], F32)
    wsl1 = sb("wsl1", 114688, [128, 1024], F32)
    qav = [sb("qa0", 118784, [128, 512], F32), sb("qa1", 120832, [128, 512], F32)]
    qbv = [sb("qb0", 122880, [128, 512], F32), sb("qb1", 124928, [128, 512], F32)]
    swsl0 = sb("swsl0", 196608, [128, 512], F32)
    swsl1 = sb("swsl1", 198656, [128, 512], F32)
    wbs0 = sb("wbs0", 110592, [128, 2048], BF16)
    wbs1 = sb("wbs1", 114688, [128, 2048], BF16)
    yo = sb("yo", 32768, [128, 16384], F32)
    outsb = sb("outsb", 32768, [128, 16384], F32)

    kffm = sb("kffm", 65536, [128, 16384], BF16)
    qoutv = [sb("qout0", 126976, [128, 512], BF16), sb("qout1", 128000, [128, 512], BF16)]
    xn2stg = sb("xn2stg", 69632, [128, 2048], BF16)
    xn2stg1 = sb("xn2stg1", 73728, [128, 2048], BF16)
    sqd2 = sb("sqd2", 77824, [128, 2048], BF16)
    qstr0 = sb("qstr0", 32768, [128, 16384], BF16)
    qstr1 = sb("qstr1", 65536, [128, 16384], BF16)

    As = sb("As", 98304, [128, 16384], BF16)
    elu_m = sb("elu_m", 98304, [128, 1024], F32)
    elu_e = sb("elu_e", 102400, [128, 1024], F32)
    elu_p = sb("elu_p", 106496, [128, 1024], F32)
    r2e0 = sb("r2e0", 98304, [128, 1024], F32)
    r2e1 = sb("r2e1", 102400, [128, 1024], F32)
    r2tmp = sb("r2tmp", 106496, [128, 1024], F32)
    f4dump = sb("f4dump", 98304, [128, 2048], BF16)
    runmax = sb("runmax", 110592, [128, 1024], F32)
    xs_hi = sb("xs_hi", 32768, [128, 1024], BF16)
    xs_m = sb("xs_m", 34816, [128, 1024], BF16)
    xs_l = sb("xs_l", 36864, [128, 1024], BF16)
    xs_rf = sb("xs_rf", 40960, [128, 1024], F32)
    xs_lf = sb("xs_lf", 45056, [128, 1024], F32)
    xs_hit = sb("xs_hit", 49152, [128, 1024], BF16)
    xs_mt = sb("xs_mt", 51200, [128, 1024], BF16)
    xs_lt = sb("xs_lt", 53248, [128, 1024], BF16)
    xT = sb("xT", 57344, [128, 1024], F32)
    padb = sb("padb", 61440, [128, 128], F32)
    r2r = [sb(f"r2r{i}", 114688 + 4096 * i, [128, 1024], F32) for i in range(4)]

    vtm = sb("vtm", 131072, [128, 16384], BF16)
    b2dump = sb("b2dump", 131072, [128, 512], BF16)
    r2rb = [sb(f"r2rb{i}", 131072 + 4096 * i, [128, 1024], F32) for i in range(4)]
    q2r = [sb(f"q2r{i}", 147456 + 4096 * i, [128, 2048], BF16) for i in range(4)]

    stripes = sb("stripes", 163840, [128, 16384], BF16)
    state = sb("state", 163840, [128, NH * 129], F32)
    kftm0 = sb("kftm0", 172096, [128, 2048], BF16)
    kftm1 = sb("kftm1", 176192, [128, 2048], BF16)
    prefb = sb("prefb", 163840, [128, NH * 129], F32)
    prefb2 = sb("prefb2", 180480, [128, NH * 129], F32)
    statebf2 = sb("statebf2", 188736, [128, NH * 129], BF16)
    statebf = sb("statebf", 172096, [128, NH * 129], BF16)
    o_tm = sb("o_tm", 176256, [128, 2048], BF16)
    zcol = sb("zcol", 180352, [128, 16], F32)
    rcol = sb("rcol", 180416, [128, 16], F32)
    q2rb = [sb(f"q2rb{i}", 163840 + 4096 * i, [128, 2048], BF16) for i in range(4)]
    hqr = [sb(f"hqr{i}", 180480 + 2048 * i, [128, 1024], BF16) for i in range(4)]
    hqrb = [sb(f"hqrb{i}", 98304 + 2048 * i, [128, 1024], BF16) for i in range(4)]
    intq = sb("intq", 188672, [128, 1024], I32)
    bvbo = sb("bvbo", 200960, [128, 2048], BF16)
    b1b = sb("b1b", 192768, [128, 1024], F32)
    b2b = sb("b2b", 196864, [128, 1024], F32)
    xbuf = sb("xbuf", 200960, [128, 2048], F32)
    cst = 209152

    def cst_sb(name, shape, dt):
        nonlocal cst
        t = sb(name, cst, shape, dt)
        nbytes = shape[1] * mybir.dt.size(dt)
        cst += (nbytes + 31) // 32 * 32
        return t

    bqk = cst_sb("bqk", [128, 32], F32)
    maskT = cst_sb("maskT", [128, 128], BF16)
    eye = cst_sb("eye", [128, 128], BF16)
    eyef = cst_sb("eyef", [128, 128], F32)
    onescol = cst_sb("onescol_s", [128, 1], BF16)
    pmask = cst_sb("pmask_s", [128, 8], F32)
    st_s1 = cst_sb("st_s1", [128, 8], F32)
    st_m = cst_sb("st_m", [128, 8], F32)
    st_s2 = cst_sb("st_s2", [128, 8], F32)
    st_v = cst_sb("st_v", [128, 8], F32)
    st_r = cst_sb("st_r", [128, 8], F32)
    st_t1 = cst_sb("st_t1", [128, 8], F32)
    scl2 = cst_sb("scl2", [128, 4], F32)
    pr12 = cst_sb("pr12", [128, 2], F32)
    prrow_t = cst_sb("prrow", [1, 128], F32)
    prsc = cst_sb("prsc", [1, 2], F32)
    xmaxc = cst_sb("xmaxc", [128, 8], F32)
    b1c = cst_sb("b1c", [128, 8], F32)
    b2c = cst_sb("b2c", [128, 8], F32)
    cstf = cst_sb("cstf", [128, 2], F32)
    dumc = cst_sb("dumc", [128, 1], F32)
    nc.const_aps.register(F32, EPS_LN, cstf[:, 0:1])
    nc.const_aps.register(F32, -0.5, cstf[:, 1:2])
    assert cst <= 211328, cst
    nc.bump_sbuf(211328)

    pq = [nc.alloc_psum_tensor(f"pq{i}", [128, 1024], F32).ap() for i in range(4)]
    pqb = [p.bitcast(BF16) for p in pq]

    sems = {n: Sem(n) for n in ("sLD", "sWD", "sST", "sPE", "sDV", "sAC", "sCC", "sTT")}
    sLD, sWD, sST, sPE, sDV, sAC, sCC, sTT = (sems[n] for n in
        ("sLD", "sWD", "sST", "sPE", "sDV", "sAC", "sCC", "sTT"))

    P = Prog()

    def finish_early(copy_from=None, sb_dump=None):
        P.op("vector", "memset", x2b, 0.0, wait=((sST, sST.n),), inc=((sDV, sDV.bump() and 1),))
        zv = sDV.n
        if copy_from is not None:
            dma("sync", y_out.flatten(), copy_from, sST,
                wait=((sDV, zv), (sST, sST.n), (sCC, sCC.n)))
        elif sb_dump is not None:
            for j in range(NT):
                dma("sync", y_out[j * 128:(j + 1) * 128, :], x2b, sST, wait=((sDV, zv),))
            dma("gpsimd", y_out[0:128, 0:sb_dump.shape[1]], sb_dump, sST,
                wait=((sST, sST.n), (sAC, sAC.n), (sDV, sDV.n), (sPE, sPE.n), (sCC, sCC.n)))
        else:
            for j in range(NT):
                dma("sync", y_out[j * 128:(j + 1) * 128, :], x2b, sST, wait=((sDV, zv),))
        P.wait("sync", sST, sST.n)
        _emit(nc, P, sems)
        return nc

    def dma(eng, out, in_, sem, wait=()):
        P.op(eng, "dma_start", wait=wait, inc=((sem, sem.bump(16) and 16),), out=out, in_=in_)

    for dst, src in ((bqk, bqk_in), (maskT, maskT_in), (eye, eye_in), (eyef, eyef_in),
                     (onescol, ones_in), (pmask, pmask_in), (cstf, cstf_in)):
        dma("sync", dst, src, sLD)
    s0_done = sLD.n
    P.wait("tensor", sLD, s0_done)
    P.wait("vector", sLD, s0_done)
    P.wait("scalar", sLD, s0_done)

    def ln_sweep(src_dram, dst_fm, stg_pair, dump, bank_lo, extra_first_wait=(),
                 bounce_dst=None, bounce_st_out=None):
        tr_done_prev = None
        prev_evac = None
        for j in range(NT):
            xb = xb0 if j % 2 == 0 else xb1
            stg = stg_pair[j % 2]
            w = list(extra_first_wait) if j < 2 else [(sAC, xn_ready[j - 2])]
            if bounce_dst is not None and j >= 2:
                w.append((sST, bounce_st_out[j - 2]))
            dma("sync", xb, src_dram[j * 128:(j + 1) * 128, :], sLD, wait=tuple(w))
            ldv = sLD.n
            if bounce_dst is not None:
                dma("sync", bounce_dst[j * 128:(j + 1) * 128, :], xb, sST,
                    wait=((sLD, ldv),))
                bounce_st_out[j] = sST.n
            P.wait("vector", sLD, ldv)
            P.op("vector", "reduce_sum", st_s1[:, j:j + 1], xb, axis=AX, inc=((sDV, sDV.bump() and 1),))
            P.wait("vector", sDV, sDV.n)
            P.wait("scalar", sLD, ldv)
            P.op("scalar", "activation", dump, xb, AF.Square, accum_out=st_s2[:, j:j + 1],
                 inc=((sAC, sAC.bump() and 1),))
            sq_done = sAC.n
            P.op("vector", "tensor_scalar", st_m[:, j:j + 1], st_s1[:, j:j + 1], 1.0 / D, None, ALU.mult,
                 inc=((sDV, sDV.bump() and 1),))
            P.wait("vector", sAC, sq_done)
            P.op("vector", "tensor_scalar", st_s2[:, j:j + 1], st_s2[:, j:j + 1], 1.0 / D, None, ALU.mult,
                 inc=((sDV, sDV.bump() and 1),))
            P.wait("vector", sDV, sDV.n)
            P.op("vector", "tensor_mul", st_t1[:, j:j + 1], st_m[:, j:j + 1], st_m[:, j:j + 1],
                 inc=((sDV, sDV.bump() and 1),))
            P.wait("vector", sDV, sDV.n)
            P.op("vector", "tensor_sub", st_v[:, j:j + 1], st_s2[:, j:j + 1], st_t1[:, j:j + 1],
                 inc=((sDV, sDV.bump() and 1),))
            var_done = sDV.n
            P.wait("scalar", sDV, var_done)
            P.op("scalar", "activation", st_t1[:, j:j + 1], st_v[:, j:j + 1], AF.Ln, bias=EPS_LN,
                 inc=((sAC, sAC.bump() and 1),))
            P.wait("scalar", sAC, sAC.n)
            P.op("scalar", "activation", st_r[:, j:j + 1], st_t1[:, j:j + 1], AF.Exp, scale=-0.5,
                 inc=((sAC, sAC.bump() and 1),))
            rstd_done = sAC.n
            P.wait("vector", sAC, rstd_done)
            P.op("vector", "tensor_scalar", st_t1[:, j:j + 1], st_m[:, j:j + 1],
                 st_r[:, j:j + 1], -1.0, ALU.mult, ALU.mult, inc=((sDV, sDV.bump() and 1),))
            bc_v = sDV.n
            if tr_done_prev is not None:
                P.wait("scalar", sTT, tr_done_prev)
            P.wait("scalar", sDV, bc_v)
            P.op("scalar", "activation", stg, xb, AF.Identity, bias=st_t1[:, j:j + 1],
                 scale=st_r[:, j:j + 1], inc=((sAC, sAC.bump() and 1),))
            if j == 0:
                xn_ready = {}
            xn_ready[j] = sAC.n
            for k in range(NKC):
                P.op("sync", "dma_start",
                     wait=((sAC, xn_ready[j]),) if k == 0 else (),
                     inc=((sTT, sTT.bump(16) and 16),),
                     out=dst_fm[:, k * 1024 + j * 128: k * 1024 + (j + 1) * 128],
                     in_=stg[:, k * 128:(k + 1) * 128], transpose=True)
            tr_done_prev = sTT.n
        return sTT.n

    xb_bounce = {}
    s1_done_ac = ln_sweep(x_in, xnfm, (xnstg, xnstg1), sqdump, 0,
                          bounce_dst=xint_d, bounce_st_out=xb_bounce)
    if stop == "s1":
        return finish_early()
    if stop == "s1x":
        return finish_early(sb_dump=xnfm[:, 0:2048])

    def proj_sweep_fm(Wsrc, out_sbuf, bias_off, first_wait=(), slot_hook=None):
        stripe_done = {}
        bank_free = {}
        for qt in range(4):
            stripe = stripes[:, (qt % 2) * 8192:(qt % 2) * 8192 + 8192]
            w = ((sPE, stripe_done[qt - 2]),) if qt >= 2 else tuple(first_wait)
            for k in range(NKC):
                dma("gpsimd", stripe[:, k * 512:(k + 1) * 512],
                    Wsrc[k * 128:(k + 1) * 128, qt * 512:(qt + 1) * 512], sWD,
                    wait=w if k == 0 else ())
            wv = sWD.n
            P.wait("tensor", sWD, wv)
            if qt == 0:
                P.wait("tensor", sTT, s1_done_ac)
            for hh in range(4):
                h = qt * 4 + hh
                pp = pq[hh]
                if hh in bank_free:
                    P.wait("tensor", sDV, bank_free[hh])
                for k in range(NKC):
                    lhsT = stripe[:, k * 512 + hh * 128: k * 512 + (hh + 1) * 128]
                    for half in range(2):
                        P.op("tensor", "matmul", pp[:, half * 512:(half + 1) * 512],
                             lhsT, xnfm[:, k * 1024 + half * 512: k * 1024 + (half + 1) * 512],
                             start=(k == 0), stop=(k == NKC - 1),
                             inc=((sPE, sPE.bump() and 1),) if (k == NKC - 1 and half == 1) else ())
                    if slot_hook is not None and k % 4 == 0:
                        slot_hook()
                mm_v = sPE.n
                bcol = bqk[:, bias_off + h: bias_off + h + 1]
                P.wait("vector", sPE, mm_v)
                P.op("vector", "tensor_scalar", elu_m, pp[:], bcol, 0.0, ALU.add, ALU.min,
                     inc=((sDV, sDV.bump() and 1),))
                mv = sDV.n
                P.wait("scalar", sDV, mv)
                P.op("scalar", "activation", elu_e, elu_m, AF.Exp, inc=((sAC, sAC.bump() and 1),))
                ev = sAC.n
                P.op("vector", "tensor_scalar", elu_p, pp[:], bcol, 0.0, ALU.add, ALU.max,
                     inc=((sDV, sDV.bump() and 1),))
                P.wait("vector", sDV, sDV.n)
                P.wait("vector", sAC, ev)
                P.op("vector", "tensor_add", out_sbuf[:, h * 1024:(h + 1) * 1024], elu_p, elu_e,
                     inc=((sDV, sDV.bump() and 1),))
                bank_free[hh] = sDV.n
            stripe_done[qt] = sPE.n
        return sDV.n

    if "f" not in noil:
        P.op("vector", "memset", pr12, 0.0, wait=((sLD, s0_done),),
             inc=((sDV, sDV.bump() and 1),))
    sc_pieces = []
    for (srcw, rows, colw, colidx) in ((w1s_in, 2, DFF, 0), (w2s_in, 8, D, 1)):
        srcv = srcw.rearrange("(n p) e -> n p e", p=128)
        for rr in range(rows):
            for cc in range(colw // 512):
                sc_pieces.append((srcv[rr, :, cc * 512:(cc + 1) * 512], colidx))
    sc_red_done = {}
    sc_state = {"i": 0}

    def scale_piece_hook():
        pi = sc_state["i"]
        if pi >= len(sc_pieces):
            return
        sc_state["i"] = pi + 1
        view, colidx = sc_pieces[pi]
        wb = swsl0 if pi % 2 == 0 else swsl1
        w = ((sDV, sc_red_done[pi - 2]),) if pi >= 2 else ()
        dma("sync", wb, view, sLD, wait=w)
        lv = sLD.n
        P.wait("vector", sLD, lv)
        P.op("vector", "reduce_sum", b1c[:, 0:1], wb, axis=AX, apply_absolute_value=True,
             inc=((sDV, sDV.bump() and 1),))
        P.wait("vector", sDV, sDV.n)
        P.op("vector", "tensor_add", pr12[:, colidx:colidx + 1], pr12[:, colidx:colidx + 1],
             b1c[:, 0:1], inc=((sDV, sDV.bump() and 1),))
        sc_red_done[pi] = sDV.n
        P.wait("vector", sDV, sDV.n)

    k_done_dv = proj_sweep_fm(
        Wk, kffm, 16,
        slot_hook=None if ("s" in noil or "f" in noil) else scale_piece_hook)
    if "f" not in noil:
        while sc_state["i"] < len(sc_pieces):
            scale_piece_hook()
    if stop == "s2":
        return finish_early()
    if stop == "s2x":
        return finish_early(sb_dump=kffm[:, 0:2048])

    if "f" in noil:
        prsc_copy_v = 0
        scl_store_v = sST.n
    else:
        padf = elu_m[:, 0:128]
        hi_b = elu_e.bitcast(BF16)[:, 0:128]
        m_b = elu_e.bitcast(BF16)[:, 128:256]
        l_b = elu_e.bitcast(BF16)[:, 256:384]
        hi_t = elu_e.bitcast(BF16)[:, 512:640]
        m_t = elu_e.bitcast(BF16)[:, 640:768]
        l_t = elu_e.bitcast(BF16)[:, 768:896]
        r_f = elu_m[:, 128:256]
        l_f = elu_m[:, 256:384]
        rec = elu_p[:, 0:128]
        P.op("vector", "memset", padf, 0.0, wait=((sDV, k_done_dv),),
             inc=((sDV, sDV.bump() and 1),))
        P.wait("vector", sDV, sDV.n)
        P.op("vector", "tensor_copy", padf[:, 0:2], pr12, inc=((sDV, sDV.bump() and 1),))
        P.wait("vector", sDV, sDV.n)
        P.op("vector", "tensor_copy", hi_b, padf, inc=((sDV, sDV.bump() and 1),))
        P.wait("vector", sDV, sDV.n)
        P.op("vector", "tensor_sub", r_f, padf, hi_b, inc=((sDV, sDV.bump() and 1),))
        P.wait("vector", sDV, sDV.n)
        P.op("vector", "tensor_copy", m_b, r_f, inc=((sDV, sDV.bump() and 1),))
        P.wait("vector", sDV, sDV.n)
        P.op("vector", "tensor_sub", l_f, r_f, m_b, inc=((sDV, sDV.bump() and 1),))
        P.wait("vector", sDV, sDV.n)
        P.op("vector", "tensor_copy", l_b, l_f, inc=((sDV, sDV.bump() and 1),))
        pr_split = sDV.n
        for (sb_, db_) in ((hi_b, hi_t), (m_b, m_t), (l_b, l_t)):
            P.op("sync", "dma_start", wait=((sDV, pr_split),),
                 inc=((sTT, sTT.bump(16) and 16),), out=db_, in_=sb_, transpose=True)
        pr_tt = sTT.n
        P.wait("vector", sTT, pr_tt)
        P.op("vector", "tensor_add", rec, hi_t, m_t, inc=((sDV, sDV.bump() and 1),))
        P.wait("vector", sDV, sDV.n)
        P.op("vector", "tensor_add", rec, rec, l_t, inc=((sDV, sDV.bump() and 1),))
        P.wait("vector", sDV, sDV.n)
        P.op("vector", "reduce_sum", elu_m[:, 384:385], rec, axis=AX,
             inc=((sDV, sDV.bump() and 1),))
        P.wait("scalar", sDV, sDV.n)
        P.op("vector", "memset", elu_m[:, 512:640], 1.0, inc=((sDV, sDV.bump() and 1),))
        P.wait("scalar", sDV, sDV.n)
        P.op("scalar", "activation", elu_p[:, 128:256], elu_m[:, 512:640], AF.Identity,
             scale=elu_m[:, 384:385], inc=((sAC, sAC.bump() and 1),))
        prsc_copy_v = sAC.n
        dma("sync", scl_in, elu_p[0:2, 128:256], sST, wait=((sAC, prsc_copy_v),))
        scl_store_v = sST.n

    dma("gpsimd", bvbo, bv_in, sWD, wait=((sDV, k_done_dv),))
    bv_ready = sWD.n
    wo_bounce_st = {}
    v_stripe_done = {}
    v_tile_done = {}
    for er in range(4):
        stripe = stripes[:, (er % 2) * 8192:(er % 2) * 8192 + 8192]
        w = ((sPE, v_stripe_done[er - 2]),) if er >= 2 else ((sDV, k_done_dv),)
        for k in range(NKC):
            dma("gpsimd", stripe[:, k * 512:(k + 1) * 512],
                Wv[k * 128:(k + 1) * 128, er * 512:(er + 1) * 512], sWD,
                wait=w if k == 0 else ())
        wv = sWD.n
        for c in range(4):
            hbc = er * 4 + c
            wbb = wbs0 if hbc % 2 == 0 else wbs1
            bw = ((sST, wo_bounce_st[hbc - 2]),) if hbc >= 2 else ()
            dma("gpsimd", wbb, Wo[hbc * 128:(hbc + 1) * 128, :], sWD, wait=bw)
            blv = sWD.n
            dma("sync", wo_bf[hbc * 128:(hbc + 1) * 128, :], wbb, sST, wait=((sWD, blv),))
            wo_bounce_st[hbc] = sST.n
        P.wait("tensor", sWD, wv)
        if er == 0:
            P.wait("tensor", sDV, k_done_dv)
        for j in range(NT):
            pp = pq[j // 2][:, (j % 2) * 512:(j % 2) * 512 + 512]
            if er >= 1:
                P.wait("tensor", sDV, v_tile_done[(er - 1, j)])
            for k in range(NKC):
                P.op("tensor", "matmul", pp,
                     xnfm[:, k * 1024 + j * 128: k * 1024 + (j + 1) * 128],
                     stripe[:, k * 512:(k + 1) * 512],
                     start=(k == 0), stop=(k == NKC - 1),
                     inc=((sPE, sPE.bump() and 1),) if k == NKC - 1 else ())
            mm_v = sPE.n
            P.wait("vector", sPE, mm_v)
            if er == 0 and j == 0:
                P.wait("vector", sWD, bv_ready)
            P.op("vector", "tensor_add",
                 vtm[:, j * 2048 + er * 512: j * 2048 + (er + 1) * 512],
                 pp, bvbo[:, er * 512:(er + 1) * 512], inc=((sDV, sDV.bump() and 1),))
            v_tile_done[(er, j)] = sDV.n
        v_stripe_done[er] = sPE.n
    v_done_dv = sDV.n
    if stop == "s4":
        return finish_early()
    if stop == "s4x":
        import os as _os
        _o = int(_os.environ.get("KMK_DOFF", "0"))
        return finish_early(sb_dump=vtm[:, _o:_o + 2048])

    if "f" in noil:
        scales_done = 0
    else:
        P.op("gpsimd", "collective_compute", "AllReduce", ALU.add,
             [list(range(NCORES))], [scl_in.opt()], [scl_out.opt()],
             wait=((sST, scl_store_v),), inc=((sCC, sCC.bump() and 1),))
        scale_ar = sCC.n
        dma("sync", scl2[:, 0:1], scl_out[0:1, 0:1].broadcast_to([128, 1]), sLD,
            wait=((sCC, scale_ar),))
        dma("sync", scl2[:, 1:2], scl_out[1:2, 0:1].broadcast_to([128, 1]), sLD)
        scl_bcast_v = sLD.n
        P.wait("vector", sLD, scl_bcast_v)
        P.op("vector", "tensor_scalar", scl2[:, 2:3], scl2[:, 0:1], 1.0 / (D * DFF), EPS_Q,
             ALU.mult, ALU.max, inc=((sDV, sDV.bump() and 1),))
        P.op("vector", "tensor_scalar", scl2[:, 3:4], scl2[:, 1:2], 1.0 / (D * DFF), EPS_Q,
             ALU.mult, ALU.max, inc=((sDV, sDV.bump() and 1),))
        P.wait("vector", sDV, sDV.n)
        P.op("vector", "reciprocal", scl2[:, 0:1], scl2[:, 2:3],
             inc=((sDV, sDV.bump() and 1),))
        P.op("vector", "reciprocal", scl2[:, 1:2], scl2[:, 3:4],
             inc=((sDV, sDV.bump() and 1),))
        P.wait("vector", sDV, sDV.n)
        scales_done = sDV.n

    P.wait("tensor", sDV, v_done_dv)
    P.op("vector", "memset", state, 0.0, wait=((sDV, v_done_dv),), inc=((sDV, sDV.bump() and 1),))
    state_zero = sDV.n
    prev_state_add = state_zero
    prefix_written = []
    kv_evac = {}
    kv_mm_done = {}
    for j in range(NT):
        ktb = kftm0 if j % 2 == 0 else kftm1
        for h in range(NH):
            w = []
            if h == 0:
                w.append((sDV, v_done_dv))
                if j >= 2:
                    w.append((sPE, kv_mm_done[j - 2]))
            P.op("sync", "dma_start", wait=tuple(w),
                 inc=((sTT, sTT.bump(16) and 16),),
                 out=ktb[:, h * 128:(h + 1) * 128],
                 in_=kffm[:, h * 1024 + j * 128: h * 1024 + (j + 1) * 128], transpose=True)
        kt_done = sTT.n
        dma("sync", prefixd[j * 128:(j + 1) * 128, :], state, sST,
            wait=((sDV, prev_state_add),))
        prefix_written.append(sST.n)
        P.wait("tensor", sTT, kt_done)
        for h in range(NH):
            i = j * NH + h
            pp = pq[h % 2]
            if i >= 2:
                P.wait("tensor", sDV, kv_evac[i - 2])
            P.op("tensor", "matmul", pp[:, 0:128], ktb[:, h * 128:(h + 1) * 128],
                 vtm[:, j * 2048 + h * 128: j * 2048 + (h + 1) * 128], start=True, stop=True)
            P.op("tensor", "matmul", pp[:, 128:129], ktb[:, h * 128:(h + 1) * 128],
                 onescol, start=True, stop=True, inc=((sPE, sPE.bump() and 1),))
            mm_v = sPE.n
            P.wait("vector", sPE, mm_v)
            if h == 0:
                P.wait("vector", sST, prefix_written[j])
            P.op("vector", "tensor_add", state[:, h * 129:(h + 1) * 129],
                 state[:, h * 129:(h + 1) * 129], pp[:, 0:129], inc=((sDV, sDV.bump() and 1),))
            kv_evac[i] = sDV.n
        prev_state_add = sDV.n
        kv_mm_done[j] = sPE.n

    dma("gpsimd", bvbo, bo_in, sWD, wait=((sDV, v_done_dv),))
    bo_ready = sWD.n
    dma("sync", sc_in, state, sST, wait=((sDV, prev_state_add),))
    scin_done = sST.n
    P.op("gpsimd", "collective_compute", "AllGather", ALU.bypass,
         [list(range(NCORES))], [sc_in.opt()], [sc_out.opt()],
         wait=((sST, scin_done),), inc=((sCC, sCC.bump() and 1),))
    state_ag_done = sCC.n
    if stop == "s5":
        return finish_early()

    q_pieces = []
    for (srcw, dstw, rows, colw, invc) in ((w1s_in, q1s_d, 2, DFF, 0),
                                           (w2s_in, q2s_d, 8, D, 1)):
        srcv = srcw.rearrange("(n p) e -> n p e", p=128)
        dstv = dstw.rearrange("(n p) e -> n p e", p=128)
        for rr in range(rows):
            for cc in range(colw // 1024):
                q_pieces.append((srcv[rr, :, cc * 1024:(cc + 1) * 1024],
                                 dstv[rr, :, cc * 1024:(cc + 1) * 1024], invc))
    qz_rd_done = {}
    qz_store = {}
    qz_state = {"i": 0}
    qz_ag = []
    qz_pend = []

    def qz_flush_one():
        if not qz_pend:
            return
        ht, dvh, st_set, qb_inc = qz_pend.pop(0)
        if ht >= 2:
            P.wait("vector", sST, qz_store[ht - 2])
        P.wait("vector", sDV, qb_inc)
        P.op("vector", "scalar_tensor_tensor", qoutv[st_set], qav[st_set], -1.0,
             qbv[st_set], ALU.add, ALU.add, inc=((sDV, sDV.bump() and 1),))
        dma("sync", dvh, qoutv[st_set], sST, wait=((sDV, sDV.n),))
        qz_store[ht] = sST.n

    def quant_piece_hook():
        pi = qz_state["i"]
        if pi >= len(q_pieces):
            while qz_pend:
                qz_flush_one()
            return
        qz_state["i"] = pi + 1
        view, dview, invc = q_pieces[pi]
        wb = wsl0 if pi % 2 == 0 else wsl1
        w = (((sDV, qz_rd_done[pi - 2]),) if pi >= 2
             else ((sST, wo_bounce_st[15]), (sDV, v_done_dv)))
        dma("sync", wb, view, sLD, wait=w)
        lv = sLD.n
        for half in range(2):
            ht = pi * 2 + half
            st_set = ht % 2
            if half == 0:
                P.wait("vector", sLD, lv)
            if pi == 0 and half == 0:
                P.wait("vector", sDV, scales_done)
            wbh = wb[:, half * 512:(half + 1) * 512]
            P.op("vector", "tensor_scalar", qav[st_set], wbh, scl2[:, invc:invc + 1],
                 0.5, ALU.mult, ALU.is_ge, inc=((sDV, sDV.bump() and 1),))
            P.op("vector", "tensor_scalar", qbv[st_set], wbh, scl2[:, invc:invc + 1],
                 -0.5, ALU.mult, ALU.is_ge, inc=((sDV, sDV.bump() and 1),))
            qb_inc = sDV.n
            if half == 1:
                qz_rd_done[pi] = sDV.n
            qz_pend.append((ht, dview[:, half * 512:(half + 1) * 512], st_set, qb_inc))
            if len(qz_pend) > 1:
                qz_flush_one()

    qh_state = {"c": 0}

    def s6_hook():
        c = qh_state["c"]
        qh_state["c"] = c + 1
        if c % 2 == 0 and not any(ch in noil for ch in "qQf"):
            quant_piece_hook()

    q_done_dv = proj_sweep_fm(Wq, qf, 0, first_wait=((sST, scin_done),),
                              slot_hook=s6_hook)
    if not any(c in noil for c in "Qf"):
        while qz_state["i"] < len(q_pieces) or qz_pend:
            quant_piece_hook()
    qz_last_store = sST.n
    if not any(c in noil for c in "QfG"):
        P.op("gpsimd", "collective_compute", "AllGather", ALU.bypass,
             [list(range(NCORES))], [q1s_d.opt()], [q1_d.opt()],
             wait=((sST, sST.n), (sWD, sWD.n)), inc=((sCC, sCC.bump() and 1),))
        P.op("gpsimd", "collective_compute", "AllGather", ALU.bypass,
             [list(range(NCORES))], [q2s_d.opt()], [q2_d.opt()],
             inc=((sCC, sCC.bump() and 1),))
        qz_ag.append(sCC.n)
    q_ag_done = qz_ag[-1] if qz_ag else 0
    if stop == "s6":
        return finish_early()

    P.wait("tensor", sDV, q_done_dv)
    P.wait("tensor", sST, qz_last_store)
    at_evac = {}
    for j in range(NT):
        for h in range(NH):
            i = j * NH + h
            pp = pq[i % 4]
            if i >= 4:
                P.wait("tensor", sDV, at_evac[i - 4])
            P.op("tensor", "matmul", pp[:, 0:128],
                 kffm[:, h * 1024 + j * 128: h * 1024 + (j + 1) * 128],
                 qf[:, h * 1024 + j * 128: h * 1024 + (j + 1) * 128],
                 start=True, stop=True, inc=((sPE, sPE.bump() and 1),))
            mm_v = sPE.n
            P.wait("vector", sPE, mm_v)
            P.op("vector", "tensor_mul", As[:, i * 128:(i + 1) * 128], pp[:, 0:128], maskT,
                 inc=((sDV, sDV.bump() and 1),))
            at_evac[i] = sDV.n
    as_done = sDV.n
    if stop == "s7":
        return finish_early()
    if stop == "s7x":
        import os as _os
        _o = int(_os.environ.get("KMK_DOFF", "0"))
        return finish_early(sb_dump=As[:, _o:_o + 2048])

    P.op("vector", "memset", Pc, 0.0, inc=((sDV, sDV.bump() and 1),))
    pc_zero = sDV.n
    P.wait("sync", sCC, state_ag_done)
    pc_ld = {}
    pc_acc_r = {}
    if "C" in noil:
        pc_acc = pc_zero
        for r in range(NCORES - 1):
            w = [(sDV, pc_acc)]
            if r == 0:
                w.append((sDV, q_done_dv))
            dma("sync", prefb, sc_out[r * 128:(r + 1) * 128, :], sLD, wait=tuple(w))
            rv = sLD.n
            P.wait("vector", sLD, rv)
            P.wait("vector", sDV, pc_acc)
            P.op("vector", "scalar_tensor_tensor", Pc, prefb, pmask[:, r:r + 1], Pc,
                 ALU.mult, ALU.add, inc=((sDV, sDV.bump() and 1),))
            pc_acc = sDV.n
        pc_done = pc_acc
    else:
        for r in range(2):
            dma("sync", prefb if r == 0 else prefb2, sc_out[r * 128:(r + 1) * 128, :],
                sLD, wait=((sDV, q_done_dv),))
            pc_ld[r] = sLD.n
        pc_acc = pc_zero
        for r in range(NCORES - 1):
            P.wait("vector", sLD, pc_ld[r])
            P.wait("vector", sDV, pc_acc)
            P.op("vector", "scalar_tensor_tensor", Pc,
                 prefb if r % 2 == 0 else prefb2, pmask[:, r:r + 1], Pc,
                 ALU.mult, ALU.add, inc=((sDV, sDV.bump() and 1),))
            pc_acc = sDV.n
            pc_acc_r[r] = pc_acc
            if r + 2 < NCORES - 1:
                dma("sync", prefb if (r + 2) % 2 == 0 else prefb2,
                    sc_out[(r + 2) * 128:(r + 3) * 128, :], sLD,
                    wait=((sDV, pc_acc),))
                pc_ld[r + 2] = sLD.n
        pc_done = pc_acc
    if stop == "b0":
        return finish_early()

    oa_evac = {}
    ofm_evacs = []
    b1_ld = {}
    b1_mm_last = {}
    noil = noil + "B"
    if "B" not in noil:
        for j in range(2):
            dma("sync", prefb if j % 2 == 0 else prefb2,
                prefixd[j * 128:(j + 1) * 128, :], sLD, wait=((sDV, pc_done),))
            b1_ld[j] = sLD.n
    for j in range(NT):
        if "B" in noil:
            pfb, stb = prefb, statebf
            dma("sync", prefb, prefixd[j * 128:(j + 1) * 128, :], sLD,
                wait=((sDV, pc_done if j == 0 else sDV.n),))
            rv = sLD.n
            P.wait("vector", sLD, rv)
            P.op("vector", "tensor_add", stb, pfb, Pc, inc=((sDV, sDV.bump() and 1),))
            stbf_v = sDV.n
        else:
            pfb = prefb if j % 2 == 0 else prefb2
            stb = statebf if j % 2 == 0 else statebf2
            P.wait("vector", sLD, b1_ld[j])
            if j == 0:
                P.wait("vector", sDV, pc_done)
            if j >= 2:
                P.wait("vector", sPE, b1_mm_last[j - 2])
            P.op("vector", "tensor_add", stb, pfb, Pc, inc=((sDV, sDV.bump() and 1),))
            stbf_v = sDV.n
            if j + 2 < NT:
                dma("sync", prefb if j % 2 == 0 else prefb2,
                    prefixd[(j + 2) * 128:(j + 3) * 128, :], sLD, wait=((sDV, stbf_v),))
                b1_ld[j + 2] = sLD.n
        P.wait("tensor", sDV, stbf_v)
        if j == 0:
            P.wait("tensor", sDV, as_done)
        zc_inc = {}
        rc_inc = {}
        for t in range(NH + 2):
            if t >= 2:
                h2 = t - 2
                i2 = j * NH + h2
                if h2 == 0 and j >= 1:
                    P.wait("vector", sTT, ofm_evacs[j - 1])
                P.wait("vector", sDV, rc_inc[h2])
                P.op("vector", "tensor_scalar", o_tm[:, h2 * 128:(h2 + 1) * 128],
                     pq[i2 % 2][:, 0:128], rcol[:, h2:h2 + 1], None, ALU.mult,
                     inc=((sDV, sDV.bump() and 1),))
                oa_evac[i2] = sDV.n
            if t >= 1 and t <= NH:
                h1 = t - 1
                P.wait("vector", sDV, zc_inc[h1])
                P.op("vector", "reciprocal", rcol[:, h1:h1 + 1], zcol[:, h1:h1 + 1],
                     inc=((sDV, sDV.bump() and 1),))
                rc_inc[h1] = sDV.n
            if t < NH:
                h = t
                i = j * NH + h
                pp = pq[i % 2]
                if i >= 2:
                    P.wait("tensor", sDV, oa_evac[i - 2])
                P.op("tensor", "matmul", pp[:, 0:128], As[:, i * 128:(i + 1) * 128],
                     vtm[:, j * 2048 + h * 128: j * 2048 + (h + 1) * 128],
                     start=True, stop=False)
                P.op("tensor", "matmul", pp[:, 128:129], As[:, i * 128:(i + 1) * 128],
                     onescol, start=False, stop=False)
                P.op("tensor", "matmul", pp[:, 0:129],
                     qf[:, h * 1024 + j * 128: h * 1024 + (j + 1) * 128],
                     stb[:, h * 129:(h + 1) * 129], start=False, stop=True,
                     inc=((sPE, sPE.bump() and 1),))
                mm_v = sPE.n
                if h == NH - 1:
                    b1_mm_last[j] = mm_v
                P.wait("vector", sPE, mm_v)
                P.op("vector", "tensor_scalar", zcol[:, h:h + 1], pp[:, 128:129],
                     EPS_AT, None, ALU.add, inc=((sDV, sDV.bump() and 1),))
                zc_inc[h] = sDV.n
        stbf_used = sDV.n
        otm_done = sDV.n
        for h in range(NH):
            P.op("sync", "dma_start",
                 wait=((sDV, otm_done),) if h == 0 else (),
                 inc=((sTT, sTT.bump(16) and 16),),
                 out=As[:, (j * NH + h) * 128:(j * NH + h + 1) * 128],
                 in_=o_tm[:, h * 128:(h + 1) * 128], transpose=True)
        ofm_evacs.append(sTT.n)
    b1_done_ac = sTT.n
    b1_done_dv = sDV.n
    if stop == "f0q":
        return finish_early(q1_d.flatten().bitcast(F32)[0:T * D])
    if stop == "f0q2":
        return finish_early(q2_d.flatten().bitcast(F32)[0:T * D])
    if stop == "b1":
        return finish_early()
    if stop == "b1x":
        import os as _os
        _o = int(_os.environ.get("KMK_DOFF", "0"))
        return finish_early(sb_dump=As[:, _o:_o + 2048])

    b2_extra = ((sCC, q_ag_done),) if "W" in noil else ()
    P.op("vector", "memset", st_s1, 0.0, wait=((sTT, b1_done_ac),))
    P.op("vector", "memset", st_s2, 0.0, inc=((sDV, sDV.bump() and 1),))
    b2_zero = sDV.n
    wo_h_done = {}
    wo_evac = {}
    for er in range(b2er):
        for h in range(min(b2h, NH)):
            ih = er * NH + h
            wb = wob0 if ih % 2 == 0 else wob1
            w = [(sPE, wo_h_done[ih - 2])] if ih >= 2 else [(sDV, q_done_dv)] + list(b2_extra)
            w.append((sST, wo_bounce_st[h]))
            dma("sync", wb, wo_bf[h * 128:(h + 1) * 128, er * 512:(er + 1) * 512], sLD,
                wait=tuple(w))
            lv = sLD.n
            P.wait("tensor", sLD, lv)
            if ih == 0:
                P.wait("tensor", sTT, b1_done_ac)
                P.wait("tensor", sDV, b1_done_dv)
            for j in range(NT):
                pp = pq[j // 2][:, (j % 2) * 512:(j % 2) * 512 + 512]
                if h == 0 and er > 0:
                    P.wait("tensor", sDV, wo_evac[(er - 1, j)])
                P.op("tensor", "matmul", pp, As[:, (j * NH + h) * 128:(j * NH + h + 1) * 128],
                     wb, start=(h == 0), stop=(h == min(b2h, NH) - 1),
                     inc=((sPE, sPE.bump() and 1),) if j == NT - 1 else ())
            wo_h_done[ih] = sPE.n
            if h == min(b2h, NH) - 1:
                mm_v = sPE.n
                for j in range(NT):
                    pp = pq[j // 2][:, (j % 2) * 512:(j % 2) * 512 + 512]
                    P.wait("vector", sPE, mm_v)
                    if er == 0 and j == 0:
                        P.wait("vector", sWD, bo_ready)
                        P.wait("vector", sDV, b2_zero)
                    P.op("vector", "tensor_add",
                         yo[:, j * 2048 + er * 512: j * 2048 + (er + 1) * 512],
                         pp, bvbo[:, er * 512:(er + 1) * 512],
                         inc=((sDV, sDV.bump() and 1),))
                    P.wait("vector", sDV, sDV.n)
                    P.op("vector", "reduce_sum", st_m[:, j:j + 1],
                         yo[:, j * 2048 + er * 512: j * 2048 + (er + 1) * 512], axis=AX,
                         inc=((sDV, sDV.bump() and 1),))
                    P.wait("vector", sDV, sDV.n)
                    P.op("vector", "tensor_add", st_s1[:, j:j + 1], st_s1[:, j:j + 1],
                         st_m[:, j:j + 1], inc=((sDV, sDV.bump() and 1),))
                    yov = sDV.n
                    P.wait("scalar", sDV, yov)
                    P.op("scalar", "activation", b2dump,
                         yo[:, j * 2048 + er * 512: j * 2048 + (er + 1) * 512],
                         AF.Square, accum_out=st_t1[:, j:j + 1], inc=((sAC, sAC.bump() and 1),))
                    sqv = sAC.n
                    P.wait("vector", sAC, sqv)
                    P.op("vector", "tensor_add", st_s2[:, j:j + 1], st_s2[:, j:j + 1],
                         st_t1[:, j:j + 1], inc=((sDV, sDV.bump() and 1),))
                    P.wait("vector", sDV, sDV.n)
                    wo_evac[(er, j)] = sDV.n
    b2_done_dv = sDV.n
    if stop == "b2":
        return finish_early()
    if stop == "b2y":
        import os as _os
        _o = int(_os.environ.get("KMK_DOFF", "0"))
        return finish_early(sb_dump=yo[:, _o:_o + 2048])

    P.wait("vector", sDV, b2_done_dv)
    P.op("vector", "tensor_scalar", st_m, st_s1, 1.0 / D, None, ALU.mult,
         inc=((sDV, sDV.bump() and 1),))
    P.op("vector", "tensor_scalar", st_s2, st_s2, 1.0 / D, None, ALU.mult,
         inc=((sDV, sDV.bump() and 1),))
    P.wait("vector", sDV, sDV.n)
    P.op("vector", "tensor_mul", st_t1, st_m, st_m, inc=((sDV, sDV.bump() and 1),))
    P.wait("vector", sDV, sDV.n)
    P.op("vector", "tensor_sub", st_v, st_s2, st_t1, inc=((sDV, sDV.bump() and 1),))
    vv = sDV.n
    P.wait("scalar", sDV, vv)
    P.op("scalar", "activation", st_t1, st_v, AF.Ln, bias=EPS_LN,
         inc=((sAC, sAC.bump() and 1),))
    P.wait("scalar", sAC, sAC.n)
    P.op("scalar", "activation", st_r, st_t1, AF.Exp, scale=-0.5, inc=((sAC, sAC.bump() and 1),))
    rv_ = sAC.n
    P.wait("vector", sAC, rv_)
    for j in range(NT):
        w = ((sDV, sDV.n),) if j >= 1 else ((sDV, rv_ and sDV.n),)
        dma("sync", xbuf, xint_d[j * 128:(j + 1) * 128, :], sLD, wait=w)
        xv = sLD.n
        P.op("vector", "tensor_scalar", tbuf, yo[:, j * 2048:(j + 1) * 2048],
             st_m[:, j:j + 1], st_r[:, j:j + 1], ALU.subtract, ALU.mult,
             inc=((sDV, sDV.bump() and 1),))
        P.wait("vector", sDV, sDV.n)
        P.wait("vector", sLD, xv)
        if j >= 1:
            P.wait("vector", sST, prev_x2_store)
        P.op("vector", "tensor_add", x2b, tbuf, xbuf, inc=((sDV, sDV.bump() and 1),))
        dma("sync", x2_d[j * 128:(j + 1) * 128, :], x2b, sST, wait=((sDV, sDV.n),))
        prev_x2_store = sST.n
    x2_done_st = sST.n
    if stop == "b3":
        return finish_early()
    if stop == "b3x":
        return finish_early(x2_d.flatten())

    pass0_done = ln_sweep(x2_d, xn2fm, (xn2stg, xn2stg1), sqd2, 384,
                          extra_first_wait=((sDV, sDV.n), (sST, x2_done_st)))
    if stop == "f1":
        return finish_early()
    if stop == "f1x":
        import os as _os
        _o = int(_os.environ.get("KMK_DOFF", "0"))
        return finish_early(sb_dump=xn2fm[:, _o:_o + 2048])

    P.op("vector", "memset", runmax, 0.0, wait=((sDV, sDV.n),), inc=((sDV, sDV.bump() and 1),))
    rm_zero = sDV.n
    q1_mm_done = {}
    hc_relu = {}
    hc_sq = {}
    hc_dv = {}
    hc_st = {}
    for g in range(8):
        qs = qstr0 if g % 2 == 0 else qstr1
        w = [(sPE, q1_mm_done[g - 2])] if g >= 2 else [(sCC, q_ag_done), (sTT, pass0_done)]
        for k in range(NKC):
            dma("gpsimd", qs[:, k * 1024:(k + 1) * 1024],
                q1_d[k * 128:(k + 1) * 128, g * 1024:(g + 1) * 1024], sWD,
                wait=tuple(w) if k == 0 else ())
        wv = sWD.n
        P.wait("tensor", sWD, wv)
        if g == 0:
            P.wait("tensor", sTT, pass0_done)
        for hc in range(8):
            Hc = g * 8 + hc
            pp = pq[hc % 2]
            if Hc >= 2:
                P.wait("tensor", sAC, hc_relu[Hc - 2])
            for k in range(NKC):
                lhsT = qs[:, k * 1024 + hc * 128: k * 1024 + (hc + 1) * 128]
                for half in range(2):
                    P.op("tensor", "matmul", pp[:, half * 512:(half + 1) * 512],
                         lhsT, xn2fm[:, k * 1024 + half * 512: k * 1024 + (half + 1) * 512],
                         start=(k == 0), stop=(k == NKC - 1),
                         inc=((sPE, sPE.bump() and 1),) if (k == NKC - 1 and half == 1) else ())
            mm_v = sPE.n
            r2b = r2e0 if Hc % 2 == 0 else r2e1
            P.wait("scalar", sPE, mm_v)
            if Hc >= 2:
                P.wait("scalar", sST, hc_st[Hc - 2])
                P.wait("scalar", sDV, hc_dv[Hc - 2])
            P.op("scalar", "activation", r2tmp, pp[:], AF.Relu, inc=((sAC, sAC.bump() and 1),))
            hc_relu[Hc] = sAC.n
            P.wait("scalar", sAC, sAC.n)
            P.op("scalar", "activation", r2b, r2tmp, AF.Square, scale=scl2[:, 2:3],
                 inc=((sAC, sAC.bump() and 1),))
            hc_sq[Hc] = sAC.n
            P.wait("vector", sAC, hc_sq[Hc])
            if Hc == 0:
                P.wait("vector", sDV, rm_zero)
            P.wait("vector", sDV, sDV.n)
            P.op("vector", "tensor_max", runmax, runmax, r2b, inc=((sDV, sDV.bump() and 1),))
            hc_dv[Hc] = sDV.n
            dma("sync", hr2_d[Hc * 128:(Hc + 1) * 128, :], r2b, sST, wait=((sAC, hc_sq[Hc]),))
            hc_st[Hc] = sST.n
        q1_mm_done[g] = sPE.n
    pass1_st = sST.n
    pass1_dv = sDV.n

    xm_prev = None

    def split3(srcf, w):
        P.op("vector", "tensor_copy", xs_hi[:, 0:w], srcf, inc=((sDV, sDV.bump() and 1),))
        P.wait("vector", sDV, sDV.n)
        P.op("vector", "tensor_sub", xs_rf[:, 0:w], srcf, xs_hi[:, 0:w],
             inc=((sDV, sDV.bump() and 1),))
        P.wait("vector", sDV, sDV.n)
        P.op("vector", "tensor_copy", xs_m[:, 0:w], xs_rf[:, 0:w],
             inc=((sDV, sDV.bump() and 1),))
        P.wait("vector", sDV, sDV.n)
        P.op("vector", "tensor_sub", xs_lf[:, 0:w], xs_rf[:, 0:w], xs_m[:, 0:w],
             inc=((sDV, sDV.bump() and 1),))
        P.wait("vector", sDV, sDV.n)
        P.op("vector", "tensor_copy", xs_l[:, 0:w], xs_lf[:, 0:w],
             inc=((sDV, sDV.bump() and 1),))
        return sDV.n

    def xpose3(w, dstf):
        nb = w // 128
        sv = sDV.n
        for jb in range(nb):
            for (sb_, db_) in ((xs_hi, xs_hit), (xs_m, xs_mt), (xs_l, xs_lt)):
                P.op("sync", "dma_start", wait=((sDV, sv),) if jb == 0 else (),
                     inc=((sTT, sTT.bump(16) and 16),),
                     out=db_[:, jb * 128:(jb + 1) * 128],
                     in_=sb_[:, jb * 128:(jb + 1) * 128], transpose=True)
        P.wait("vector", sTT, sTT.n)
        P.op("vector", "tensor_add", dstf, xs_hit[:, 0:w], xs_mt[:, 0:w],
             inc=((sDV, sDV.bump() and 1),))
        P.wait("vector", sDV, sDV.n)
        P.op("vector", "tensor_add", dstf, dstf, xs_lt[:, 0:w],
             inc=((sDV, sDV.bump() and 1),))
        P.wait("vector", sDV, sDV.n)
        return sDV.n

    P.wait("vector", sDV, pass1_dv)
    split3(runmax, 1024)
    xpose3(1024, xT)
    for j in range(NT):
        P.op("vector", "reduce_max", xmaxc[:, j:j + 1], xT[:, j * 128:(j + 1) * 128], axis=AX,
             inc=((sDV, sDV.bump() and 1),))
        P.wait("vector", sDV, sDV.n)
        P.op("vector", "tensor_scalar", xmaxc[:, j:j + 1], xmaxc[:, j:j + 1], EPS_Q, None, ALU.max,
             inc=((sDV, sDV.bump() and 1),))
        P.wait("vector", sDV, sDV.n)
        P.op("vector", "reciprocal", b1c[:, j:j + 1], xmaxc[:, j:j + 1],
             inc=((sDV, sDV.bump() and 1),))
        P.op("vector", "tensor_scalar", b2c[:, j:j + 1], xmaxc[:, j:j + 1], scl2[:, 3:4],
             1.0 / 255.0, ALU.mult, ALU.mult, inc=((sDV, sDV.bump() and 1),))
        P.wait("vector", sDV, sDV.n)
        P.op("vector", "tensor_scalar", b1c[:, j:j + 1], b1c[:, j:j + 1], 255.0, None, ALU.mult,
             inc=((sDV, sDV.bump() and 1),))
        xm_prev = sDV.n
    P.op("vector", "memset", xs_lf, 0.0, wait=((sDV, xm_prev),),
         inc=((sDV, sDV.bump() and 1),))
    P.wait("vector", sDV, sDV.n)
    for r in range(NT):
        P.op("vector", "tensor_copy", xs_lf[:, r * 128 + r: r * 128 + r + 1],
             b1c[:, r:r + 1], inc=((sDV, sDV.bump() and 1),) if r == NT - 1 else ())
    P.wait("vector", sDV, sDV.n)
    P.op("vector", "tensor_copy", xs_hi, xs_lf, inc=((sDV, sDV.bump() and 1),))
    P.wait("vector", sDV, sDV.n)
    P.op("vector", "tensor_sub", xs_rf, xs_lf, xs_hi, inc=((sDV, sDV.bump() and 1),))
    P.wait("vector", sDV, sDV.n)
    P.op("vector", "tensor_copy", xs_m, xs_rf, inc=((sDV, sDV.bump() and 1),))
    P.wait("vector", sDV, sDV.n)
    P.op("vector", "tensor_sub", xs_rf, xs_rf, xs_m, inc=((sDV, sDV.bump() and 1),))
    P.wait("vector", sDV, sDV.n)
    P.op("vector", "tensor_copy", xs_l, xs_rf, inc=((sDV, sDV.bump() and 1),))
    xpose3(1024, xT)
    brow_v = sDV.n
    P.op("vector", "memset", padb[0:8, 0:128], 1.0, wait=((sDV, brow_v),),
         inc=((sDV, sDV.bump() and 1),))
    ones8 = sDV.n
    P.wait("tensor", sDV, ones8)
    for half in range(2):
        P.op("tensor", "matmul", pq[2][:, half * 512:(half + 1) * 512],
             padb[0:8, 0:128], xT[0:8, half * 512:(half + 1) * 512],
             start=True, stop=True,
             inc=((sPE, sPE.bump() and 1),) if half == 1 else ())
    b1bc_mm = sPE.n
    P.wait("vector", sPE, b1bc_mm)
    P.op("vector", "tensor_copy", b1b, pq[2], inc=((sDV, sDV.bump() and 1),))
    bb_ld = sDV.n
    bmax_st = sST.n
    if stop == "f2":
        return finish_early()
    if stop == "f2b":
        P.wait("vector", sDV, bb_ld)
        P.op("vector", "tensor_copy", r2e0, b1b, inc=((sDV, sDV.bump() and 1),))
        P.op("vector", "tensor_copy", r2e1[:, 0:8], b2c, inc=((sDV, sDV.bump() and 1),))
        bbv = sDV.n
        P.op("vector", "memset", x2b, 0.0, wait=((sST, sST.n),), inc=((sDV, sDV.bump() and 1),))
        zv = sDV.n
        for j in range(NT):
            dma("sync", y_out[j * 128:(j + 1) * 128, :], x2b, sST, wait=((sDV, zv),))
        dma("sync", y_out[0:128, 0:1024], r2e0, sST, wait=((sDV, bbv), (sST, sST.n)))
        dma("sync", y_out[128:256, 0:1024], r2e1, sST)
        dma("sync", y_out[256:384, 0:4], scl2, sST)
        P.wait("sync", sST, sST.n)
        _emit(nc, P, sems)
        return nc
    if stop == "f2h":
        return finish_early(hr2_d.flatten()[0:T * D])

    P.op("vector", "memset", outsb[:, 0:8192], 0.0,
         wait=((sDV, sDV.n), (sPE, b1bc_mm)))
    P.op("vector", "memset", outsb[:, 8192:16384], 0.0, inc=((sDV, sDV.bump() and 1),))
    out_zero = sDV.n
    p2_mm_done = {}
    p2_acc = {}
    hq_ready = {}
    for kg in range(16):
        r2bufs = r2r if kg % 2 == 0 else r2rb
        q2bufs = q2r if kg % 2 == 0 else q2rb
        hqb = hqr if (kg % 2 == 0 or "H" in noil) else hqrb
        for kk in range(4):
            kc = kg * 4 + kk
            w = [(sDV, hq_ready[kc - 8])] if kc >= 8 else [(sST, pass1_st)]
            dma("sync", r2bufs[kk], hr2_d[kc * 128:(kc + 1) * 128, :], sLD, wait=tuple(w))
            rl = sLD.n
            w2_ = [(sPE, p2_mm_done[kg - 2])] if kg >= 2 else [(sCC, q_ag_done)]
            dma("gpsimd", q2bufs[kk], q2_d[kc * 128:(kc + 1) * 128, :], sWD, wait=tuple(w2_))
            P.wait("vector", sLD, rl)
            if kc == 0:
                P.wait("vector", sDV, bb_ld)
            if "H" in noil:
                if kg >= 1:
                    P.wait("vector", sPE, p2_mm_done[kg - 1])
            elif kg >= 2:
                P.wait("vector", sPE, p2_mm_done[kg - 2])
            P.op("vector", "tensor_mul", r2bufs[kk], r2bufs[kk], b1b,
                 inc=((sDV, sDV.bump() and 1),))
            P.wait("vector", sDV, sDV.n)
            P.op("vector", "tensor_scalar", intq, r2bufs[kk], RHALF, None, ALU.add,
                 inc=((sDV, sDV.bump() and 1),))
            P.wait("vector", sDV, sDV.n)
            P.op("vector", "tensor_copy", r2bufs[kk], intq,
                 inc=((sDV, sDV.bump() and 1),))
            P.wait("vector", sDV, sDV.n)
            P.op("vector", "tensor_copy", hqb[kk], r2bufs[kk], inc=((sDV, sDV.bump() and 1),))
            hq_ready[kc] = sDV.n
        qload_v = sWD.n
        for j in range(NT):
            for eh in range(2):
                pp = pq[(j % 2) * 2 + eh]
                P.wait("tensor", sDV, hq_ready[kg * 4 + 3])
                P.wait("tensor", sWD, qload_v)
                if kg > 0 or j >= 2:
                    key = (kg, j - 2, eh) if j >= 2 else (kg - 1, NT - 2 + j, eh)
                    P.wait("tensor", sDV, p2_acc[key])
                for kk in range(4):
                    for q_ in range(2):
                        P.op("tensor", "matmul", pp[:, q_ * 512:(q_ + 1) * 512],
                             hqb[kk][:, j * 128:(j + 1) * 128],
                             q2bufs[kk][:, eh * 1024 + q_ * 512: eh * 1024 + (q_ + 1) * 512],
                             start=(kk == 0), stop=(kk == 3),
                             inc=((sPE, sPE.bump() and 1),) if (kk == 3 and q_ == 1) else ())
                mm_v = sPE.n
                P.wait("vector", sPE, mm_v)
                if kg == 0 and j == 0 and eh == 0:
                    P.wait("vector", sDV, out_zero)
                P.op("vector", "scalar_tensor_tensor",
                     outsb[:, j * 2048 + eh * 1024: j * 2048 + (eh + 1) * 1024],
                     pp[:], b2c[:, j:j + 1],
                     outsb[:, j * 2048 + eh * 1024: j * 2048 + (eh + 1) * 1024],
                     ALU.mult, ALU.add, inc=((sDV, sDV.bump() and 1),))
                p2_acc[(kg, j, eh)] = sDV.n
        p2_mm_done[kg] = sPE.n
    pass2_dv = sDV.n
    if stop == "f3":
        return finish_early()
    if stop == "f3z":
        for j in range(NT):
            dma("sync", y_out[j * 128:(j + 1) * 128, :], outsb[:, j * 2048:(j + 1) * 2048],
                sST, wait=((sDV, pass2_dv),) if j == 0 else ())
        P.wait("sync", sST, sST.n)
        _emit(nc, P, sems)
        return nc
    if stop == "f3x":
        import os as _os
        _o = int(_os.environ.get("KMK_DOFF", "0"))
        return finish_early(sb_dump=outsb[:, _o:_o + 2048])

    P.wait("vector", sDV, pass2_dv)
    P.op("vector", "memset", st_s1, 0.0)
    P.op("vector", "memset", st_s2, 0.0, inc=((sDV, sDV.bump() and 1),))
    for j in range(NT):
        P.op("vector", "reduce_sum", st_s1[:, j:j + 1], outsb[:, j * 2048:(j + 1) * 2048],
             axis=AX, inc=((sDV, sDV.bump() and 1),))
        ssv = sDV.n
        P.wait("scalar", sDV, ssv)
        P.op("scalar", "activation", f4dump, outsb[:, j * 2048:(j + 1) * 2048], AF.Square,
             accum_out=st_s2[:, j:j + 1], inc=((sAC, sAC.bump() and 1),))
    f4_sq = sAC.n
    P.wait("vector", sAC, f4_sq)
    P.op("vector", "tensor_scalar", st_m, st_s1, 1.0 / D, None, ALU.mult,
         inc=((sDV, sDV.bump() and 1),))
    P.op("vector", "tensor_scalar", st_s2, st_s2, 1.0 / D, None, ALU.mult,
         inc=((sDV, sDV.bump() and 1),))
    P.wait("vector", sDV, sDV.n)
    P.op("vector", "tensor_mul", st_t1, st_m, st_m, inc=((sDV, sDV.bump() and 1),))
    P.wait("vector", sDV, sDV.n)
    P.op("vector", "tensor_sub", st_v, st_s2, st_t1, inc=((sDV, sDV.bump() and 1),))
    vv2 = sDV.n
    P.wait("scalar", sDV, vv2)
    P.op("scalar", "activation", st_t1, st_v, AF.Ln, bias=EPS_LN,
         inc=((sAC, sAC.bump() and 1),))
    P.wait("scalar", sAC, sAC.n)
    P.op("scalar", "activation", st_r, st_t1, AF.Exp, scale=-0.5, inc=((sAC, sAC.bump() and 1),))
    rv2 = sAC.n
    P.wait("vector", sAC, rv2)
    for j in range(NT):
        w = ((sDV, sDV.n),) if j >= 1 else ()
        dma("sync", xbuf, x2_d[j * 128:(j + 1) * 128, :], sLD, wait=w)
        xv = sLD.n
        P.op("vector", "tensor_scalar", tbuf, outsb[:, j * 2048:(j + 1) * 2048],
             st_m[:, j:j + 1], st_r[:, j:j + 1], ALU.subtract, ALU.mult,
             inc=((sDV, sDV.bump() and 1),))
        P.wait("vector", sDV, sDV.n)
        P.wait("vector", sLD, xv)
        if j >= 1:
            P.wait("vector", sST, prev_y_store)
        P.op("vector", "tensor_add", x2b, tbuf, xbuf, inc=((sDV, sDV.bump() and 1),))
        dma("sync", y_out[j * 128:(j + 1) * 128, :], x2b, sST, wait=((sDV, sDV.n),))
        prev_y_store = sST.n
    P.wait("sync", sST, sST.n)
    _emit(nc, P, sems)
    return nc


def _emit(nc, P, sems):
    from contextlib import ExitStack
    with ExitStack() as ctx:
        for s in sems.values():
            s.h = ctx.enter_context(nc.semaphore(s.name))
        block = ctx.enter_context(nc.Block())
        block.sync(lambda e: [f(e) for f in P.q["sync"]])
        block.vector(lambda e: [f(e) for f in P.q["vector"]])
        block.scalar(lambda e: [f(e) for f in P.q["scalar"]])
        block.tensor(lambda e: [f(e) for f in P.q["tensor"]])
        block.gpsimd(lambda e: [f(e) for f in P.q["gpsimd"]])


_NC_CACHE = None


def kernel(L0_attn_x, L0_attn_Wq, L0_attn_bq, L0_attn_Wk, L0_attn_bk,
           L0_attn_Wv, L0_attn_bv, L0_attn_Wo, L0_attn_bo,
           L0_ffn_c_fc, L0_ffn_c_proj):
    global _NC_CACHE
    x = np.ascontiguousarray(np.asarray(L0_attn_x, dtype=np.float32).reshape(8192, 2048))
    Wq_ = np.ascontiguousarray(np.asarray(L0_attn_Wq, dtype=np.float32))
    Wk_ = np.ascontiguousarray(np.asarray(L0_attn_Wk, dtype=np.float32))
    Wv_ = np.ascontiguousarray(np.asarray(L0_attn_Wv, dtype=np.float32))
    Wo_ = np.ascontiguousarray(np.asarray(L0_attn_Wo, dtype=np.float32))
    bq = np.asarray(L0_attn_bq, dtype=np.float32)
    bk = np.asarray(L0_attn_bk, dtype=np.float32)
    bv = np.asarray(L0_attn_bv, dtype=np.float32)
    bo = np.asarray(L0_attn_bo, dtype=np.float32)
    c_fc = np.ascontiguousarray(np.asarray(L0_ffn_c_fc, dtype=np.float32))
    c_projT = np.ascontiguousarray(np.asarray(L0_ffn_c_proj, dtype=np.float32).T)

    bqk = np.ascontiguousarray(
        np.concatenate([bq.reshape(16, 128).T, bk.reshape(16, 128).T], axis=1))
    bv_b = np.ascontiguousarray(np.broadcast_to(bv[None, :], (128, 2048)))
    bo_b = np.ascontiguousarray(np.broadcast_to(bo[None, :], (128, 2048)))
    maskT = np.triu(np.ones((128, 128), dtype=np.float32)).astype(ml_dtypes.bfloat16)
    eye = np.eye(128, dtype=ml_dtypes.bfloat16)
    eyef32 = np.eye(128, dtype=np.float32)
    onescol = np.ones((128, 1), dtype=ml_dtypes.bfloat16)
    cstf_host = np.ascontiguousarray(
        np.broadcast_to(np.array([1e-5, -0.5], dtype=np.float32)[None, :], (128, 2)))

    if _NC_CACHE is None:
        import os
        _NC_CACHE = build_program(stop=os.environ.get("KMK_STOP", "full"),
                                  noil=os.environ.get("KMK_NOIL", ""),
                                  b2er=int(os.environ.get("KMK_B2ER", "4")),
                                  b2h=int(os.environ.get("KMK_B2H", "16")))
    nc = _NC_CACHE

    in_maps = []
    for c in range(NCORES):
        pm = np.zeros((128, 8), dtype=np.float32)
        pm[:, :c] = 1.0
        in_maps.append({
            "x": np.ascontiguousarray(x[c * T:(c + 1) * T]),
            "Wq": Wq_, "Wk": Wk_, "Wv": Wv_, "Wo": Wo_,
            "bqk": bqk, "bv_b": bv_b, "bo_b": bo_b,
            "w1s": np.ascontiguousarray(c_fc[c * 256:(c + 1) * 256]),
            "w2s": np.ascontiguousarray(c_projT[c * 1024:(c + 1) * 1024]),
            "maskT": maskT, "eye": eye, "eyef32": eyef32, "onescol": onescol,
            "pmask": pm, "cstf": cstf_host,
        })
    res = run_bass_kernel_spmd(nc, in_maps, core_ids=list(range(NCORES)))
    y = np.concatenate([res.results[c]["y"] for c in range(NCORES)], axis=0)
    return y.reshape(1, 8192, 2048)


# version 12
# speedup vs baseline: 1.0500x; 1.0037x over previous
import numpy as np
import ml_dtypes
import concourse.bass as bass
import concourse.mybir as mybir
from concourse.bass_utils import run_bass_kernel_spmd

F32 = mybir.dt.float32
BF16 = mybir.dt.bfloat16
I32 = mybir.dt.int32
ALU = mybir.AluOpType
AF = mybir.ActivationFunctionType
AX = mybir.AxisListType.X

NCORES = 8
T = 1024
NT = 8
D = 2048
NH = 16
DFF = 8192
NKC = 16
EPS_LN = 1e-5
EPS_AT = 1e-6
EPS_Q = 1e-5
RHALF = 0.49999994

ENGS = ("sync", "vector", "scalar", "tensor", "gpsimd")


class Sem:
    def __init__(self, name):
        self.name = name
        self.n = 0
        self.h = None

    def bump(self, k=1):
        self.n += k
        return self.n


class Prog:
    def __init__(self):
        self.q = {k: [] for k in ENGS}

    def op(self, eng, meth, *args, wait=(), inc=(), **kw):
        def f(e, meth=meth, args=args, kw=kw, wait=tuple(wait), inc=tuple(inc)):
            for s, v in wait:
                e.wait_ge(s.h, v)
            bi = getattr(e, meth)(*args, **kw)
            for s, k in inc:
                bi.then_inc(s.h, k)
        self.q[eng].append(f)

    def wait(self, eng, sem, val):
        def f(e, sem=sem, val=val):
            e.wait_ge(sem.h, val)
        self.q[eng].append(f)


def build_program(stop="full", b2er=4, b2h=16, noil=""):
    nc = bass.Bass("TRN2", target_bir_lowering=False, detect_race_conditions=False)

    def din(name, shape, dt=F32):
        return nc.dram_tensor(name, shape, dt, kind="ExternalInput").ap()

    x_in = din("x", [T, D])
    Wq = din("Wq", [D, D]); Wk = din("Wk", [D, D]); Wv = din("Wv", [D, D]); Wo = din("Wo", [D, D])
    bqk_in = din("bqk", [128, 32])
    bv_in = din("bv_b", [128, D])
    bo_in = din("bo_b", [128, D])
    w1s_in = din("w1s", [256, DFF])
    w2s_in = din("w2s", [T, D])
    maskT_in = din("maskT", [128, 128], BF16)
    eye_in = din("eye", [128, 128], BF16)
    eyef_in = din("eyef32", [128, 128], F32)
    ones_in = din("onescol", [128, 1], BF16)
    pmask_in = din("pmask", [128, 8])
    cstf_in = din("cstf", [128, 2])

    y_out = nc.dram_tensor("y", [T, D], F32, kind="ExternalOutput").ap()

    def dint(name, shape, dt=F32):
        return nc.dram_tensor(name, shape, dt, kind="Internal").ap()

    def dshr(name, shape, dt=F32):
        return nc.dram_tensor(name, shape, dt, kind="Internal",
                              addr_space="Shared").ap()

    prefixd = dint("prefixd", [NT * 128, NH * 129])
    sc_in = dint("sc_in", [128, NH * 129])
    sc_out = dshr("sc_out", [NCORES * 128, NH * 129])
    p1c_d = dint("p1c_d", [128, 1]); p2c_d = dint("p2c_d", [128, 1])
    scl_in = dint("scl_in", [2, 128]); scl_out = dshr("scl_out", [2, 128])
    q1s_d = dint("q1s_d", [256, DFF], BF16)
    q1_d = dshr("q1_d", [D, DFF], BF16)
    q2s_d = dint("q2s_d", [T, D], BF16)
    q2_d = dshr("q2_d", [DFF, D], BF16)
    x2_d = dint("x2_d", [T, D])
    wo_bf = dint("wo_bf", [D, D], BF16)
    xint_d = dint("xint_d", [T, D])
    hr2_d = dint("hr2_d", [DFF, T])
    b1_d = dint("b1_d", [1, T]); b2_d = dint("b2_d", [1, T])

    SBB = (nc.sbuf_base + 31) // 32 * 32

    def sb(name, off, shape, dt):
        return nc.alloc_sbuf_tensor_at(name, shape, dt, offset=SBB + off).ap()

    xnfm = sb("xnfm", 0, [128, 16384], BF16)
    wob0 = sb("wob0", 16384, [128, 512], BF16)
    wob1 = sb("wob1", 17440, [128, 512], BF16)
    Pc = sb("Pc", 0, [128, NH * 129], F32)
    tbuf = sb("tbuf", 0, [128, 2048], F32)
    x2b = sb("x2b", 8192, [128, 2048], F32)
    xn2fm = sb("xn2fm", 0, [128, 16384], BF16)

    qf = sb("qf", 32768, [128, 16384], BF16)
    xb0 = sb("xb0", 32768, [128, 2048], F32)
    xb1 = sb("xb1", 40960, [128, 2048], F32)
    xnstg = sb("xnstg", 49152, [128, 2048], BF16)
    xnstg1 = sb("xnstg1", 53248, [128, 2048], BF16)
    sqdump = sb("sqdump", 57344, [128, 2048], BF16)
    wsl0 = sb("wsl0", 110592, [128, 1024], F32)
    wsl1 = sb("wsl1", 114688, [128, 1024], F32)
    qav = [sb("qa0", 118784, [128, 512], F32), sb("qa1", 120832, [128, 512], F32)]
    qbv = [sb("qb0", 122880, [128, 512], F32), sb("qb1", 124928, [128, 512], F32)]
    swsl0 = sb("swsl0", 196608, [128, 512], F32)
    swsl1 = sb("swsl1", 198656, [128, 512], F32)
    wbs0 = sb("wbs0", 110592, [128, 2048], BF16)
    wbs1 = sb("wbs1", 114688, [128, 2048], BF16)
    yo = sb("yo", 32768, [128, 16384], F32)
    outsb = sb("outsb", 32768, [128, 16384], F32)

    kffm = sb("kffm", 65536, [128, 16384], BF16)
    qoutv = [sb("qout0", 126976, [128, 512], BF16), sb("qout1", 128000, [128, 512], BF16)]
    xn2stg = sb("xn2stg", 69632, [128, 2048], BF16)
    xn2stg1 = sb("xn2stg1", 73728, [128, 2048], BF16)
    sqd2 = sb("sqd2", 77824, [128, 2048], BF16)
    qstr0 = sb("qstr0", 32768, [128, 16384], BF16)
    qstr1 = sb("qstr1", 65536, [128, 16384], BF16)

    As = sb("As", 98304, [128, 16384], BF16)
    elu_m = sb("elu_m", 98304, [128, 1024], F32)
    elu_e = sb("elu_e", 102400, [128, 1024], F32)
    elu_p = sb("elu_p", 106496, [128, 1024], F32)
    r2e0 = sb("r2e0", 98304, [128, 1024], F32)
    r2e1 = sb("r2e1", 102400, [128, 1024], F32)
    r2tmp = sb("r2tmp", 106496, [128, 1024], F32)
    f4dump = sb("f4dump", 98304, [128, 2048], BF16)
    runmax = sb("runmax", 110592, [128, 1024], F32)
    xs_hi = sb("xs_hi", 32768, [128, 1024], BF16)
    xs_m = sb("xs_m", 34816, [128, 1024], BF16)
    xs_l = sb("xs_l", 36864, [128, 1024], BF16)
    xs_rf = sb("xs_rf", 40960, [128, 1024], F32)
    xs_lf = sb("xs_lf", 45056, [128, 1024], F32)
    xs_hit = sb("xs_hit", 49152, [128, 1024], BF16)
    xs_mt = sb("xs_mt", 51200, [128, 1024], BF16)
    xs_lt = sb("xs_lt", 53248, [128, 1024], BF16)
    xT = sb("xT", 57344, [128, 1024], F32)
    padb = sb("padb", 61440, [128, 128], F32)
    r2r = [sb(f"r2r{i}", 114688 + 4096 * i, [128, 1024], F32) for i in range(4)]

    vtm = sb("vtm", 131072, [128, 16384], BF16)
    b2dump = sb("b2dump", 131072, [128, 512], BF16)
    r2rb = [sb(f"r2rb{i}", 131072 + 4096 * i, [128, 1024], F32) for i in range(4)]
    q2r = [sb(f"q2r{i}", 147456 + 4096 * i, [128, 2048], BF16) for i in range(4)]

    stripes = sb("stripes", 163840, [128, 16384], BF16)
    state = sb("state", 163840, [128, NH * 129], F32)
    kftm0 = sb("kftm0", 172096, [128, 2048], BF16)
    kftm1 = sb("kftm1", 176192, [128, 2048], BF16)
    prefb = sb("prefb", 163840, [128, NH * 129], F32)
    prefb2 = sb("prefb2", 180480, [128, NH * 129], F32)
    statebf2 = sb("statebf2", 188736, [128, NH * 129], BF16)
    statebf = sb("statebf", 172096, [128, NH * 129], BF16)
    o_tm = sb("o_tm", 176256, [128, 2048], BF16)
    zcol = sb("zcol", 180352, [128, 16], F32)
    rcol = sb("rcol", 180416, [128, 16], F32)
    q2rb = [sb(f"q2rb{i}", 163840 + 4096 * i, [128, 2048], BF16) for i in range(4)]
    hqr = [sb(f"hqr{i}", 180480 + 2048 * i, [128, 1024], BF16) for i in range(4)]
    hqrb = [sb(f"hqrb{i}", 98304 + 2048 * i, [128, 1024], BF16) for i in range(4)]
    intq = sb("intq", 188672, [128, 1024], I32)
    bvbo = sb("bvbo", 200960, [128, 2048], BF16)
    b1b = sb("b1b", 192768, [128, 1024], F32)
    b2b = sb("b2b", 196864, [128, 1024], F32)
    xbuf = sb("xbuf", 200960, [128, 2048], F32)
    cst = 209152

    def cst_sb(name, shape, dt):
        nonlocal cst
        t = sb(name, cst, shape, dt)
        nbytes = shape[1] * mybir.dt.size(dt)
        cst += (nbytes + 31) // 32 * 32
        return t

    bqk = cst_sb("bqk", [128, 32], F32)
    maskT = cst_sb("maskT", [128, 128], BF16)
    eye = cst_sb("eye", [128, 128], BF16)
    eyef = cst_sb("eyef", [128, 128], F32)
    onescol = cst_sb("onescol_s", [128, 1], BF16)
    pmask = cst_sb("pmask_s", [128, 8], F32)
    st_s1 = cst_sb("st_s1", [128, 8], F32)
    st_m = cst_sb("st_m", [128, 8], F32)
    st_s2 = cst_sb("st_s2", [128, 8], F32)
    st_v = cst_sb("st_v", [128, 8], F32)
    st_r = cst_sb("st_r", [128, 8], F32)
    st_t1 = cst_sb("st_t1", [128, 8], F32)
    scl2 = cst_sb("scl2", [128, 4], F32)
    pr12 = cst_sb("pr12", [128, 2], F32)
    prrow_t = cst_sb("prrow", [1, 128], F32)
    prsc = cst_sb("prsc", [1, 2], F32)
    xmaxc = cst_sb("xmaxc", [128, 8], F32)
    b1c = cst_sb("b1c", [128, 8], F32)
    b2c = cst_sb("b2c", [128, 8], F32)
    cstf = cst_sb("cstf", [128, 2], F32)
    dumc = cst_sb("dumc", [128, 1], F32)
    nc.const_aps.register(F32, EPS_LN, cstf[:, 0:1])
    nc.const_aps.register(F32, -0.5, cstf[:, 1:2])
    assert cst <= 211328, cst
    nc.bump_sbuf(211328)

    pq = [nc.alloc_psum_tensor(f"pq{i}", [128, 1024], F32).ap() for i in range(4)]
    pqb = [p.bitcast(BF16) for p in pq]

    sems = {n: Sem(n) for n in ("sLD", "sWD", "sST", "sPE", "sDV", "sAC", "sCC", "sTT")}
    sLD, sWD, sST, sPE, sDV, sAC, sCC, sTT = (sems[n] for n in
        ("sLD", "sWD", "sST", "sPE", "sDV", "sAC", "sCC", "sTT"))

    P = Prog()

    def finish_early(copy_from=None, sb_dump=None):
        P.op("vector", "memset", x2b, 0.0, wait=((sST, sST.n),), inc=((sDV, sDV.bump() and 1),))
        zv = sDV.n
        if copy_from is not None:
            dma("sync", y_out.flatten(), copy_from, sST,
                wait=((sDV, zv), (sST, sST.n), (sCC, sCC.n)))
        elif sb_dump is not None:
            for j in range(NT):
                dma("sync", y_out[j * 128:(j + 1) * 128, :], x2b, sST, wait=((sDV, zv),))
            dma("gpsimd", y_out[0:128, 0:sb_dump.shape[1]], sb_dump, sST,
                wait=((sST, sST.n), (sAC, sAC.n), (sDV, sDV.n), (sPE, sPE.n), (sCC, sCC.n)))
        else:
            for j in range(NT):
                dma("sync", y_out[j * 128:(j + 1) * 128, :], x2b, sST, wait=((sDV, zv),))
        P.wait("sync", sST, sST.n)
        _emit(nc, P, sems)
        return nc

    def dma(eng, out, in_, sem, wait=()):
        P.op(eng, "dma_start", wait=wait, inc=((sem, sem.bump(16) and 16),), out=out, in_=in_)

    for dst, src in ((bqk, bqk_in), (maskT, maskT_in), (eye, eye_in), (eyef, eyef_in),
                     (onescol, ones_in), (pmask, pmask_in), (cstf, cstf_in)):
        dma("sync", dst, src, sLD)
    s0_done = sLD.n
    P.wait("tensor", sLD, s0_done)
    P.wait("vector", sLD, s0_done)
    P.wait("scalar", sLD, s0_done)

    def ln_sweep(src_dram, dst_fm, stg_pair, dump, bank_lo, extra_first_wait=(),
                 bounce_dst=None, bounce_st_out=None):
        tr_done_prev = None
        prev_evac = None
        for j in range(NT):
            xb = xb0 if j % 2 == 0 else xb1
            stg = stg_pair[j % 2]
            w = list(extra_first_wait) if j < 2 else [(sAC, xn_ready[j - 2])]
            if bounce_dst is not None and j >= 2:
                w.append((sST, bounce_st_out[j - 2]))
            dma("sync", xb, src_dram[j * 128:(j + 1) * 128, :], sLD, wait=tuple(w))
            ldv = sLD.n
            if bounce_dst is not None:
                dma("sync", bounce_dst[j * 128:(j + 1) * 128, :], xb, sST,
                    wait=((sLD, ldv),))
                bounce_st_out[j] = sST.n
            P.wait("vector", sLD, ldv)
            P.op("vector", "reduce_sum", st_s1[:, j:j + 1], xb, axis=AX, inc=((sDV, sDV.bump() and 1),))
            P.wait("vector", sDV, sDV.n)
            P.wait("scalar", sLD, ldv)
            P.op("scalar", "activation", dump, xb, AF.Square, accum_out=st_s2[:, j:j + 1],
                 inc=((sAC, sAC.bump() and 1),))
            sq_done = sAC.n
            P.op("vector", "tensor_scalar", st_m[:, j:j + 1], st_s1[:, j:j + 1], 1.0 / D, None, ALU.mult,
                 inc=((sDV, sDV.bump() and 1),))
            P.wait("vector", sAC, sq_done)
            P.op("vector", "tensor_scalar", st_s2[:, j:j + 1], st_s2[:, j:j + 1], 1.0 / D, None, ALU.mult,
                 inc=((sDV, sDV.bump() and 1),))
            P.wait("vector", sDV, sDV.n)
            P.op("vector", "tensor_mul", st_t1[:, j:j + 1], st_m[:, j:j + 1], st_m[:, j:j + 1],
                 inc=((sDV, sDV.bump() and 1),))
            P.wait("vector", sDV, sDV.n)
            P.op("vector", "tensor_sub", st_v[:, j:j + 1], st_s2[:, j:j + 1], st_t1[:, j:j + 1],
                 inc=((sDV, sDV.bump() and 1),))
            var_done = sDV.n
            P.wait("scalar", sDV, var_done)
            P.op("scalar", "activation", st_t1[:, j:j + 1], st_v[:, j:j + 1], AF.Ln, bias=EPS_LN,
                 inc=((sAC, sAC.bump() and 1),))
            P.wait("scalar", sAC, sAC.n)
            P.op("scalar", "activation", st_r[:, j:j + 1], st_t1[:, j:j + 1], AF.Exp, scale=-0.5,
                 inc=((sAC, sAC.bump() and 1),))
            rstd_done = sAC.n
            P.wait("vector", sAC, rstd_done)
            P.op("vector", "tensor_scalar", st_t1[:, j:j + 1], st_m[:, j:j + 1],
                 st_r[:, j:j + 1], -1.0, ALU.mult, ALU.mult, inc=((sDV, sDV.bump() and 1),))
            bc_v = sDV.n
            if tr_done_prev is not None:
                P.wait("scalar", sTT, tr_done_prev)
            P.wait("scalar", sDV, bc_v)
            P.op("scalar", "activation", stg, xb, AF.Identity, bias=st_t1[:, j:j + 1],
                 scale=st_r[:, j:j + 1], inc=((sAC, sAC.bump() and 1),))
            if j == 0:
                xn_ready = {}
            xn_ready[j] = sAC.n
            for k in range(NKC):
                P.op("sync", "dma_start",
                     wait=((sAC, xn_ready[j]),) if k == 0 else (),
                     inc=((sTT, sTT.bump(16) and 16),),
                     out=dst_fm[:, k * 1024 + j * 128: k * 1024 + (j + 1) * 128],
                     in_=stg[:, k * 128:(k + 1) * 128], transpose=True)
            tr_done_prev = sTT.n
        return sTT.n

    xb_bounce = {}
    s1_done_ac = ln_sweep(x_in, xnfm, (xnstg, xnstg1), sqdump, 0,
                          bounce_dst=xint_d, bounce_st_out=xb_bounce)
    if stop == "s1":
        return finish_early()
    if stop == "s1x":
        return finish_early(sb_dump=xnfm[:, 0:2048])

    def proj_sweep_fm(Wsrc, out_sbuf, bias_off, first_wait=(), slot_hook=None):
        stripe_done = {}
        bank_free = {}
        for qt in range(4):
            stripe = stripes[:, (qt % 2) * 8192:(qt % 2) * 8192 + 8192]
            w = ((sPE, stripe_done[qt - 2]),) if qt >= 2 else tuple(first_wait)
            for k in range(NKC):
                dma("gpsimd", stripe[:, k * 512:(k + 1) * 512],
                    Wsrc[k * 128:(k + 1) * 128, qt * 512:(qt + 1) * 512], sWD,
                    wait=w if k == 0 else ())
            wv = sWD.n
            P.wait("tensor", sWD, wv)
            if qt == 0:
                P.wait("tensor", sTT, s1_done_ac)
            for hh in range(4):
                h = qt * 4 + hh
                pp = pq[hh]
                if hh in bank_free:
                    P.wait("tensor", sDV, bank_free[hh])
                for k in range(NKC):
                    lhsT = stripe[:, k * 512 + hh * 128: k * 512 + (hh + 1) * 128]
                    for half in range(2):
                        P.op("tensor", "matmul", pp[:, half * 512:(half + 1) * 512],
                             lhsT, xnfm[:, k * 1024 + half * 512: k * 1024 + (half + 1) * 512],
                             start=(k == 0), stop=(k == NKC - 1),
                             inc=((sPE, sPE.bump() and 1),) if (k == NKC - 1 and half == 1) else ())
                    if slot_hook is not None and k % 4 == 0:
                        slot_hook()
                mm_v = sPE.n
                bcol = bqk[:, bias_off + h: bias_off + h + 1]
                P.wait("vector", sPE, mm_v)
                P.op("vector", "tensor_scalar", elu_m, pp[:], bcol, 0.0, ALU.add, ALU.min,
                     inc=((sDV, sDV.bump() and 1),))
                mv = sDV.n
                P.wait("scalar", sDV, mv)
                P.op("scalar", "activation", elu_e, elu_m, AF.Exp, inc=((sAC, sAC.bump() and 1),))
                ev = sAC.n
                P.op("vector", "tensor_scalar", elu_p, pp[:], bcol, 0.0, ALU.add, ALU.max,
                     inc=((sDV, sDV.bump() and 1),))
                P.wait("vector", sDV, sDV.n)
                P.wait("vector", sAC, ev)
                P.op("vector", "tensor_add", out_sbuf[:, h * 1024:(h + 1) * 1024], elu_p, elu_e,
                     inc=((sDV, sDV.bump() and 1),))
                bank_free[hh] = sDV.n
            stripe_done[qt] = sPE.n
        return sDV.n

    if "f" not in noil:
        P.op("vector", "memset", pr12, 0.0, wait=((sLD, s0_done),),
             inc=((sDV, sDV.bump() and 1),))
    sc_pieces = []
    for (srcw, rows, colw, colidx) in ((w1s_in, 2, DFF, 0), (w2s_in, 8, D, 1)):
        srcv = srcw.rearrange("(n p) e -> n p e", p=128)
        for rr in range(rows):
            for cc in range(colw // 512):
                sc_pieces.append((srcv[rr, :, cc * 512:(cc + 1) * 512], colidx))
    sc_red_done = {}
    sc_state = {"i": 0}

    def scale_piece_hook():
        pi = sc_state["i"]
        if pi >= len(sc_pieces):
            return
        sc_state["i"] = pi + 1
        view, colidx = sc_pieces[pi]
        wb = swsl0 if pi % 2 == 0 else swsl1
        w = ((sDV, sc_red_done[pi - 2]),) if pi >= 2 else ()
        dma("sync", wb, view, sLD, wait=w)
        lv = sLD.n
        P.wait("vector", sLD, lv)
        P.op("vector", "reduce_sum", b1c[:, 0:1], wb, axis=AX, apply_absolute_value=True,
             inc=((sDV, sDV.bump() and 1),))
        P.wait("vector", sDV, sDV.n)
        P.op("vector", "tensor_add", pr12[:, colidx:colidx + 1], pr12[:, colidx:colidx + 1],
             b1c[:, 0:1], inc=((sDV, sDV.bump() and 1),))
        sc_red_done[pi] = sDV.n
        P.wait("vector", sDV, sDV.n)

    k_done_dv = proj_sweep_fm(
        Wk, kffm, 16,
        slot_hook=None if ("s" in noil or "f" in noil) else scale_piece_hook)
    if "f" not in noil:
        while sc_state["i"] < len(sc_pieces):
            scale_piece_hook()
    if stop == "s2":
        return finish_early()
    if stop == "s2x":
        return finish_early(sb_dump=kffm[:, 0:2048])

    if "f" in noil:
        prsc_copy_v = 0
        scl_store_v = sST.n
    else:
        padf = elu_m[:, 0:128]
        hi_b = elu_e.bitcast(BF16)[:, 0:128]
        m_b = elu_e.bitcast(BF16)[:, 128:256]
        l_b = elu_e.bitcast(BF16)[:, 256:384]
        hi_t = elu_e.bitcast(BF16)[:, 512:640]
        m_t = elu_e.bitcast(BF16)[:, 640:768]
        l_t = elu_e.bitcast(BF16)[:, 768:896]
        r_f = elu_m[:, 128:256]
        l_f = elu_m[:, 256:384]
        rec = elu_p[:, 0:128]
        P.op("vector", "memset", padf, 0.0, wait=((sDV, k_done_dv),),
             inc=((sDV, sDV.bump() and 1),))
        P.wait("vector", sDV, sDV.n)
        P.op("vector", "tensor_copy", padf[:, 0:2], pr12, inc=((sDV, sDV.bump() and 1),))
        P.wait("vector", sDV, sDV.n)
        P.op("vector", "tensor_copy", hi_b, padf, inc=((sDV, sDV.bump() and 1),))
        P.wait("vector", sDV, sDV.n)
        P.op("vector", "tensor_sub", r_f, padf, hi_b, inc=((sDV, sDV.bump() and 1),))
        P.wait("vector", sDV, sDV.n)
        P.op("vector", "tensor_copy", m_b, r_f, inc=((sDV, sDV.bump() and 1),))
        P.wait("vector", sDV, sDV.n)
        P.op("vector", "tensor_sub", l_f, r_f, m_b, inc=((sDV, sDV.bump() and 1),))
        P.wait("vector", sDV, sDV.n)
        P.op("vector", "tensor_copy", l_b, l_f, inc=((sDV, sDV.bump() and 1),))
        pr_split = sDV.n
        for (sb_, db_) in ((hi_b, hi_t), (m_b, m_t), (l_b, l_t)):
            P.op("sync", "dma_start", wait=((sDV, pr_split),),
                 inc=((sTT, sTT.bump(16) and 16),), out=db_, in_=sb_, transpose=True)
        pr_tt = sTT.n
        P.wait("vector", sTT, pr_tt)
        P.op("vector", "tensor_add", rec, hi_t, m_t, inc=((sDV, sDV.bump() and 1),))
        P.wait("vector", sDV, sDV.n)
        P.op("vector", "tensor_add", rec, rec, l_t, inc=((sDV, sDV.bump() and 1),))
        P.wait("vector", sDV, sDV.n)
        P.op("vector", "reduce_sum", elu_m[:, 384:385], rec, axis=AX,
             inc=((sDV, sDV.bump() and 1),))
        P.wait("scalar", sDV, sDV.n)
        P.op("vector", "memset", elu_m[:, 512:640], 1.0, inc=((sDV, sDV.bump() and 1),))
        P.wait("scalar", sDV, sDV.n)
        P.op("scalar", "activation", elu_p[:, 128:256], elu_m[:, 512:640], AF.Identity,
             scale=elu_m[:, 384:385], inc=((sAC, sAC.bump() and 1),))
        prsc_copy_v = sAC.n
        dma("sync", scl_in, elu_p[0:2, 128:256], sST, wait=((sAC, prsc_copy_v),))
        scl_store_v = sST.n

    dma("gpsimd", bvbo, bv_in, sWD, wait=((sDV, k_done_dv),))
    bv_ready = sWD.n
    wo_bounce_st = {}
    v_stripe_done = {}
    v_tile_done = {}
    for er in range(4):
        stripe = stripes[:, (er % 2) * 8192:(er % 2) * 8192 + 8192]
        w = ((sPE, v_stripe_done[er - 2]),) if er >= 2 else ((sDV, k_done_dv),)
        for k in range(NKC):
            dma("gpsimd", stripe[:, k * 512:(k + 1) * 512],
                Wv[k * 128:(k + 1) * 128, er * 512:(er + 1) * 512], sWD,
                wait=w if k == 0 else ())
        wv = sWD.n
        for c in range(4):
            hbc = er * 4 + c
            wbb = wbs0 if hbc % 2 == 0 else wbs1
            bw = ((sST, wo_bounce_st[hbc - 2]),) if hbc >= 2 else ()
            dma("gpsimd", wbb, Wo[hbc * 128:(hbc + 1) * 128, :], sWD, wait=bw)
            blv = sWD.n
            dma("sync", wo_bf[hbc * 128:(hbc + 1) * 128, :], wbb, sST, wait=((sWD, blv),))
            wo_bounce_st[hbc] = sST.n
        P.wait("tensor", sWD, wv)
        if er == 0:
            P.wait("tensor", sDV, k_done_dv)
        for j in range(NT):
            pp = pq[j // 2][:, (j % 2) * 512:(j % 2) * 512 + 512]
            if er >= 1:
                P.wait("tensor", sDV, v_tile_done[(er - 1, j)])
            for k in range(NKC):
                P.op("tensor", "matmul", pp,
                     xnfm[:, k * 1024 + j * 128: k * 1024 + (j + 1) * 128],
                     stripe[:, k * 512:(k + 1) * 512],
                     start=(k == 0), stop=(k == NKC - 1),
                     inc=((sPE, sPE.bump() and 1),) if k == NKC - 1 else ())
            mm_v = sPE.n
            P.wait("vector", sPE, mm_v)
            if er == 0 and j == 0:
                P.wait("vector", sWD, bv_ready)
            P.op("vector", "tensor_add",
                 vtm[:, j * 2048 + er * 512: j * 2048 + (er + 1) * 512],
                 pp, bvbo[:, er * 512:(er + 1) * 512], inc=((sDV, sDV.bump() and 1),))
            v_tile_done[(er, j)] = sDV.n
        v_stripe_done[er] = sPE.n
    v_done_dv = sDV.n
    if stop == "s4":
        return finish_early()
    if stop == "s4x":
        import os as _os
        _o = int(_os.environ.get("KMK_DOFF", "0"))
        return finish_early(sb_dump=vtm[:, _o:_o + 2048])

    if "f" in noil:
        scales_done = 0
    else:
        P.op("gpsimd", "collective_compute", "AllReduce", ALU.add,
             [list(range(NCORES))], [scl_in.opt()], [scl_out.opt()],
             wait=((sST, scl_store_v),), inc=((sCC, sCC.bump() and 1),))
        scale_ar = sCC.n
        dma("sync", scl2[:, 0:1], scl_out[0:1, 0:1].broadcast_to([128, 1]), sLD,
            wait=((sCC, scale_ar),))
        dma("sync", scl2[:, 1:2], scl_out[1:2, 0:1].broadcast_to([128, 1]), sLD)
        scl_bcast_v = sLD.n
        P.wait("vector", sLD, scl_bcast_v)
        P.op("vector", "tensor_scalar", scl2[:, 2:3], scl2[:, 0:1], 1.0 / (D * DFF), EPS_Q,
             ALU.mult, ALU.max, inc=((sDV, sDV.bump() and 1),))
        P.op("vector", "tensor_scalar", scl2[:, 3:4], scl2[:, 1:2], 1.0 / (D * DFF), EPS_Q,
             ALU.mult, ALU.max, inc=((sDV, sDV.bump() and 1),))
        P.wait("vector", sDV, sDV.n)
        P.op("vector", "reciprocal", scl2[:, 0:1], scl2[:, 2:3],
             inc=((sDV, sDV.bump() and 1),))
        P.op("vector", "reciprocal", scl2[:, 1:2], scl2[:, 3:4],
             inc=((sDV, sDV.bump() and 1),))
        P.wait("vector", sDV, sDV.n)
        scales_done = sDV.n

    P.wait("tensor", sDV, v_done_dv)
    P.op("vector", "memset", state, 0.0, wait=((sDV, v_done_dv),), inc=((sDV, sDV.bump() and 1),))
    state_zero = sDV.n
    prev_state_add = state_zero
    prefix_written = []
    kv_evac = {}
    kv_mm_done = {}
    for j in range(NT):
        ktb = kftm0 if j % 2 == 0 else kftm1
        for h in range(NH):
            w = []
            if h == 0:
                w.append((sDV, v_done_dv))
                if j >= 2:
                    w.append((sPE, kv_mm_done[j - 2]))
            P.op("sync", "dma_start", wait=tuple(w),
                 inc=((sTT, sTT.bump(16) and 16),),
                 out=ktb[:, h * 128:(h + 1) * 128],
                 in_=kffm[:, h * 1024 + j * 128: h * 1024 + (j + 1) * 128], transpose=True)
        kt_done = sTT.n
        dma("sync", prefixd[j * 128:(j + 1) * 128, :], state, sST,
            wait=((sDV, prev_state_add),))
        prefix_written.append(sST.n)
        P.wait("tensor", sTT, kt_done)
        for h in range(NH):
            i = j * NH + h
            pp = pq[h % 2]
            if i >= 2:
                P.wait("tensor", sDV, kv_evac[i - 2])
            P.op("tensor", "matmul", pp[:, 0:128], ktb[:, h * 128:(h + 1) * 128],
                 vtm[:, j * 2048 + h * 128: j * 2048 + (h + 1) * 128], start=True, stop=True)
            P.op("tensor", "matmul", pp[:, 128:129], ktb[:, h * 128:(h + 1) * 128],
                 onescol, start=True, stop=True, inc=((sPE, sPE.bump() and 1),))
            mm_v = sPE.n
            P.wait("vector", sPE, mm_v)
            if h == 0:
                P.wait("vector", sST, prefix_written[j])
            P.op("vector", "tensor_add", state[:, h * 129:(h + 1) * 129],
                 state[:, h * 129:(h + 1) * 129], pp[:, 0:129], inc=((sDV, sDV.bump() and 1),))
            kv_evac[i] = sDV.n
        prev_state_add = sDV.n
        kv_mm_done[j] = sPE.n

    dma("gpsimd", bvbo, bo_in, sWD, wait=((sDV, v_done_dv),))
    bo_ready = sWD.n
    dma("sync", sc_in, state, sST, wait=((sDV, prev_state_add),))
    scin_done = sST.n
    P.op("gpsimd", "collective_compute", "AllGather", ALU.bypass,
         [list(range(NCORES))], [sc_in.opt()], [sc_out.opt()],
         wait=((sST, scin_done),), inc=((sCC, sCC.bump() and 1),))
    state_ag_done = sCC.n
    if stop == "s5":
        return finish_early()

    q_pieces = []
    for (srcw, dstw, rows, colw, invc) in ((w1s_in, q1s_d, 2, DFF, 0),
                                           (w2s_in, q2s_d, 8, D, 1)):
        srcv = srcw.rearrange("(n p) e -> n p e", p=128)
        dstv = dstw.rearrange("(n p) e -> n p e", p=128)
        for rr in range(rows):
            for cc in range(colw // 1024):
                q_pieces.append((srcv[rr, :, cc * 1024:(cc + 1) * 1024],
                                 dstv[rr, :, cc * 1024:(cc + 1) * 1024], invc))
    qz_rd_done = {}
    qz_store = {}
    qz_state = {"i": 0}
    qz_ag = []
    qz_pend = []

    def qz_flush_one():
        if not qz_pend:
            return
        ht, dvh, st_set, qb_inc = qz_pend.pop(0)
        if ht >= 2:
            P.wait("vector", sST, qz_store[ht - 2])
        P.wait("vector", sDV, qb_inc)
        P.op("vector", "scalar_tensor_tensor", qoutv[st_set], qav[st_set], -1.0,
             qbv[st_set], ALU.add, ALU.add, inc=((sDV, sDV.bump() and 1),))
        dma("sync", dvh, qoutv[st_set], sST, wait=((sDV, sDV.n),))
        qz_store[ht] = sST.n

    def quant_piece_hook():
        pi = qz_state["i"]
        if pi >= len(q_pieces):
            while qz_pend:
                qz_flush_one()
            return
        qz_state["i"] = pi + 1
        view, dview, invc = q_pieces[pi]
        wb = wsl0 if pi % 2 == 0 else wsl1
        w = (((sDV, qz_rd_done[pi - 2]),) if pi >= 2
             else ((sST, wo_bounce_st[15]), (sDV, v_done_dv)))
        dma("sync", wb, view, sLD, wait=w)
        lv = sLD.n
        for half in range(2):
            ht = pi * 2 + half
            st_set = ht % 2
            if half == 0:
                P.wait("vector", sLD, lv)
            if pi == 0 and half == 0:
                P.wait("vector", sDV, scales_done)
            wbh = wb[:, half * 512:(half + 1) * 512]
            P.op("vector", "tensor_scalar", qav[st_set], wbh, scl2[:, invc:invc + 1],
                 0.5, ALU.mult, ALU.is_ge, inc=((sDV, sDV.bump() and 1),))
            P.op("vector", "tensor_scalar", qbv[st_set], wbh, scl2[:, invc:invc + 1],
                 -0.5, ALU.mult, ALU.is_ge, inc=((sDV, sDV.bump() and 1),))
            qb_inc = sDV.n
            if half == 1:
                qz_rd_done[pi] = sDV.n
            qz_pend.append((ht, dview[:, half * 512:(half + 1) * 512], st_set, qb_inc))
            if len(qz_pend) > 1:
                qz_flush_one()

    qh_state = {"c": 0}

    def s6_hook():
        c = qh_state["c"]
        qh_state["c"] = c + 1
        if c % 2 == 0 and not any(ch in noil for ch in "qQf"):
            quant_piece_hook()

    q_done_dv = proj_sweep_fm(Wq, qf, 0, first_wait=((sST, scin_done),),
                              slot_hook=s6_hook)
    if not any(c in noil for c in "Qf"):
        while qz_state["i"] < len(q_pieces) or qz_pend:
            quant_piece_hook()
    qz_last_store = sST.n
    if not any(c in noil for c in "QfG"):
        P.op("gpsimd", "collective_compute", "AllGather", ALU.bypass,
             [list(range(NCORES))], [q1s_d.opt()], [q1_d.opt()],
             wait=((sST, sST.n), (sWD, sWD.n)), inc=((sCC, sCC.bump() and 1),))
        P.op("gpsimd", "collective_compute", "AllGather", ALU.bypass,
             [list(range(NCORES))], [q2s_d.opt()], [q2_d.opt()],
             inc=((sCC, sCC.bump() and 1),))
        qz_ag.append(sCC.n)
    q_ag_done = qz_ag[-1] if qz_ag else 0
    if stop == "s6":
        return finish_early()

    P.wait("tensor", sDV, q_done_dv)
    P.wait("tensor", sST, qz_last_store)
    at_evac = {}
    for j in range(NT):
        for h in range(NH):
            i = j * NH + h
            pp = pq[i % 4]
            if i >= 4:
                P.wait("tensor", sDV, at_evac[i - 4])
            P.op("tensor", "matmul", pp[:, 0:128],
                 kffm[:, h * 1024 + j * 128: h * 1024 + (j + 1) * 128],
                 qf[:, h * 1024 + j * 128: h * 1024 + (j + 1) * 128],
                 start=True, stop=True, inc=((sPE, sPE.bump() and 1),))
            mm_v = sPE.n
            P.wait("vector", sPE, mm_v)
            P.op("vector", "tensor_mul", As[:, i * 128:(i + 1) * 128], pp[:, 0:128], maskT,
                 inc=((sDV, sDV.bump() and 1),))
            at_evac[i] = sDV.n
    as_done = sDV.n
    if stop == "s7":
        return finish_early()
    if stop == "s7x":
        import os as _os
        _o = int(_os.environ.get("KMK_DOFF", "0"))
        return finish_early(sb_dump=As[:, _o:_o + 2048])

    P.op("vector", "memset", Pc, 0.0, inc=((sDV, sDV.bump() and 1),))
    pc_zero = sDV.n
    P.wait("sync", sCC, state_ag_done)
    pc_ld = {}
    pc_acc_r = {}
    if "C" in noil:
        pc_acc = pc_zero
        for r in range(NCORES - 1):
            w = [(sDV, pc_acc)]
            if r == 0:
                w.append((sDV, q_done_dv))
            dma("sync", prefb, sc_out[r * 128:(r + 1) * 128, :], sLD, wait=tuple(w))
            rv = sLD.n
            P.wait("vector", sLD, rv)
            P.wait("vector", sDV, pc_acc)
            P.op("vector", "scalar_tensor_tensor", Pc, prefb, pmask[:, r:r + 1], Pc,
                 ALU.mult, ALU.add, inc=((sDV, sDV.bump() and 1),))
            pc_acc = sDV.n
        pc_done = pc_acc
    else:
        for r in range(2):
            dma("sync", prefb if r == 0 else prefb2, sc_out[r * 128:(r + 1) * 128, :],
                sLD, wait=((sDV, q_done_dv),))
            pc_ld[r] = sLD.n
        pc_acc = pc_zero
        for r in range(NCORES - 1):
            P.wait("vector", sLD, pc_ld[r])
            P.wait("vector", sDV, pc_acc)
            P.op("vector", "scalar_tensor_tensor", Pc,
                 prefb if r % 2 == 0 else prefb2, pmask[:, r:r + 1], Pc,
                 ALU.mult, ALU.add, inc=((sDV, sDV.bump() and 1),))
            pc_acc = sDV.n
            pc_acc_r[r] = pc_acc
            if r + 2 < NCORES - 1:
                dma("sync", prefb if (r + 2) % 2 == 0 else prefb2,
                    sc_out[(r + 2) * 128:(r + 3) * 128, :], sLD,
                    wait=((sDV, pc_acc),))
                pc_ld[r + 2] = sLD.n
        pc_done = pc_acc
    if stop == "b0":
        return finish_early()

    oa_evac = {}
    ofm_evacs = []
    b1_ld = {}
    b1_mm_last = {}
    noil = noil + "B"
    if "B" not in noil:
        for j in range(2):
            dma("sync", prefb if j % 2 == 0 else prefb2,
                prefixd[j * 128:(j + 1) * 128, :], sLD, wait=((sDV, pc_done),))
            b1_ld[j] = sLD.n
    for j in range(NT):
        if "B" in noil:
            pfb, stb = prefb, statebf
            dma("sync", prefb, prefixd[j * 128:(j + 1) * 128, :], sLD,
                wait=((sDV, pc_done if j == 0 else sDV.n),))
            rv = sLD.n
            P.wait("vector", sLD, rv)
            P.op("vector", "tensor_add", stb, pfb, Pc, inc=((sDV, sDV.bump() and 1),))
            stbf_v = sDV.n
        else:
            pfb = prefb if j % 2 == 0 else prefb2
            stb = statebf if j % 2 == 0 else statebf2
            P.wait("vector", sLD, b1_ld[j])
            if j == 0:
                P.wait("vector", sDV, pc_done)
            if j >= 2:
                P.wait("vector", sPE, b1_mm_last[j - 2])
            P.op("vector", "tensor_add", stb, pfb, Pc, inc=((sDV, sDV.bump() and 1),))
            stbf_v = sDV.n
            if j + 2 < NT:
                dma("sync", prefb if j % 2 == 0 else prefb2,
                    prefixd[(j + 2) * 128:(j + 3) * 128, :], sLD, wait=((sDV, stbf_v),))
                b1_ld[j + 2] = sLD.n
        P.wait("tensor", sDV, stbf_v)
        if j == 0:
            P.wait("tensor", sDV, as_done)
        zc_inc = {}
        rc_inc = {}
        for t in range(NH + 2):
            if t >= 2:
                h2 = t - 2
                i2 = j * NH + h2
                if h2 == 0 and j >= 1:
                    P.wait("vector", sTT, ofm_evacs[j - 1])
                P.wait("vector", sDV, rc_inc[h2])
                P.op("vector", "tensor_scalar", o_tm[:, h2 * 128:(h2 + 1) * 128],
                     pq[i2 % 2][:, 0:128], rcol[:, h2:h2 + 1], None, ALU.mult,
                     inc=((sDV, sDV.bump() and 1),))
                oa_evac[i2] = sDV.n
            if t >= 1 and t <= NH:
                h1 = t - 1
                P.wait("vector", sDV, zc_inc[h1])
                P.op("vector", "reciprocal", rcol[:, h1:h1 + 1], zcol[:, h1:h1 + 1],
                     inc=((sDV, sDV.bump() and 1),))
                rc_inc[h1] = sDV.n
            if t < NH:
                h = t
                i = j * NH + h
                pp = pq[i % 2]
                if i >= 2:
                    P.wait("tensor", sDV, oa_evac[i - 2])
                P.op("tensor", "matmul", pp[:, 0:128], As[:, i * 128:(i + 1) * 128],
                     vtm[:, j * 2048 + h * 128: j * 2048 + (h + 1) * 128],
                     start=True, stop=False)
                P.op("tensor", "matmul", pp[:, 128:129], As[:, i * 128:(i + 1) * 128],
                     onescol, start=False, stop=False)
                P.op("tensor", "matmul", pp[:, 0:129],
                     qf[:, h * 1024 + j * 128: h * 1024 + (j + 1) * 128],
                     stb[:, h * 129:(h + 1) * 129], start=False, stop=True,
                     inc=((sPE, sPE.bump() and 1),))
                mm_v = sPE.n
                if h == NH - 1:
                    b1_mm_last[j] = mm_v
                P.wait("vector", sPE, mm_v)
                P.op("vector", "tensor_scalar", zcol[:, h:h + 1], pp[:, 128:129],
                     EPS_AT, None, ALU.add, inc=((sDV, sDV.bump() and 1),))
                zc_inc[h] = sDV.n
        stbf_used = sDV.n
        otm_done = sDV.n
        for h in range(NH):
            P.op("sync", "dma_start",
                 wait=((sDV, otm_done),) if h == 0 else (),
                 inc=((sTT, sTT.bump(16) and 16),),
                 out=As[:, (j * NH + h) * 128:(j * NH + h + 1) * 128],
                 in_=o_tm[:, h * 128:(h + 1) * 128], transpose=True)
        ofm_evacs.append(sTT.n)
    b1_done_ac = sTT.n
    b1_done_dv = sDV.n
    if stop == "f0q":
        return finish_early(q1_d.flatten().bitcast(F32)[0:T * D])
    if stop == "f0q2":
        return finish_early(q2_d.flatten().bitcast(F32)[0:T * D])
    if stop == "b1":
        return finish_early()
    if stop == "b1x":
        import os as _os
        _o = int(_os.environ.get("KMK_DOFF", "0"))
        return finish_early(sb_dump=As[:, _o:_o + 2048])

    b2_extra = ((sCC, q_ag_done),) if "W" in noil else ()
    P.op("vector", "memset", st_s1, 0.0, wait=((sTT, b1_done_ac),))
    P.op("vector", "memset", st_s2, 0.0, inc=((sDV, sDV.bump() and 1),))
    b2_zero = sDV.n
    wo_h_done = {}
    wo_evac = {}
    for er in range(b2er):
        for h in range(min(b2h, NH)):
            ih = er * NH + h
            wb = wob0 if ih % 2 == 0 else wob1
            w = [(sPE, wo_h_done[ih - 2])] if ih >= 2 else [(sDV, q_done_dv)] + list(b2_extra)
            w.append((sST, wo_bounce_st[h]))
            dma("sync", wb, wo_bf[h * 128:(h + 1) * 128, er * 512:(er + 1) * 512], sLD,
                wait=tuple(w))
            lv = sLD.n
            P.wait("tensor", sLD, lv)
            if ih == 0:
                P.wait("tensor", sTT, b1_done_ac)
                P.wait("tensor", sDV, b1_done_dv)
            for j in range(NT):
                pp = pq[j // 2][:, (j % 2) * 512:(j % 2) * 512 + 512]
                if h == 0 and er > 0:
                    P.wait("tensor", sDV, wo_evac[(er - 1, j)])
                P.op("tensor", "matmul", pp, As[:, (j * NH + h) * 128:(j * NH + h + 1) * 128],
                     wb, start=(h == 0), stop=(h == min(b2h, NH) - 1),
                     inc=((sPE, sPE.bump() and 1),) if j == NT - 1 else ())
            wo_h_done[ih] = sPE.n
            if h == min(b2h, NH) - 1:
                mm_v = sPE.n
                for j in range(NT):
                    pp = pq[j // 2][:, (j % 2) * 512:(j % 2) * 512 + 512]
                    P.wait("vector", sPE, mm_v)
                    if er == 0 and j == 0:
                        P.wait("vector", sWD, bo_ready)
                        P.wait("vector", sDV, b2_zero)
                    P.op("vector", "tensor_add",
                         yo[:, j * 2048 + er * 512: j * 2048 + (er + 1) * 512],
                         pp, bvbo[:, er * 512:(er + 1) * 512],
                         inc=((sDV, sDV.bump() and 1),))
                    P.wait("vector", sDV, sDV.n)
                    P.op("vector", "reduce_sum", st_m[:, j:j + 1],
                         yo[:, j * 2048 + er * 512: j * 2048 + (er + 1) * 512], axis=AX,
                         inc=((sDV, sDV.bump() and 1),))
                    P.wait("vector", sDV, sDV.n)
                    P.op("vector", "tensor_add", st_s1[:, j:j + 1], st_s1[:, j:j + 1],
                         st_m[:, j:j + 1], inc=((sDV, sDV.bump() and 1),))
                    yov = sDV.n
                    P.wait("scalar", sDV, yov)
                    P.op("scalar", "activation", b2dump,
                         yo[:, j * 2048 + er * 512: j * 2048 + (er + 1) * 512],
                         AF.Square, accum_out=st_t1[:, j:j + 1], inc=((sAC, sAC.bump() and 1),))
                    sqv = sAC.n
                    P.wait("vector", sAC, sqv)
                    P.op("vector", "tensor_add", st_s2[:, j:j + 1], st_s2[:, j:j + 1],
                         st_t1[:, j:j + 1], inc=((sDV, sDV.bump() and 1),))
                    P.wait("vector", sDV, sDV.n)
                    wo_evac[(er, j)] = sDV.n
    b2_done_dv = sDV.n
    if stop == "b2":
        return finish_early()
    if stop == "b2y":
        import os as _os
        _o = int(_os.environ.get("KMK_DOFF", "0"))
        return finish_early(sb_dump=yo[:, _o:_o + 2048])

    P.wait("vector", sDV, b2_done_dv)
    P.op("vector", "tensor_scalar", st_m, st_s1, 1.0 / D, None, ALU.mult,
         inc=((sDV, sDV.bump() and 1),))
    P.op("vector", "tensor_scalar", st_s2, st_s2, 1.0 / D, None, ALU.mult,
         inc=((sDV, sDV.bump() and 1),))
    P.wait("vector", sDV, sDV.n)
    P.op("vector", "tensor_mul", st_t1, st_m, st_m, inc=((sDV, sDV.bump() and 1),))
    P.wait("vector", sDV, sDV.n)
    P.op("vector", "tensor_sub", st_v, st_s2, st_t1, inc=((sDV, sDV.bump() and 1),))
    vv = sDV.n
    P.wait("scalar", sDV, vv)
    P.op("scalar", "activation", st_t1, st_v, AF.Ln, bias=EPS_LN,
         inc=((sAC, sAC.bump() and 1),))
    P.wait("scalar", sAC, sAC.n)
    P.op("scalar", "activation", st_r, st_t1, AF.Exp, scale=-0.5, inc=((sAC, sAC.bump() and 1),))
    rv_ = sAC.n
    P.wait("vector", sAC, rv_)
    for j in range(NT):
        w = ((sDV, sDV.n),) if j >= 1 else ((sDV, rv_ and sDV.n),)
        dma("sync", xbuf, xint_d[j * 128:(j + 1) * 128, :], sLD, wait=w)
        xv = sLD.n
        P.op("vector", "tensor_scalar", tbuf, yo[:, j * 2048:(j + 1) * 2048],
             st_m[:, j:j + 1], st_r[:, j:j + 1], ALU.subtract, ALU.mult,
             inc=((sDV, sDV.bump() and 1),))
        P.wait("vector", sDV, sDV.n)
        P.wait("vector", sLD, xv)
        if j >= 1:
            P.wait("vector", sST, prev_x2_store)
        P.op("vector", "tensor_add", x2b, tbuf, xbuf, inc=((sDV, sDV.bump() and 1),))
        dma("sync", x2_d[j * 128:(j + 1) * 128, :], x2b, sST, wait=((sDV, sDV.n),))
        prev_x2_store = sST.n
    x2_done_st = sST.n
    if stop == "b3":
        return finish_early()
    if stop == "b3x":
        return finish_early(x2_d.flatten())

    pass0_done = ln_sweep(x2_d, xn2fm, (xn2stg, xn2stg1), sqd2, 384,
                          extra_first_wait=((sDV, sDV.n), (sST, x2_done_st)))
    if stop == "f1":
        return finish_early()
    if stop == "f1x":
        import os as _os
        _o = int(_os.environ.get("KMK_DOFF", "0"))
        return finish_early(sb_dump=xn2fm[:, _o:_o + 2048])

    P.op("vector", "memset", runmax, 0.0, wait=((sDV, sDV.n),), inc=((sDV, sDV.bump() and 1),))
    rm_zero = sDV.n
    q1_mm_done = {}
    hc_relu = {}
    hc_sq = {}
    hc_dv = {}
    hc_st = {}
    for g in range(8):
        qs = qstr0 if g % 2 == 0 else qstr1
        w = [(sPE, q1_mm_done[g - 2])] if g >= 2 else [(sCC, q_ag_done), (sTT, pass0_done)]
        for k in range(NKC):
            dma("gpsimd", qs[:, k * 1024:(k + 1) * 1024],
                q1_d[k * 128:(k + 1) * 128, g * 1024:(g + 1) * 1024], sWD,
                wait=tuple(w) if k == 0 else ())
        wv = sWD.n
        P.wait("tensor", sWD, wv)
        if g == 0:
            P.wait("tensor", sTT, pass0_done)
        for hc in range(8):
            Hc = g * 8 + hc
            pp = pq[hc % 2]
            if Hc >= 2:
                P.wait("tensor", sAC, hc_relu[Hc - 2])
            for k in range(NKC):
                lhsT = qs[:, k * 1024 + hc * 128: k * 1024 + (hc + 1) * 128]
                for half in range(2):
                    P.op("tensor", "matmul", pp[:, half * 512:(half + 1) * 512],
                         lhsT, xn2fm[:, k * 1024 + half * 512: k * 1024 + (half + 1) * 512],
                         start=(k == 0), stop=(k == NKC - 1),
                         inc=((sPE, sPE.bump() and 1),) if (k == NKC - 1 and half == 1) else ())
            mm_v = sPE.n
            r2b = r2e0 if Hc % 2 == 0 else r2e1
            P.wait("scalar", sPE, mm_v)
            if Hc >= 2:
                P.wait("scalar", sST, hc_st[Hc - 2])
                P.wait("scalar", sDV, hc_dv[Hc - 2])
            P.op("scalar", "activation", r2tmp, pp[:], AF.Relu, inc=((sAC, sAC.bump() and 1),))
            hc_relu[Hc] = sAC.n
            P.wait("scalar", sAC, sAC.n)
            P.op("scalar", "activation", r2b, r2tmp, AF.Square, scale=scl2[:, 2:3],
                 inc=((sAC, sAC.bump() and 1),))
            hc_sq[Hc] = sAC.n
            P.wait("vector", sAC, hc_sq[Hc])
            if Hc == 0:
                P.wait("vector", sDV, rm_zero)
            P.wait("vector", sDV, sDV.n)
            P.op("vector", "tensor_max", runmax, runmax, r2b, inc=((sDV, sDV.bump() and 1),))
            hc_dv[Hc] = sDV.n
            dma("sync", hr2_d[Hc * 128:(Hc + 1) * 128, :], r2b, sST, wait=((sAC, hc_sq[Hc]),))
            hc_st[Hc] = sST.n
        q1_mm_done[g] = sPE.n
    pass1_st = sST.n
    pass1_dv = sDV.n

    xm_prev = None

    def split3(srcf, w):
        P.op("vector", "tensor_copy", xs_hi[:, 0:w], srcf, inc=((sDV, sDV.bump() and 1),))
        P.wait("vector", sDV, sDV.n)
        P.op("vector", "tensor_sub", xs_rf[:, 0:w], srcf, xs_hi[:, 0:w],
             inc=((sDV, sDV.bump() and 1),))
        P.wait("vector", sDV, sDV.n)
        P.op("vector", "tensor_copy", xs_m[:, 0:w], xs_rf[:, 0:w],
             inc=((sDV, sDV.bump() and 1),))
        P.wait("vector", sDV, sDV.n)
        P.op("vector", "tensor_sub", xs_lf[:, 0:w], xs_rf[:, 0:w], xs_m[:, 0:w],
             inc=((sDV, sDV.bump() and 1),))
        P.wait("vector", sDV, sDV.n)
        P.op("vector", "tensor_copy", xs_l[:, 0:w], xs_lf[:, 0:w],
             inc=((sDV, sDV.bump() and 1),))
        return sDV.n

    def xpose3(w, dstf):
        nb = w // 128
        sv = sDV.n
        for jb in range(nb):
            for (sb_, db_) in ((xs_hi, xs_hit), (xs_m, xs_mt), (xs_l, xs_lt)):
                P.op("sync", "dma_start", wait=((sDV, sv),) if jb == 0 else (),
                     inc=((sTT, sTT.bump(16) and 16),),
                     out=db_[:, jb * 128:(jb + 1) * 128],
                     in_=sb_[:, jb * 128:(jb + 1) * 128], transpose=True)
        P.wait("vector", sTT, sTT.n)
        P.op("vector", "tensor_add", dstf, xs_hit[:, 0:w], xs_mt[:, 0:w],
             inc=((sDV, sDV.bump() and 1),))
        P.wait("vector", sDV, sDV.n)
        P.op("vector", "tensor_add", dstf, dstf, xs_lt[:, 0:w],
             inc=((sDV, sDV.bump() and 1),))
        P.wait("vector", sDV, sDV.n)
        return sDV.n

    P.wait("vector", sDV, pass1_dv)
    split3(runmax, 1024)
    xpose3(1024, xT)
    for j in range(NT):
        P.op("vector", "reduce_max", xmaxc[:, j:j + 1], xT[:, j * 128:(j + 1) * 128], axis=AX,
             inc=((sDV, sDV.bump() and 1),))
        P.wait("vector", sDV, sDV.n)
        P.op("vector", "tensor_scalar", xmaxc[:, j:j + 1], xmaxc[:, j:j + 1], EPS_Q, None, ALU.max,
             inc=((sDV, sDV.bump() and 1),))
        P.wait("vector", sDV, sDV.n)
        P.op("vector", "reciprocal", b1c[:, j:j + 1], xmaxc[:, j:j + 1],
             inc=((sDV, sDV.bump() and 1),))
        P.op("vector", "tensor_scalar", b2c[:, j:j + 1], xmaxc[:, j:j + 1], scl2[:, 3:4],
             1.0 / 255.0, ALU.mult, ALU.mult, inc=((sDV, sDV.bump() and 1),))
        P.wait("vector", sDV, sDV.n)
        P.op("vector", "tensor_scalar", b1c[:, j:j + 1], b1c[:, j:j + 1], 255.0, None, ALU.mult,
             inc=((sDV, sDV.bump() and 1),))
        xm_prev = sDV.n
    P.op("vector", "memset", xs_lf, 0.0, wait=((sDV, xm_prev),),
         inc=((sDV, sDV.bump() and 1),))
    P.wait("vector", sDV, sDV.n)
    for r in range(NT):
        P.op("vector", "tensor_copy", xs_lf[:, r * 128 + r: r * 128 + r + 1],
             b1c[:, r:r + 1], inc=((sDV, sDV.bump() and 1),) if r == NT - 1 else ())
    P.wait("vector", sDV, sDV.n)
    P.op("vector", "tensor_copy", xs_hi, xs_lf, inc=((sDV, sDV.bump() and 1),))
    P.wait("vector", sDV, sDV.n)
    P.op("vector", "tensor_sub", xs_rf, xs_lf, xs_hi, inc=((sDV, sDV.bump() and 1),))
    P.wait("vector", sDV, sDV.n)
    P.op("vector", "tensor_copy", xs_m, xs_rf, inc=((sDV, sDV.bump() and 1),))
    P.wait("vector", sDV, sDV.n)
    P.op("vector", "tensor_sub", xs_rf, xs_rf, xs_m, inc=((sDV, sDV.bump() and 1),))
    P.wait("vector", sDV, sDV.n)
    P.op("vector", "tensor_copy", xs_l, xs_rf, inc=((sDV, sDV.bump() and 1),))
    xpose3(1024, xT)
    brow_v = sDV.n
    P.op("vector", "memset", padb[0:8, 0:128], 1.0, wait=((sDV, brow_v),),
         inc=((sDV, sDV.bump() and 1),))
    ones8 = sDV.n
    P.wait("tensor", sDV, ones8)
    for half in range(2):
        P.op("tensor", "matmul", pq[2][:, half * 512:(half + 1) * 512],
             padb[0:8, 0:128], xT[0:8, half * 512:(half + 1) * 512],
             start=True, stop=True,
             inc=((sPE, sPE.bump() and 1),) if half == 1 else ())
    b1bc_mm = sPE.n
    P.wait("vector", sPE, b1bc_mm)
    P.op("vector", "tensor_copy", b1b, pq[2], inc=((sDV, sDV.bump() and 1),))
    bb_ld = sDV.n
    bmax_st = sST.n
    if stop == "f2":
        return finish_early()
    if stop == "f2b":
        P.wait("vector", sDV, bb_ld)
        P.op("vector", "tensor_copy", r2e0, b1b, inc=((sDV, sDV.bump() and 1),))
        P.op("vector", "tensor_copy", r2e1[:, 0:8], b2c, inc=((sDV, sDV.bump() and 1),))
        bbv = sDV.n
        P.op("vector", "memset", x2b, 0.0, wait=((sST, sST.n),), inc=((sDV, sDV.bump() and 1),))
        zv = sDV.n
        for j in range(NT):
            dma("sync", y_out[j * 128:(j + 1) * 128, :], x2b, sST, wait=((sDV, zv),))
        dma("sync", y_out[0:128, 0:1024], r2e0, sST, wait=((sDV, bbv), (sST, sST.n)))
        dma("sync", y_out[128:256, 0:1024], r2e1, sST)
        dma("sync", y_out[256:384, 0:4], scl2, sST)
        P.wait("sync", sST, sST.n)
        _emit(nc, P, sems)
        return nc
    if stop == "f2h":
        return finish_early(hr2_d.flatten()[0:T * D])

    P.op("vector", "memset", outsb[:, 0:8192], 0.0,
         wait=((sDV, sDV.n), (sPE, b1bc_mm)))
    P.op("vector", "memset", outsb[:, 8192:16384], 0.0, inc=((sDV, sDV.bump() and 1),))
    out_zero = sDV.n
    p2_mm_done = {}
    p2_acc = {}
    hq_ready = {}
    for kg in range(16):
        r2bufs = r2r if kg % 2 == 0 else r2rb
        q2bufs = q2r if kg % 2 == 0 else q2rb
        hqb = hqr if (kg % 2 == 0 or "H" in noil) else hqrb
        for kk in range(4):
            kc = kg * 4 + kk
            w = [(sDV, hq_ready[kc - 8])] if kc >= 8 else [(sST, pass1_st)]
            dma("sync", r2bufs[kk], hr2_d[kc * 128:(kc + 1) * 128, :], sLD, wait=tuple(w))
            rl = sLD.n
            w2_ = [(sPE, p2_mm_done[kg - 2])] if kg >= 2 else [(sCC, q_ag_done)]
            dma("gpsimd", q2bufs[kk], q2_d[kc * 128:(kc + 1) * 128, :], sWD, wait=tuple(w2_))
            P.wait("vector", sLD, rl)
            if kc == 0:
                P.wait("vector", sDV, bb_ld)
            if "H" in noil:
                if kg >= 1:
                    P.wait("vector", sPE, p2_mm_done[kg - 1])
            elif kg >= 2:
                P.wait("vector", sPE, p2_mm_done[kg - 2])
            P.op("vector", "tensor_mul", r2bufs[kk], r2bufs[kk], b1b,
                 inc=((sDV, sDV.bump() and 1),))
            P.wait("vector", sDV, sDV.n)
            P.op("vector", "tensor_scalar", intq, r2bufs[kk], RHALF, None, ALU.add,
                 inc=((sDV, sDV.bump() and 1),))
            P.wait("vector", sDV, sDV.n)
            P.op("vector", "tensor_copy", r2bufs[kk], intq,
                 inc=((sDV, sDV.bump() and 1),))
            P.wait("vector", sDV, sDV.n)
            P.op("vector", "tensor_copy", hqb[kk], r2bufs[kk], inc=((sDV, sDV.bump() and 1),))
            hq_ready[kc] = sDV.n
        qload_v = sWD.n
        for j in range(NT):
            for eh in range(2):
                pp = pq[(j % 2) * 2 + eh]
                P.wait("tensor", sDV, hq_ready[kg * 4 + 3])
                P.wait("tensor", sWD, qload_v)
                if kg > 0 or j >= 2:
                    key = (kg, j - 2, eh) if j >= 2 else (kg - 1, NT - 2 + j, eh)
                    P.wait("tensor", sDV, p2_acc[key])
                for kk in range(4):
                    for q_ in range(2):
                        P.op("tensor", "matmul", pp[:, q_ * 512:(q_ + 1) * 512],
                             hqb[kk][:, j * 128:(j + 1) * 128],
                             q2bufs[kk][:, eh * 1024 + q_ * 512: eh * 1024 + (q_ + 1) * 512],
                             start=(kk == 0), stop=(kk == 3),
                             inc=((sPE, sPE.bump() and 1),) if (kk == 3 and q_ == 1) else ())
                mm_v = sPE.n
                P.wait("vector", sPE, mm_v)
                if kg == 0 and j == 0 and eh == 0:
                    P.wait("vector", sDV, out_zero)
                P.op("vector", "scalar_tensor_tensor",
                     outsb[:, j * 2048 + eh * 1024: j * 2048 + (eh + 1) * 1024],
                     pp[:], b2c[:, j:j + 1],
                     outsb[:, j * 2048 + eh * 1024: j * 2048 + (eh + 1) * 1024],
                     ALU.mult, ALU.add, inc=((sDV, sDV.bump() and 1),))
                p2_acc[(kg, j, eh)] = sDV.n
        p2_mm_done[kg] = sPE.n
    pass2_dv = sDV.n
    if stop == "f3":
        return finish_early()
    if stop == "f3z":
        for j in range(NT):
            dma("sync", y_out[j * 128:(j + 1) * 128, :], outsb[:, j * 2048:(j + 1) * 2048],
                sST, wait=((sDV, pass2_dv),) if j == 0 else ())
        P.wait("sync", sST, sST.n)
        _emit(nc, P, sems)
        return nc
    if stop == "f3x":
        import os as _os
        _o = int(_os.environ.get("KMK_DOFF", "0"))
        return finish_early(sb_dump=outsb[:, _o:_o + 2048])

    P.wait("vector", sDV, pass2_dv)
    P.op("vector", "memset", st_s1, 0.0)
    P.op("vector", "memset", st_s2, 0.0, inc=((sDV, sDV.bump() and 1),))
    for j in range(NT):
        P.op("vector", "reduce_sum", st_s1[:, j:j + 1], outsb[:, j * 2048:(j + 1) * 2048],
             axis=AX, inc=((sDV, sDV.bump() and 1),))
        ssv = sDV.n
        P.wait("scalar", sDV, ssv)
        P.op("scalar", "activation", f4dump, outsb[:, j * 2048:(j + 1) * 2048], AF.Square,
             accum_out=st_s2[:, j:j + 1], inc=((sAC, sAC.bump() and 1),))
    f4_sq = sAC.n
    P.wait("vector", sAC, f4_sq)
    P.op("vector", "tensor_scalar", st_m, st_s1, 1.0 / D, None, ALU.mult,
         inc=((sDV, sDV.bump() and 1),))
    P.op("vector", "tensor_scalar", st_s2, st_s2, 1.0 / D, None, ALU.mult,
         inc=((sDV, sDV.bump() and 1),))
    P.wait("vector", sDV, sDV.n)
    P.op("vector", "tensor_mul", st_t1, st_m, st_m, inc=((sDV, sDV.bump() and 1),))
    P.wait("vector", sDV, sDV.n)
    P.op("vector", "tensor_sub", st_v, st_s2, st_t1, inc=((sDV, sDV.bump() and 1),))
    vv2 = sDV.n
    P.wait("scalar", sDV, vv2)
    P.op("scalar", "activation", st_t1, st_v, AF.Ln, bias=EPS_LN,
         inc=((sAC, sAC.bump() and 1),))
    P.wait("scalar", sAC, sAC.n)
    P.op("scalar", "activation", st_r, st_t1, AF.Exp, scale=-0.5, inc=((sAC, sAC.bump() and 1),))
    rv2 = sAC.n
    P.wait("vector", sAC, rv2)
    for j in range(NT):
        w = ((sDV, sDV.n),) if j >= 1 else ()
        dma("sync", xbuf, x2_d[j * 128:(j + 1) * 128, :], sLD, wait=w)
        xv = sLD.n
        P.op("vector", "tensor_scalar", tbuf, outsb[:, j * 2048:(j + 1) * 2048],
             st_m[:, j:j + 1], st_r[:, j:j + 1], ALU.subtract, ALU.mult,
             inc=((sDV, sDV.bump() and 1),))
        P.wait("vector", sDV, sDV.n)
        P.wait("vector", sLD, xv)
        if j >= 1:
            P.wait("vector", sST, prev_y_store)
        P.op("vector", "tensor_add", x2b, tbuf, xbuf, inc=((sDV, sDV.bump() and 1),))
        dma("sync", y_out[j * 128:(j + 1) * 128, :], x2b, sST, wait=((sDV, sDV.n),))
        prev_y_store = sST.n
    P.wait("sync", sST, sST.n)
    _emit(nc, P, sems)
    return nc


def _emit(nc, P, sems):
    from contextlib import ExitStack
    with ExitStack() as ctx:
        for s in sems.values():
            s.h = ctx.enter_context(nc.semaphore(s.name))
        block = ctx.enter_context(nc.Block())
        block.sync(lambda e: [f(e) for f in P.q["sync"]])
        block.vector(lambda e: [f(e) for f in P.q["vector"]])
        block.scalar(lambda e: [f(e) for f in P.q["scalar"]])
        block.tensor(lambda e: [f(e) for f in P.q["tensor"]])
        block.gpsimd(lambda e: [f(e) for f in P.q["gpsimd"]])


_NC_CACHE = None


def kernel(L0_attn_x, L0_attn_Wq, L0_attn_bq, L0_attn_Wk, L0_attn_bk,
           L0_attn_Wv, L0_attn_bv, L0_attn_Wo, L0_attn_bo,
           L0_ffn_c_fc, L0_ffn_c_proj):
    global _NC_CACHE
    x = np.ascontiguousarray(np.asarray(L0_attn_x, dtype=np.float32).reshape(8192, 2048))
    Wq_ = np.ascontiguousarray(np.asarray(L0_attn_Wq, dtype=np.float32))
    Wk_ = np.ascontiguousarray(np.asarray(L0_attn_Wk, dtype=np.float32))
    Wv_ = np.ascontiguousarray(np.asarray(L0_attn_Wv, dtype=np.float32))
    Wo_ = np.ascontiguousarray(np.asarray(L0_attn_Wo, dtype=np.float32))
    bq = np.asarray(L0_attn_bq, dtype=np.float32)
    bk = np.asarray(L0_attn_bk, dtype=np.float32)
    bv = np.asarray(L0_attn_bv, dtype=np.float32)
    bo = np.asarray(L0_attn_bo, dtype=np.float32)
    c_fc = np.ascontiguousarray(np.asarray(L0_ffn_c_fc, dtype=np.float32))
    c_projT = np.ascontiguousarray(np.asarray(L0_ffn_c_proj, dtype=np.float32).T)

    bqk = np.ascontiguousarray(
        np.concatenate([bq.reshape(16, 128).T, bk.reshape(16, 128).T], axis=1))
    bv_b = np.ascontiguousarray(np.broadcast_to(bv[None, :], (128, 2048)))
    bo_b = np.ascontiguousarray(np.broadcast_to(bo[None, :], (128, 2048)))
    maskT = np.triu(np.ones((128, 128), dtype=np.float32)).astype(ml_dtypes.bfloat16)
    eye = np.eye(128, dtype=ml_dtypes.bfloat16)
    eyef32 = np.eye(128, dtype=np.float32)
    onescol = np.ones((128, 1), dtype=ml_dtypes.bfloat16)
    cstf_host = np.ascontiguousarray(
        np.broadcast_to(np.array([1e-5, -0.5], dtype=np.float32)[None, :], (128, 2)))

    if _NC_CACHE is None:
        import os
        _NC_CACHE = build_program(stop=os.environ.get("KMK_STOP", "full"),
                                  noil=os.environ.get("KMK_NOIL", ""),
                                  b2er=int(os.environ.get("KMK_B2ER", "4")),
                                  b2h=int(os.environ.get("KMK_B2H", "16")))
    nc = _NC_CACHE

    in_maps = []
    for c in range(NCORES):
        pm = np.zeros((128, 8), dtype=np.float32)
        pm[:, :c] = 1.0
        in_maps.append({
            "x": np.ascontiguousarray(x[c * T:(c + 1) * T]),
            "Wq": Wq_, "Wk": Wk_, "Wv": Wv_, "Wo": Wo_,
            "bqk": bqk, "bv_b": bv_b, "bo_b": bo_b,
            "w1s": np.ascontiguousarray(c_fc[c * 256:(c + 1) * 256]),
            "w2s": np.ascontiguousarray(c_projT[c * 1024:(c + 1) * 1024]),
            "maskT": maskT, "eye": eye, "eyef32": eyef32, "onescol": onescol,
            "pmask": pm, "cstf": cstf_host,
        })
    res = run_bass_kernel_spmd(nc, in_maps, core_ids=list(range(NCORES)))
    y = np.concatenate([res.results[c]["y"] for c in range(NCORES)], axis=0)
    return y.reshape(1, 8192, 2048)
